# version 01
# baseline (speedup 1.0000x reference)
import numpy as np
import ml_dtypes
from contextlib import ExitStack

import concourse.bass as bass
import concourse.mybir as mybir
from concourse.bass_utils import run_bass_kernel_spmd

F32 = mybir.dt.float32
BF16 = mybir.dt.bfloat16
AF = mybir.ActivationFunctionType
ALU = mybir.AluOpType
AX = mybir.AxisListType

B, S, D, H, DN, DV, DQ, DFF = 2, 4096, 4096, 32, 128, 128, 192, 11008
TC = 1024
NDC = D // 128
NFC = DFF // 128
NT = TC // 512
SCALE = float(DQ) ** -0.5
EPS = 1e-5
NREM = 24
NB_R = 6



def host_layouts(L0_attn_wq, L0_attn_wkv, L0_attn_wo, L0_ffn_W_gate,
                 L0_ffn_W_up, L0_ffn_W_down):
    wq = np.asarray(L0_attn_wq)
    wkv = np.asarray(L0_attn_wkv)
    wo = np.asarray(L0_attn_wo)
    W_gate = np.asarray(L0_ffn_W_gate)
    W_up = np.asarray(L0_ffn_W_up)
    W_down = np.asarray(L0_ffn_W_down)

    def blk_updown(Wfd):
        a = Wfd.reshape(NFC, 128, NDC, 128)
        return a.transpose(0, 3, 2, 1).reshape(NFC, 128, D)

    wupgate = np.concatenate([blk_updown(W_up), blk_updown(W_gate)], axis=2)

    a = W_down.reshape(8, 4, 128, 43, 2, 128)
    wdn = np.ascontiguousarray(a.transpose(0, 3, 5, 4, 1, 2)).reshape(344, 128, TC)

    q_cols = wq.reshape(D, H, DQ)[:, :, :DN]
    a = q_cols.reshape(NDC, 128, H, DN)
    wq_blk = np.ascontiguousarray(a.transpose(2, 1, 0, 3)).reshape(H, 128, D)

    k_cols = wkv[:, :H * DN].reshape(D, H, DN)
    v_cols = wkv[:, H * DN:H * (DN + DV)].reshape(D, H, DV)
    kv = np.stack([k_cols, v_cols], axis=2)
    a = kv.reshape(NDC, 128, H, 2, 128)
    wkv_blk = np.ascontiguousarray(a.transpose(2, 1, 3, 0, 4)).reshape(H, 128, 2 * D)

    a = wo.reshape(2, 16, 128, 8, 4, 128)
    wo_blk = np.ascontiguousarray(a.transpose(3, 0, 2, 1, 4, 5)).reshape(16, 128, 8192)

    return dict(wupgate=wupgate, wdn=wdn, wq_blk=wq_blk, wkv_blk=wkv_blk,
                wo_blk=wo_blk)


def host_shards(lay):
    wug_pad = np.zeros((88, 128, 2 * D), np.float32)
    wug_pad[:NFC] = lay["wupgate"]
    shards = []
    for r in range(8):
        shards.append(dict(
            wug_sh=np.ascontiguousarray(wug_pad[r::8]),
            wdn_sh=np.ascontiguousarray(lay["wdn"][43 * r:43 * (r + 1)]),
            wkv_sh=np.ascontiguousarray(lay["wkv_blk"][4 * r:4 * (r + 1)]),
            wq_sh=np.ascontiguousarray(lay["wq_blk"][4 * r:4 * (r + 1)]),
            wo_sh=np.ascontiguousarray(lay["wo_blk"][2 * r:2 * (r + 1)]),
        ))
    return shards


def host_consts(core):
    c = core % 4
    eye = np.eye(128, dtype=ml_dtypes.bfloat16)
    tri = np.triu(np.ones((128, 128), np.float32)).astype(ml_dtypes.bfloat16)
    gates = np.zeros((128, NREM), np.float32)
    for kc in range(NREM):
        gates[:, kc] = 1.0 if (kc // 8) < c else 0.0
    return dict(eye=eye, tri=tri, gates=gates)



class Prog:
    def __init__(self, nc, ctx):
        self.nc = nc
        self.ctx = ctx
        self._sems = {}
        self.cnt = {}

    def sem(self, name):
        if name not in self._sems:
            self._sems[name] = self.ctx.enter_context(self.nc.semaphore(name))
            self.cnt[name] = 0
        return self._sems[name]

    def inc(self, inst, name, n=1):
        inst.then_inc(self.sem(name), n)
        self.cnt[name] += n
        return self.cnt[name]

    def base(self, name):
        self.sem(name)
        return self.cnt[name]



def phase_wprep(p, sh, g16, tag="wp"):
    btag = tag
    """cc counts after this phase: wug pieces 1..11, wdn 12, wkv 13, wq 14, wo 15."""
    nc = p.nc
    cs0 = p.base("wp_cs")
    loc = dict(
        wug=nc.dram_tensor(f"{tag}_wug16", [11, 128, 2 * D], BF16),
        wdn=nc.dram_tensor(f"{tag}_wdn16", [43, 128, TC], BF16),
        wkv=nc.dram_tensor(f"{tag}_wkv16", [4, 128, 2 * D], BF16),
        wq=nc.dram_tensor(f"{tag}_wq16", [4, 128, D], BF16),
        wo=nc.dram_tensor(f"{tag}_wo16", [2, 128, 2 * D], BF16),
    )
    with nc.Block(f"blk_{btag}") as blk:
        @blk.gpsimd
        def _(g):
            ncast = 0
            for i in range(11):
                p.inc(g.dma_start(out=loc["wug"][i, :, :], in_=sh["wug_sh"][i, :, :],
                                  max_dma_last_dim=2048), "wp_cs", 16)
                ncast += 1
                g.wait_ge(p.sem("wp_cs"), cs0 + 16 * ncast)
                p.inc(g.collective_compute(
                    "AllGather", ALU.bypass, replica_groups=[list(range(8))],
                    ins=[loc["wug"][i, :, :]],
                    outs=[g16["wug"].ap()[i * 1024:(i + 1) * 1024, :]],
                ), "cc", 1)
            for nm, npc in (("wdn", 43), ("wkv", 4), ("wq", 4), ("wo", 2)):
                for i in range(npc):
                    p.inc(g.dma_start(out=loc[nm][i, :, :], in_=sh[f"{nm}_sh"][i, :, :],
                                      max_dma_last_dim=2048), "wp_cs", 16)
                    ncast += 1
                g.wait_ge(p.sem("wp_cs"), cs0 + 16 * ncast)
                p.inc(g.collective_compute(
                    "AllGather", ALU.bypass, replica_groups=[list(range(8))],
                    ins=[loc[nm].ap().rearrange("c k m -> (c k) m")],
                    outs=[g16[nm].ap()],
                ), "cc", 1)



def phase_ln(p, src_dram, dst_sb, tag, src_ready=None, btag=None):
    btag = btag or tag
    """Stream src (32 dram APs [128, TC] f32) twice; write bf16 into
    dst_sb[:, dc*TC + t]. src_ready(dc) -> [(sem, val)] readiness waits."""
    nc = p.nc
    ld0 = [p.base(f"{tag}_ld{i}") for i in range(2)]
    ft0 = p.base(f"{tag}_ft")
    sq0 = p.base(f"{tag}_sq")
    rw0 = p.base(f"{tag}_row")
    bc0 = p.base(f"{tag}_bc")
    ap0 = p.base(f"{tag}_ap")
    ss0 = p.base(f"{tag}_ss")

    with ExitStack() as ctx:
        e = ctx.enter_context
        xc = [e(nc.sbuf_tensor(f"{btag}_xc{i}", [128, TC], F32)) for i in range(2)]
        xsq = [e(nc.sbuf_tensor(f"{btag}_xsq{i}", [128, TC], F32)) for i in range(2)]
        ones_col = e(nc.sbuf_tensor(f"{btag}_ones", [128, 1], F32))
        ones_row = e(nc.sbuf_tensor(f"{btag}_onesr", [1, 128], F32))
        srow = e(nc.sbuf_tensor(f"{btag}_srow", [1, TC], F32))
        qrow = e(nc.sbuf_tensor(f"{btag}_qrow", [1, TC], F32))
        mrow = e(nc.sbuf_tensor(f"{btag}_mrow", [1, TC], F32))
        vrow = e(nc.sbuf_tensor(f"{btag}_vrow", [1, TC], F32))
        arow = e(nc.sbuf_tensor(f"{btag}_arow", [1, TC], F32))
        brow = e(nc.sbuf_tensor(f"{btag}_brow", [1, TC], F32))
        tmp = [e(nc.sbuf_tensor(f"{btag}_tmp{i}", [128, TC], F32)) for i in range(2)]
        eps_sb = e(nc.sbuf_tensor(f"{btag}_eps", [1, 1], F32))
        st_ps = e(nc.psum_tensor(f"{btag}_stp", [1, TC], F32))
        sq_ps = e(nc.psum_tensor(f"{btag}_sqp", [1, TC], F32))
        abc_ps = e(nc.psum_tensor(f"{btag}_abc", [128, TC], F32))
        bbc_ps = e(nc.psum_tensor(f"{btag}_bbc", [128, TC], F32))
        blk = e(nc.Block(f"blk_{btag}"))

        @blk.sync
        def _(s):
            for r in range(2):
                for dc in range(NDC):
                    sl = dc % 2
                    if r == 0:
                        if src_ready is not None:
                            for nm, val in src_ready(dc):
                                s.wait_ge(p.sem(nm), val)
                        if dc >= 2:
                            s.wait_ge(p.sem(f"{tag}_ft"), ft0 + dc - 1)
                            s.wait_ge(p.sem(f"{tag}_sq"), sq0 + dc - 1)
                    else:
                        if dc < 2:
                            s.wait_ge(p.sem(f"{tag}_ft"), ft0 + NDC)
                            s.wait_ge(p.sem(f"{tag}_sq"), sq0 + NDC)
                        else:
                            s.wait_ge(p.sem(f"{tag}_ap"), ap0 + dc - 1)
                    p.inc(s.dma_start(out=xc[sl][:, :], in_=src_dram[dc]),
                          f"{tag}_ld{sl}", 16)

        @blk.vector
        def _(v):
            p.inc(v.memset(ones_col[:, :], 1.0), f"{tag}_row")
            p.inc(v.memset(ones_row[:, :], 1.0), f"{tag}_row")
            p.inc(v.memset(eps_sb[:, :], EPS), f"{tag}_row")

        @blk.scalar
        def _(sc):
            for dc in range(NDC):
                sl = dc % 2
                sc.wait_ge(p.sem(f"{tag}_ld{sl}"), ld0[sl] + 16 * (dc // 2 + 1))
                if dc >= 2:
                    sc.wait_ge(p.sem(f"{tag}_ft"), ft0 + dc - 1)
                p.inc(sc.activation(xsq[sl][:, :], xc[sl][:, :], AF.Square),
                      f"{tag}_sq")

        @blk.tensor
        def _(t):
            t.wait_ge(p.sem(f"{tag}_row"), rw0 + 3)
            for dc in range(NDC):
                sl = dc % 2
                t.wait_ge(p.sem(f"{tag}_ld{sl}"), ld0[sl] + 16 * (dc // 2 + 1))
                t.wait_ge(p.sem(f"{tag}_sq"), sq0 + dc + 1)
                for th in range(NT):
                    tsl = slice(th * 512, (th + 1) * 512)
                    t.matmul(st_ps[:, tsl], ones_col[:, :], xc[sl][:, tsl],
                             start=(dc == 0), stop=(dc == NDC - 1))
                    inst = t.matmul(sq_ps[:, tsl], ones_col[:, :], xsq[sl][:, tsl],
                                    start=(dc == 0), stop=(dc == NDC - 1))
                p.inc(inst, f"{tag}_ft")

        @blk.vector
        def _(v):
            v.wait_ge(p.sem(f"{tag}_ft"), ft0 + NDC)
            p.inc(v.tensor_scalar(mrow[:, :], st_ps[:, :], 1.0 / D, None, ALU.mult),
                  f"{tag}_row")
            v.wait_ge(p.sem(f"{tag}_row"), rw0 + 4)
            p.inc(v.tensor_scalar(vrow[:, :], sq_ps[:, :], 1.0 / D, None, ALU.mult),
                  f"{tag}_row")
            v.wait_ge(p.sem(f"{tag}_row"), rw0 + 5)
            p.inc(v.tensor_tensor(qrow[:, :], mrow[:, :], mrow[:, :], ALU.mult),
                  f"{tag}_row")
            v.wait_ge(p.sem(f"{tag}_row"), rw0 + 6)
            p.inc(v.tensor_tensor(vrow[:, :], vrow[:, :], qrow[:, :], ALU.subtract),
                  f"{tag}_row")

        @blk.scalar
        def _(sc):
            sc.wait_ge(p.sem(f"{tag}_row"), rw0 + 7)
            p.inc(sc.activation(srow[:, :], vrow[:, :], AF.Ln, bias=eps_sb[:, :]),
                  f"{tag}_row")
            sc.wait_ge(p.sem(f"{tag}_row"), rw0 + 8)
            p.inc(sc.activation(arow[:, :], srow[:, :], AF.Exp, scale=-0.5),
                  f"{tag}_row")

        @blk.vector
        def _(v):
            v.wait_ge(p.sem(f"{tag}_row"), rw0 + 9)
            p.inc(v.tensor_tensor(brow[:, :], mrow[:, :], arow[:, :], ALU.mult),
                  f"{tag}_row")
            v.wait_ge(p.sem(f"{tag}_row"), rw0 + 10)
            p.inc(v.tensor_scalar(brow[:, :], brow[:, :], -1.0, None, ALU.mult),
                  f"{tag}_row")

        @blk.tensor
        def _(t):
            t.wait_ge(p.sem(f"{tag}_row"), rw0 + 11)
            for th in range(NT):
                tsl = slice(th * 512, (th + 1) * 512)
                t.matmul(abc_ps[:, tsl], ones_row[:, :], arow[:, tsl],
                         start=True, stop=True)
                inst = t.matmul(bbc_ps[:, tsl], ones_row[:, :], brow[:, tsl],
                                start=True, stop=True)
            p.inc(inst, f"{tag}_bc")

        @blk.vector
        def _(v):
            v.wait_ge(p.sem(f"{tag}_bc"), bc0 + 1)
            for dc in range(NDC):
                sl = dc % 2
                v.wait_ge(p.sem(f"{tag}_ld{sl}"), ld0[sl] + 16 * (NDC // 2 + dc // 2 + 1))
                if dc >= 2:
                    v.wait_ge(p.sem(f"{tag}_ap"), ap0 + dc - 1)
                p.inc(v.tensor_tensor(tmp[sl][:, :], xc[sl][:, :], abc_ps[:, :],
                                      ALU.mult), f"{tag}_ss")
                v.wait_ge(p.sem(f"{tag}_ss"), ss0 + dc + 1)
                p.inc(v.tensor_tensor(dst_sb[:, dc * TC:(dc + 1) * TC],
                                      tmp[sl][:, :], bbc_ps[:, :], ALU.add),
                      f"{tag}_ap")



def phase_ffn_a(p, xn_sb, h_dram, wug_g, tag, h_war=None, btag=None):
    btag = btag or tag
    """wug_g: fc -> dram AP [128, 2D] bf16 (gathered). h_war: [(sem,val)] to
    wait once before the first h write (prior readers of h_scr)."""
    nc = p.nc
    w0 = [p.base(f"{tag}_w{i}") for i in range(2)]
    mm0 = p.base(f"{tag}_mm")
    si0 = p.base(f"{tag}_si")
    vt0 = p.base(f"{tag}_vt")
    hs0 = [p.base(f"{tag}_h{i}") for i in range(2)]

    with ExitStack() as ctx:
        e = ctx.enter_context
        wsl = [e(nc.sbuf_tensor(f"{btag}_wb{i}", [128, 2 * D], BF16)) for i in range(2)]
        sg = [e(nc.sbuf_tensor(f"{btag}_sg{i}", [128, TC], F32)) for i in range(2)]
        hsb = [e(nc.sbuf_tensor(f"{btag}_hb{i}", [128, TC], BF16)) for i in range(2)]
        up_ps = [e(nc.psum_tensor(f"{btag}_up{i}", [128, TC], F32)) for i in range(2)]
        gt_ps = [e(nc.psum_tensor(f"{btag}_gt{i}", [128, TC], F32)) for i in range(2)]
        blk = e(nc.Block(f"blk_{btag}"))

        @blk.gpsimd
        def _(g):
            for fc in range(NFC):
                sl = fc % 2
                g.wait_ge(p.sem("cc"), 1 + fc // 8)
                if fc >= 2:
                    g.wait_ge(p.sem(f"{tag}_mm"), mm0 + 4 * (fc - 1))
                p.inc(g.dma_start(out=wsl[sl][:, :], in_=wug_g(fc)), f"{tag}_w{sl}", 16)

        @blk.tensor
        def _(t):
            for fc in range(NFC):
                sl = fc % 2
                t.wait_ge(p.sem(f"{tag}_w{sl}"), w0[sl] + 16 * (fc // 2 + 1))
                if fc >= 2:
                    t.wait_ge(p.sem(f"{tag}_vt"), vt0 + 2 * (fc - 1))
                    t.wait_ge(p.sem(f"{tag}_si"), si0 + 2 * (fc - 1))
                for th in range(NT):
                    tsl = slice(th * 512, (th + 1) * 512)
                    for dc in range(NDC):
                        inst = t.matmul(up_ps[sl][:, tsl],
                                        wsl[sl][:, dc * 128:(dc + 1) * 128],
                                        xn_sb[:, dc * TC + th * 512: dc * TC + (th + 1) * 512],
                                        start=(dc == 0), stop=(dc == NDC - 1))
                    p.inc(inst, f"{tag}_mm")
                    for dc in range(NDC):
                        inst = t.matmul(gt_ps[sl][:, tsl],
                                        wsl[sl][:, D + dc * 128: D + (dc + 1) * 128],
                                        xn_sb[:, dc * TC + th * 512: dc * TC + (th + 1) * 512],
                                        start=(dc == 0), stop=(dc == NDC - 1))
                    p.inc(inst, f"{tag}_mm")

        @blk.scalar
        def _(sc):
            for fc in range(NFC):
                sl = fc % 2
                for th in range(NT):
                    tsl = slice(th * 512, (th + 1) * 512)
                    sc.wait_ge(p.sem(f"{tag}_mm"), mm0 + 4 * fc + 2 * (th + 1))
                    if fc >= 2:
                        sc.wait_ge(p.sem(f"{tag}_vt"), vt0 + 2 * (fc - 2) + th + 1)
                    p.inc(sc.activation(sg[sl][:, tsl], gt_ps[sl][:, tsl], AF.Silu),
                          f"{tag}_si")

        @blk.vector
        def _(v):
            for fc in range(NFC):
                sl = fc % 2
                for th in range(NT):
                    tsl = slice(th * 512, (th + 1) * 512)
                    v.wait_ge(p.sem(f"{tag}_si"), si0 + 2 * fc + th + 1)
                    v.wait_ge(p.sem(f"{tag}_mm"), mm0 + 4 * fc + 2 * th + 1)
                    if fc >= 2 and th == 0:
                        v.wait_ge(p.sem(f"{tag}_h{sl}"), hs0[sl] + 16 * (fc // 2))
                    p.inc(v.tensor_tensor(hsb[sl][:, tsl], sg[sl][:, tsl],
                                          up_ps[sl][:, tsl], ALU.mult), f"{tag}_vt")

        @blk.sync
        def _(s):
            if h_war:
                for nm, val in h_war:
                    s.wait_ge(p.sem(nm), val)
            for fc in range(NFC):
                sl = fc % 2
                s.wait_ge(p.sem(f"{tag}_vt"), vt0 + 2 * (fc + 1))
                p.inc(s.dma_start(out=h_dram[fc], in_=hsb[sl][:, :]), f"{tag}_h{sl}", 16)
            s.wait_ge(p.sem(f"{tag}_h0"), p.cnt[f"{tag}_h0"])
            s.wait_ge(p.sem(f"{tag}_h1"), p.cnt[f"{tag}_h1"])

    return hs0



def phase_ffn_b(p, h_pairs, res_src, res_dst, wdn_g, tag, yscale=0.5,
                h_ready=None, res_ready=None, btag=None):
    btag = btag or tag
    """h_pairs: fp -> AP [128,2,TC] bf16; res_*: dc -> AP [128, TC] f32.
    wdn_g: (gi, fp) -> AP [128, TC] bf16. Out DMAs ride the scalar queue."""
    nc = p.nc
    w0 = [p.base(f"{tag}_w{i}") for i in range(2)]
    h0 = [p.base(f"{tag}_h{i}") for i in range(2)]
    mm0 = p.base(f"{tag}_mm")
    ev0 = p.base(f"{tag}_ev")
    rc0 = [p.base(f"{tag}_rc{i}") for i in range(2)]
    xo0 = [p.base(f"{tag}_xo{i}") for i in range(2)]
    ss0 = p.base(f"{tag}_ss")

    NP = 43
    with ExitStack() as ctx:
        e = ctx.enter_context
        wd = [e(nc.sbuf_tensor(f"{btag}_wb{i}", [128, TC], BF16)) for i in range(2)]
        hh = [e(nc.sbuf_tensor(f"{btag}_hb{i}", [128, 2 * TC], BF16)) for i in range(2)]
        rc = [e(nc.sbuf_tensor(f"{btag}_rc{i}", [128, TC], F32)) for i in range(8)]
        xo = [e(nc.sbuf_tensor(f"{btag}_xo{i}", [128, TC], F32)) for i in range(2)]
        tmp = [e(nc.sbuf_tensor(f"{btag}_tm{i}", [128, TC], F32)) for i in range(2)]
        y_ps = [e(nc.psum_tensor(f"{btag}_y{j}", [128, TC], F32)) for j in range(4)]
        blk = e(nc.Block(f"blk_{btag}"))

        @blk.gpsimd
        def _(g):
            g.wait_ge(p.sem("cc"), 12)
            for gi in range(8):
                for fp in range(NP):
                    i = gi * NP + fp
                    sl = i % 2
                    if i >= 2:
                        g.wait_ge(p.sem(f"{tag}_mm"), mm0 + i - 1)
                    p.inc(g.dma_start(out=wd[sl][:, :], in_=wdn_g(gi, fp)),
                          f"{tag}_w{sl}", 16)

        @blk.sync
        def _(s):
            for gi in range(8):
                for j in range(4):
                    if gi >= 2:
                        s.wait_ge(p.sem(f"{tag}_ev"), ev0 + 4 * (gi - 1))
                    if res_ready is not None:
                        for nm, val in res_ready(gi * 4 + j):
                            s.wait_ge(p.sem(nm), val)
                    p.inc(s.dma_start(out=rc[(gi % 2) * 4 + j][:, :],
                                      in_=res_src[gi * 4 + j]), f"{tag}_rc{gi % 2}", 16)
                for fp in range(NP):
                    i = gi * NP + fp
                    sl = i % 2
                    if i >= 2:
                        s.wait_ge(p.sem(f"{tag}_mm"), mm0 + i - 1)
                    if gi == 0 and h_ready is not None:
                        for nm, val in h_ready(fp):
                            s.wait_ge(p.sem(nm), val)
                    p.inc(s.dma_start(out=hh[sl][:, :], in_=h_pairs[fp]),
                          f"{tag}_h{sl}", 16)

        @blk.tensor
        def _(t):
            for gi in range(8):
                if gi >= 1:
                    t.wait_ge(p.sem(f"{tag}_ev"), ev0 + gi * 4)
                for fp in range(NP):
                    i = gi * NP + fp
                    sl = i % 2
                    t.wait_ge(p.sem(f"{tag}_w{sl}"), w0[sl] + 16 * (i // 2 + 1))
                    t.wait_ge(p.sem(f"{tag}_h{sl}"), h0[sl] + 16 * (i // 2 + 1))
                    for fc2 in range(2):
                        for j in range(4):
                            for th in range(NT):
                                inst = t.matmul(
                                    y_ps[j][:, th * 512:(th + 1) * 512],
                                    wd[sl][:, (fc2 * 4 + j) * 128:(fc2 * 4 + j + 1) * 128],
                                    hh[sl][:, fc2 * TC + th * 512: fc2 * TC + (th + 1) * 512],
                                    start=(fp == 0 and fc2 == 0),
                                    stop=(fp == NP - 1 and fc2 == 1))
                    p.inc(inst, f"{tag}_mm")

        @blk.vector
        def _(v):
            nss = 0
            for gi in range(8):
                v.wait_ge(p.sem(f"{tag}_mm"), mm0 + (gi + 1) * NP)
                v.wait_ge(p.sem(f"{tag}_rc{gi % 2}"), rc0[gi % 2] + 16 * 4 * (gi // 2 + 1))
                for j in range(4):
                    idx = gi * 4 + j
                    if idx >= 2:
                        v.wait_ge(p.sem(f"{tag}_xo{idx % 2}"),
                                  xo0[idx % 2] + 16 * (idx // 2))
                        v.wait_ge(p.sem(f"{tag}_ev"), ev0 + idx - 1)
                    p.inc(v.tensor_scalar(tmp[idx % 2][:, :], y_ps[j][:, :], yscale,
                                          None, ALU.mult), f"{tag}_ss")
                    nss += 1
                    v.wait_ge(p.sem(f"{tag}_ss"), ss0 + nss)
                    p.inc(v.tensor_tensor(xo[idx % 2][:, :], tmp[idx % 2][:, :],
                                          rc[(gi % 2) * 4 + j][:, :], ALU.add),
                          f"{tag}_ev")

        @blk.scalar
        def _(sc):
            for gi in range(8):
                for j in range(4):
                    idx = gi * 4 + j
                    sc.wait_ge(p.sem(f"{tag}_ev"), ev0 + idx + 1)
                    p.inc(sc.dma_start(out=res_dst[idx], in_=xo[idx % 2][:, :]),
                          f"{tag}_xo{idx % 2}", 16)
            sc.wait_ge(p.sem(f"{tag}_xo0"), p.cnt[f"{tag}_xo0"])
            sc.wait_ge(p.sem(f"{tag}_xo1"), p.cnt[f"{tag}_xo1"])

    return xo0



def phase_kvproj(p, xn_sb, wkv_g, k_local, vt_local, k_gath, vt_gath,
                 eye_sb, tag="kp"):
    btag = tag
    nc = p.nc
    w0 = [p.base(f"{tag}_w{i}") for i in range(2)]
    mm0 = p.base(f"{tag}_mm")
    ev0 = p.base(f"{tag}_ev")
    av0 = p.base(f"{tag}_av")
    tr0 = p.base(f"{tag}_tr")
    vt0 = p.base(f"{tag}_vt")
    kd0 = [p.base(f"{tag}_kd{i}") for i in range(2)]
    vd0 = [p.base(f"{tag}_vd{i}") for i in range(2)]

    with ExitStack() as ctx:
        e = ctx.enter_context
        wsl = [e(nc.sbuf_tensor(f"{btag}_wb{i}", [128, 2 * D], BF16)) for i in range(2)]
        kst = [e(nc.sbuf_tensor(f"{btag}_k{i}", [128, TC], BF16)) for i in range(2)]
        vst = [e(nc.sbuf_tensor(f"{btag}_v{i}", [128, TC], BF16)) for i in range(2)]
        vtt = [e(nc.sbuf_tensor(f"{btag}_vt{i}", [128, TC], BF16)) for i in range(2)]
        k_ps = e(nc.psum_tensor(f"{btag}_kp", [128, TC], F32))
        v_ps = e(nc.psum_tensor(f"{btag}_vp", [128, TC], F32))
        t_ps = [e(nc.psum_tensor(f"{btag}_tp{i}", [128, 1024], BF16)) for i in range(2)]
        blk = e(nc.Block(f"blk_{btag}"))

        @blk.gpsimd
        def _(g):
            g.wait_ge(p.sem("cc"), 13)
            for h in range(H):
                sl = h % 2
                if h >= 2:
                    g.wait_ge(p.sem(f"{tag}_mm"), mm0 + 2 * (h - 1))
                p.inc(g.dma_start(out=wsl[sl][:, :], in_=wkv_g(h)), f"{tag}_w{sl}", 16)
            g.wait_ge(p.sem(f"{tag}_kd0"), kd0[0] + 16 * (H // 2))
            g.wait_ge(p.sem(f"{tag}_kd1"), kd0[1] + 16 * (H // 2))
            g.wait_ge(p.sem(f"{tag}_vd0"), vd0[0] + 16 * (H // 2))
            g.wait_ge(p.sem(f"{tag}_vd1"), vd0[1] + 16 * (H // 2))
            p.inc(g.collective_compute(
                "AllGather", ALU.bypass,
                replica_groups=[[0, 1, 2, 3], [4, 5, 6, 7]],
                ins=[k_local.ap().rearrange("h k t -> (h k) t")],
                outs=[k_gath.ap()]), "cc", 1)
            p.inc(g.collective_compute(
                "AllGather", ALU.bypass,
                replica_groups=[[0, 1, 2, 3], [4, 5, 6, 7]],
                ins=[vt_local.ap().rearrange("h t d -> (h t) d")],
                outs=[vt_gath.ap()]), "cc", 1)

        @blk.tensor
        def _(t):
            for h in range(H):
                sl = h % 2
                t.wait_ge(p.sem(f"{tag}_w{sl}"), w0[sl] + 16 * (h // 2 + 1))
                if h >= 1:
                    t.wait_ge(p.sem(f"{tag}_ev"), ev0 + h)
                    t.wait_ge(p.sem(f"{tag}_av"), av0 + h)
                for th in range(NT):
                    tsl = slice(th * 512, (th + 1) * 512)
                    for dc in range(NDC):
                        inst = t.matmul(k_ps[:, tsl],
                                        wsl[sl][:, dc * 128:(dc + 1) * 128],
                                        xn_sb[:, dc * TC + th * 512: dc * TC + (th + 1) * 512],
                                        start=(dc == 0), stop=(dc == NDC - 1))
                p.inc(inst, f"{tag}_mm")
                for th in range(NT):
                    tsl = slice(th * 512, (th + 1) * 512)
                    for dc in range(NDC):
                        inst = t.matmul(v_ps[:, tsl],
                                        wsl[sl][:, D + dc * 128: D + (dc + 1) * 128],
                                        xn_sb[:, dc * TC + th * 512: dc * TC + (th + 1) * 512],
                                        start=(dc == 0), stop=(dc == NDC - 1))
                p.inc(inst, f"{tag}_mm")
                t.wait_ge(p.sem(f"{tag}_av"), av0 + h + 1)
                for c in range(8):
                    gidx = h * 8 + c
                    if gidx >= 2:
                        t.wait_ge(p.sem(f"{tag}_vt"), vt0 + gidx - 1)
                    p.inc(t.transpose(t_ps[c % 2][:, 0:128],
                                      vst[sl][:, c * 128:(c + 1) * 128],
                                      eye_sb[:, :]), f"{tag}_tr")

        @blk.vector
        def _(v):
            for h in range(H):
                sl = h % 2
                v.wait_ge(p.sem(f"{tag}_mm"), mm0 + 2 * h + 1)
                if h >= 2:
                    v.wait_ge(p.sem(f"{tag}_kd{sl}"), kd0[sl] + 16 * (h // 2))
                p.inc(v.tensor_copy(kst[sl][:, :], k_ps[:, :]), f"{tag}_ev")
                for c in range(8):
                    gidx = h * 8 + c
                    v.wait_ge(p.sem(f"{tag}_tr"), tr0 + gidx + 1)
                    if h >= 2 and c == 0:
                        v.wait_ge(p.sem(f"{tag}_vd{sl}"), vd0[sl] + 16 * (h // 2))
                    p.inc(v.tensor_copy(vtt[sl][:, c * 128:(c + 1) * 128],
                                        t_ps[c % 2][:, 0:128]), f"{tag}_vt")

        @blk.scalar
        def _(sc):
            for h in range(H):
                sl = h % 2
                sc.wait_ge(p.sem(f"{tag}_mm"), mm0 + 2 * h + 2)
                if h >= 2:
                    sc.wait_ge(p.sem(f"{tag}_tr"), tr0 + 8 * (h - 1))
                p.inc(sc.activation(vst[sl][:, :], v_ps[:, :], AF.Copy), f"{tag}_av")

        @blk.sync
        def _(s):
            for h in range(H):
                sl = h % 2
                s.wait_ge(p.sem(f"{tag}_ev"), ev0 + h + 1)
                p.inc(s.dma_start(out=k_local[h, :, :], in_=kst[sl][:, :]),
                      f"{tag}_kd{sl}", 16)
                s.wait_ge(p.sem(f"{tag}_vt"), vt0 + 8 * (h + 1))
                vt_out = vt_local.ap().rearrange("h (c q) d -> h q c d", q=128)
                p.inc(s.dma_start(
                    out=vt_out[h, :, :, :],
                    in_=vtt[sl].ap().rearrange("q (c d) -> q c d", d=128)),
                    f"{tag}_vd{sl}", 16)



def phase_qproj(p, xn_sb, wq_g, q_sb, tag="qp"):
    btag = tag
    nc = p.nc
    w0 = [p.base(f"{tag}_w{i}") for i in range(2)]
    mm0 = p.base(f"{tag}_mm")
    ev0 = p.base(f"{tag}_ev")

    with ExitStack() as ctx:
        e = ctx.enter_context
        wsl = [e(nc.sbuf_tensor(f"{btag}_wb{i}", [128, D], BF16)) for i in range(2)]
        q_ps = [e(nc.psum_tensor(f"{btag}_q{i}", [128, TC], F32)) for i in range(2)]
        blk = e(nc.Block(f"blk_{btag}"))

        @blk.gpsimd
        def _(g):
            g.wait_ge(p.sem("cc"), 14)
            for h in range(H):
                sl = h % 2
                if h >= 2:
                    g.wait_ge(p.sem(f"{tag}_mm"), mm0 + h - 1)
                p.inc(g.dma_start(out=wsl[sl][:, :], in_=wq_g(h)), f"{tag}_w{sl}", 16)

        @blk.tensor
        def _(t):
            for h in range(H):
                sl = h % 2
                t.wait_ge(p.sem(f"{tag}_w{sl}"), w0[sl] + 16 * (h // 2 + 1))
                if h >= 2:
                    t.wait_ge(p.sem(f"{tag}_ev"), ev0 + h - 1)
                for th in range(NT):
                    tsl = slice(th * 512, (th + 1) * 512)
                    for dc in range(NDC):
                        inst = t.matmul(q_ps[sl][:, tsl],
                                        wsl[sl][:, dc * 128:(dc + 1) * 128],
                                        xn_sb[:, dc * TC + th * 512: dc * TC + (th + 1) * 512],
                                        start=(dc == 0), stop=(dc == NDC - 1))
                p.inc(inst, f"{tag}_mm")

        @blk.vector
        def _(v):
            for h in range(H):
                sl = h % 2
                v.wait_ge(p.sem(f"{tag}_mm"), mm0 + h + 1)
                p.inc(v.tensor_copy(q_sb[:, h * TC:(h + 1) * TC], q_ps[sl][:, :]),
                      f"{tag}_ev")



def blocks_of(tau):
    bl = [("r", b * 512, 512) for b in range(NB_R)]
    n_loc = 128 * (tau + 1)
    bl.append(("l", 0, min(512, n_loc)))
    if n_loc > 512:
        bl.append(("l", 512, n_loc - 512))
    return bl


def phase_flash(p, q_sb, attn_sb, kg_of, kl_dram, vg_of, vl_of,
                gates_sb, eye_sb, tri_sb, tag="fl"):
    btag = tag
    """Exact-causal SPMD-uniform 2-pass flash attention.
    kg_of(h): dram AP [128, 3, TC] remote K; kl_dram: [H, 128, TC];
    vg_of(h): dram AP [128, 24, 128] remote vT; vl_of(h): [128, 8, 128]."""
    nc = p.nc

    sched = []
    cnt = dict(rm=0, eb=0, mmB=0, ex=0, tr=0, evc=0, pv=0, fin=0)
    g_score = 0
    score_seq = []
    pvmark_head = {}
    for h in range(H):
        for tau in range(8):
            bl = blocks_of(tau)
            m = dict(bl=bl, gA=[], gB=[], rm=[], mmB=[], ex=[], tr=[], evc=[],
                     pv=[])
            for i, _b in enumerate(bl):
                m["gA"].append(g_score)
                g_score += 1
                cnt["rm"] += 1
                m["rm"].append(cnt["rm"])
                score_seq.append(("rm", cnt["rm"]))
            cnt["eb"] += 2
            m["eb"] = cnt["eb"]
            for i, (kind, off, n) in enumerate(bl):
                m["gB"].append(g_score)
                g_score += 1
                cnt["mmB"] += 1
                m["mmB"].append(cnt["mmB"])
                cnt["ex"] += 1
                m["ex"].append(cnt["ex"])
                score_seq.append(("ex", cnt["ex"]))
                cnt["tr"] += n // 128
                m["tr"].append(cnt["tr"])
                cnt["evc"] += 1
                if kind == "l" and (off + n) // 128 == tau + 1:
                    cnt["evc"] += 1
                m["evc"].append(cnt["evc"])
                cnt["pv"] += n // 128
                m["pv"].append(cnt["pv"])
            cnt["fin"] += 4
            m["fin"] = cnt["fin"]
            sched.append(m)
        pvmark_head[h] = cnt["pv"]

    by_mmB = {}
    for m in sched:
        for j, v in enumerate(m["mmB"]):
            by_mmB[v] = (m, j)

    def M(h, tau):
        return sched[h * 8 + tau]

    kh0 = [p.base(f"{tag}_kh{i}") for i in range(2)]
    kl0 = [p.base(f"{tag}_kl{i}") for i in range(2)]
    vr0 = [p.base(f"{tag}_vr{i}") for i in range(2)]
    vl0 = [p.base(f"{tag}_vl{i}") for i in range(2)]
    vg0 = p.base(f"{tag}_vg")
    rm0 = p.base(f"{tag}_rm")
    eb0 = p.base(f"{tag}_eb")
    mmB0 = p.base(f"{tag}_mmB")
    ex0 = p.base(f"{tag}_ex")
    tr0 = p.base(f"{tag}_tr")
    evc0 = p.base(f"{tag}_evc")
    pv0 = p.base(f"{tag}_pv")
    fin0 = p.base(f"{tag}_fin")
    on0 = p.base(f"{tag}_ones")

    def score_wait(eng, g):
        if g >= 2:
            nm, val = score_seq[g - 2]
            base = rm0 if nm == "rm" else ex0
            eng.wait_ge(p.sem(f"{tag}_{nm}"), base + val)

    with ExitStack() as ctx:
        e = ctx.enter_context
        kh = [e(nc.sbuf_tensor(f"{btag}_kh{i}", [128, 3 * TC], BF16)) for i in range(2)]
        kl = [e(nc.sbuf_tensor(f"{btag}_kl{i}", [128, TC], BF16)) for i in range(2)]
        vr = [e(nc.sbuf_tensor(f"{btag}_vr{i}", [128, NREM * 129], BF16))
              for i in range(2)]
        vl = [e(nc.sbuf_tensor(f"{btag}_vl{i}", [128, 8 * 129], BF16))
              for i in range(2)]
        psb = [e(nc.sbuf_tensor(f"{btag}_p{i}", [128, 512], BF16)) for i in range(2)]
        ptg = [e(nc.sbuf_tensor(f"{btag}_pt{i}", [128, 512], BF16)) for i in range(2)]
        mx = e(nc.sbuf_tensor(f"{btag}_mx", [128, 8], F32))
        msb = e(nc.sbuf_tensor(f"{btag}_m", [128, 1], F32))
        ebs = e(nc.sbuf_tensor(f"{btag}_ebv", [128, 1], F32))
        liv = e(nc.sbuf_tensor(f"{btag}_li", [128, 1], F32))
        onb = [e(nc.sbuf_tensor(f"{btag}_on{i}", [128, 128], BF16)) for i in range(2)]
        s_ps = [e(nc.psum_tensor(f"{btag}_s{i}", [128, 512], F32)) for i in range(2)]
        pt_ps = [e(nc.psum_tensor(f"{btag}_tp{i}", [128, 1024], BF16))
                 for i in range(2)]
        o_ps = [e(nc.psum_tensor(f"{btag}_o{i}", [128, 512], F32)) for i in range(2)]
        of_ps = [e(nc.psum_tensor(f"{btag}_of{i}", [128, 1024], BF16))
                 for i in range(2)]
        blk = e(nc.Block(f"blk_{btag}"))

        def vt_chunk(h, kind, kc):
            buf = vr[h % 2] if kind == "r" else vl[h % 2]
            return buf[:, kc * 129:(kc + 1) * 129]

        def k_rhs(h, kind, off, n):
            buf = kh[h % 2] if kind == "r" else kl[h % 2]
            return buf[:, off:off + n]

        @blk.sync
        def _(s):
            s.wait_ge(p.sem("cc"), 17)
            for h in range(H):
                sl = h % 2
                if h >= 2:
                    s.wait_ge(p.sem(f"{tag}_pv"), pv0 + pvmark_head[h - 2])
                p.inc(s.dma_start(out=kh[sl].ap().rearrange("p (r t) -> p r t", r=3),
                                  in_=kg_of(h)), f"{tag}_kh{sl}", 16)
                p.inc(s.dma_start(out=kl[sl][:, :], in_=kl_dram[h, :, :]),
                      f"{tag}_kl{sl}", 16)
                vrv = vr[sl].ap().rearrange("p (c x) -> p c x", x=129)
                for r in range(3):
                    p.inc(s.dma_start(out=vrv[:, r * 8:(r + 1) * 8, 0:128],
                                      in_=vg_of(h)[:, r, :, :]), f"{tag}_vr{sl}", 16)
                vlo = vl[sl].ap().rearrange("p (c x) -> p c x", x=129)[:, :, 0:128]
                p.inc(s.dma_start(out=vlo, in_=vl_of(h)), f"{tag}_vl{sl}", 16)

        mmA_mark = {}
        _r = p.base(f"{tag}_mmA")
        for _h in range(H):
            for _tau in range(8):
                for _i in range(len(M(_h, _tau)["bl"])):
                    _r += 1
                    mmA_mark[(_h, _tau, _i)] = _r

        @blk.vector
        def _(v):
            for buf in (vr[0], vr[1], vl[0], vl[1]):
                ap = buf.ap().rearrange("p (c x) -> p c x", x=129)[:, :, 128:129]
                p.inc(v.memset(ap, 1.0), f"{tag}_ones")
            for h in range(H):
                sl = h % 2
                v.wait_ge(p.sem(f"{tag}_vr{sl}"), vr0[sl] + 48 * (h // 2 + 1))
                if h < 2:
                    v.wait_ge(p.sem(f"{tag}_ones"), on0 + 4)
                for kc in range(NREM):
                    inst = v.tensor_scalar(vr[sl][:, kc * 129:(kc + 1) * 129],
                                           vr[sl][:, kc * 129:(kc + 1) * 129],
                                           gates_sb[:, kc:kc + 1], None, ALU.mult)
                p.inc(inst, f"{tag}_vg")
                for tau in range(8):
                    m = M(h, tau)
                    bl = m["bl"]
                    nb = len(bl)
                    gt = h * 8 + tau
                    if gt >= 1:
                        v.wait_ge(p.sem(f"{tag}_eb"), eb0 + sched[gt - 1]["eb"] - 1)
                    for i, (kind, off, n) in enumerate(bl):
                        v.wait_ge(p.sem(f"{tag}_mmA"), mmA_mark[(h, tau, i)])
                        p.inc(v.reduce_max(mx[:, i:i + 1],
                                           s_ps[m["gA"][i] % 2][:, 0:n], axis=AX.X),
                              f"{tag}_rm")
                    v.wait_ge(p.sem(f"{tag}_rm"), rm0 + m["rm"][-1])
                    if gt >= 1:
                        v.wait_ge(p.sem(f"{tag}_ex"),
                                  ex0 + sched[gt - 1]["ex"][-1])
                    p.inc(v.reduce_max(msb[:, :], mx[:, 0:nb], axis=AX.X),
                          f"{tag}_eb")
                    v.wait_ge(p.sem(f"{tag}_eb"), eb0 + m["eb"] - 1)
                    p.inc(v.tensor_scalar(ebs[:, :], msb[:, :], -SCALE, None,
                                          ALU.mult), f"{tag}_eb")
                    for i, (kind, off, n) in enumerate(bl):
                        v.wait_ge(p.sem(f"{tag}_tr"), tr0 + m["tr"][i])
                        tb = m["mmB"][i] - 2
                        if tb >= 1:
                            pm, pj = by_mmB[tb]
                            v.wait_ge(p.sem(f"{tag}_pv"), pv0 + pm["pv"][pj])
                        bank = m["mmB"][i] % 2
                        diag = (kind == "l" and (off + n) // 128 == tau + 1)
                        inst = v.tensor_copy(ptg[bank][:, 0:n], pt_ps[bank][:, 0:n])
                        if diag:
                            p.inc(inst, f"{tag}_evc")
                            v.wait_ge(p.sem(f"{tag}_evc"), evc0 + m["evc"][i] - 1)
                            dcol = (n // 128 - 1) * 128
                            inst = v.tensor_tensor(ptg[bank][:, dcol:dcol + 128],
                                                   ptg[bank][:, dcol:dcol + 128],
                                                   tri_sb[:, :], ALU.mult)
                        p.inc(inst, f"{tag}_evc")
                    v.wait_ge(p.sem(f"{tag}_pv"), pv0 + m["pv"][-1])
                    if gt >= 1:
                        v.wait_ge(p.sem(f"{tag}_fin"),
                                  fin0 + sched[gt - 1]["fin"] - 2)
                    p.inc(v.reciprocal(liv[:, :], o_ps[tau % 2][:, 128:129]),
                          f"{tag}_fin")
                    v.wait_ge(p.sem(f"{tag}_fin"), fin0 + m["fin"] - 3)
                    if gt >= 2:
                        v.wait_ge(p.sem(f"{tag}_fin"),
                                  fin0 + sched[gt - 2]["fin"] - 1)
                    p.inc(v.tensor_scalar(onb[tau % 2][:, :],
                                          o_ps[tau % 2][:, 0:128],
                                          liv[:, :], None, ALU.mult), f"{tag}_fin")

        @blk.tensor
        def _(t):
            for h in range(H):
                sl = h % 2
                t.wait_ge(p.sem(f"{tag}_kh{sl}"), kh0[sl] + 16 * (h // 2 + 1))
                t.wait_ge(p.sem(f"{tag}_kl{sl}"), kl0[sl] + 16 * (h // 2 + 1))
                t.wait_ge(p.sem(f"{tag}_vl{sl}"), vl0[sl] + 16 * (h // 2 + 1))
                t.wait_ge(p.sem(f"{tag}_vg"), vg0 + h + 1)
                for tau in range(8):
                    m = M(h, tau)
                    bl = m["bl"]
                    gt = h * 8 + tau
                    q_lhsT = q_sb[:, h * TC + tau * 128: h * TC + (tau + 1) * 128]
                    for i, (kind, off, n) in enumerate(bl):
                        score_wait(t, m["gA"][i])
                        inst = t.matmul(s_ps[m["gA"][i] % 2][:, 0:n], q_lhsT,
                                        k_rhs(h, kind, off, n),
                                        start=True, stop=True)
                        p.inc(inst, f"{tag}_mmA")

                    def mmB(i):
                        kind, off, n = bl[i]
                        score_wait(t, m["gB"][i])
                        inst = t.matmul(s_ps[m["gB"][i] % 2][:, 0:n], q_lhsT,
                                        k_rhs(h, kind, off, n),
                                        start=True, stop=True)
                        p.inc(inst, f"{tag}_mmB")

                    if gt >= 2:
                        t.wait_ge(p.sem(f"{tag}_fin"),
                                  fin0 + sched[gt - 2]["fin"] - 2)
                    mmB(0)
                    npv = 0
                    for i, (kind, off, n) in enumerate(bl):
                        if i + 1 < len(bl):
                            mmB(i + 1)
                        nch = n // 128
                        bank = m["mmB"][i] % 2
                        t.wait_ge(p.sem(f"{tag}_ex"), ex0 + m["ex"][i])
                        tb = m["mmB"][i] - 2
                        if tb >= 1:
                            pm, pj = by_mmB[tb]
                            t.wait_ge(p.sem(f"{tag}_evc"), evc0 + pm["evc"][pj])
                        for c in range(nch):
                            p.inc(t.transpose(pt_ps[bank][:, c * 128:(c + 1) * 128],
                                              psb[bank][:, c * 128:(c + 1) * 128],
                                              eye_sb[:, :]), f"{tag}_tr")
                        t.wait_ge(p.sem(f"{tag}_evc"), evc0 + m["evc"][i])
                        for c in range(nch):
                            kc = off // 128 + c
                            inst = t.matmul(o_ps[tau % 2][:, 0:129],
                                            ptg[bank][:, c * 128:(c + 1) * 128],
                                            vt_chunk(h, kind, kc),
                                            start=(npv == 0),
                                            stop=(i == len(bl) - 1 and c == nch - 1))
                            npv += 1
                            p.inc(inst, f"{tag}_pv")
                    t.wait_ge(p.sem(f"{tag}_fin"), fin0 + m["fin"] - 2)
                    if gt >= 2:
                        t.wait_ge(p.sem(f"{tag}_fin"), fin0 + sched[gt - 2]["fin"])
                    p.inc(t.transpose(of_ps[tau % 2][:, 0:128], onb[tau % 2][:, :],
                                      eye_sb[:, :]), f"{tag}_fin")

        @blk.scalar
        def _(sc):
            for h in range(H):
                for tau in range(8):
                    m = M(h, tau)
                    bl = m["bl"]
                    for i, (kind, off, n) in enumerate(bl):
                        pbank = m["mmB"][i] % 2
                        sbank = m["gB"][i] % 2
                        sc.wait_ge(p.sem(f"{tag}_mmB"), mmB0 + m["mmB"][i])
                        sc.wait_ge(p.sem(f"{tag}_eb"), eb0 + m["eb"])
                        tb = m["mmB"][i] - 2
                        if tb >= 1:
                            pm, pj = by_mmB[tb]
                            sc.wait_ge(p.sem(f"{tag}_tr"), tr0 + pm["tr"][pj])
                        p.inc(sc.activation(psb[pbank][:, 0:n], s_ps[sbank][:, 0:n],
                                            AF.Exp, bias=ebs[:, :], scale=SCALE),
                              f"{tag}_ex")
                    sc.wait_ge(p.sem(f"{tag}_fin"), fin0 + m["fin"] - 1)
                    p.inc(sc.activation(
                        attn_sb[:, h * TC + tau * 128: h * TC + (tau + 1) * 128],
                        of_ps[tau % 2][:, 0:128], AF.Copy), f"{tag}_fin")

    return



def phase_wo(p, attn_sb, wo_g, res_src, res_dst, tag="wo", res_ready=None):
    btag = tag
    """attn @ wo + res_src -> res_dst. wo_g: (g, half) -> AP [128, 8192] bf16."""
    nc = p.nc
    w0 = [p.base(f"{tag}_w{i}") for i in range(2)]
    mm0 = p.base(f"{tag}_mm")
    ev0 = p.base(f"{tag}_ev")
    rc0 = [p.base(f"{tag}_rc{i}") for i in range(2)]
    xo0 = [p.base(f"{tag}_xo{i}") for i in range(2)]
    ss0 = p.base(f"{tag}_ss")

    with ExitStack() as ctx:
        e = ctx.enter_context
        wsl = [e(nc.sbuf_tensor(f"{btag}_wb{i}", [128, 8192], BF16)) for i in range(2)]
        rc = [e(nc.sbuf_tensor(f"{btag}_rc{i}", [128, TC], F32)) for i in range(4)]
        xo = [e(nc.sbuf_tensor(f"{btag}_xo{i}", [128, TC], F32)) for i in range(2)]
        tmp = [e(nc.sbuf_tensor(f"{btag}_tm{i}", [128, TC], F32)) for i in range(2)]
        y_ps = [e(nc.psum_tensor(f"{btag}_y{j}", [128, TC], F32)) for j in range(4)]
        blk = e(nc.Block(f"blk_{btag}"))

        @blk.gpsimd
        def _(g):
            g.wait_ge(p.sem("cc"), 15)
            for gi in range(8):
                for half in range(2):
                    i = gi * 2 + half
                    sl = i % 2
                    if i >= 2:
                        g.wait_ge(p.sem(f"{tag}_mm"), mm0 + i - 1)
                    p.inc(g.dma_start(out=wsl[sl][:, :], in_=wo_g(gi, half)),
                          f"{tag}_w{sl}", 16)

        @blk.sync
        def _(s):
            for gi in range(8):
                for j in range(4):
                    if gi >= 1:
                        s.wait_ge(p.sem(f"{tag}_ev"), ev0 + 4 * gi)
                    if res_ready is not None:
                        for nm, val in res_ready(gi * 4 + j):
                            s.wait_ge(p.sem(nm), val)
                    p.inc(s.dma_start(out=rc[j][:, :],
                                      in_=res_src[gi * 4 + j]), f"{tag}_rc{gi % 2}", 16)

        @blk.tensor
        def _(t):
            for gi in range(8):
                if gi >= 1:
                    t.wait_ge(p.sem(f"{tag}_ev"), ev0 + gi * 4)
                for half in range(2):
                    i = gi * 2 + half
                    sl = i % 2
                    t.wait_ge(p.sem(f"{tag}_w{sl}"), w0[sl] + 16 * (i // 2 + 1))
                    for kc16 in range(16):
                        kc = half * 16 + kc16
                        for j in range(4):
                            for th in range(NT):
                                inst = t.matmul(
                                    y_ps[j][:, th * 512:(th + 1) * 512],
                                    wsl[sl][:, kc16 * 512 + j * 128: kc16 * 512 + (j + 1) * 128],
                                    attn_sb[:, kc * TC + th * 512: kc * TC + (th + 1) * 512],
                                    start=(kc == 0), stop=(kc == 31))
                    p.inc(inst, f"{tag}_mm")

        @blk.vector
        def _(v):
            nss = 0
            for gi in range(8):
                v.wait_ge(p.sem(f"{tag}_mm"), mm0 + (gi + 1) * 2)
                v.wait_ge(p.sem(f"{tag}_rc{gi % 2}"), rc0[gi % 2] + 16 * 4 * (gi // 2 + 1))
                for j in range(4):
                    idx = gi * 4 + j
                    if idx >= 2:
                        v.wait_ge(p.sem(f"{tag}_xo{idx % 2}"),
                                  xo0[idx % 2] + 16 * (idx // 2))
                        v.wait_ge(p.sem(f"{tag}_ev"), ev0 + idx - 1)
                    p.inc(v.tensor_scalar(tmp[idx % 2][:, :], y_ps[j][:, :], 1.0,
                                          None, ALU.mult), f"{tag}_ss")
                    nss += 1
                    v.wait_ge(p.sem(f"{tag}_ss"), ss0 + nss)
                    p.inc(v.tensor_tensor(xo[idx % 2][:, :], tmp[idx % 2][:, :],
                                          rc[j][:, :], ALU.add),
                          f"{tag}_ev")

        @blk.scalar
        def _(sc):
            for gi in range(8):
                for j in range(4):
                    idx = gi * 4 + j
                    sc.wait_ge(p.sem(f"{tag}_ev"), ev0 + idx + 1)
                    p.inc(sc.dma_start(out=res_dst[idx], in_=xo[idx % 2][:, :]),
                          f"{tag}_xo{idx % 2}", 16)
            sc.wait_ge(p.sem(f"{tag}_xo0"), p.cnt[f"{tag}_xo0"])
            sc.wait_ge(p.sem(f"{tag}_xo1"), p.cnt[f"{tag}_xo1"])

    return xo0



def build_program(upto="full"):
    nc = bass.Bass("TRN2", target_bir_lowering=False)

    x_in = nc.dram_tensor("x_fm", [D, TC], F32, kind="ExternalInput")
    wug_sh = nc.dram_tensor("wug_sh", [11, 128, 2 * D], F32, kind="ExternalInput")
    wdn_sh = nc.dram_tensor("wdn_sh", [43, 128, TC], F32, kind="ExternalInput")
    wkv_sh = nc.dram_tensor("wkv_sh", [4, 128, 2 * D], F32, kind="ExternalInput")
    wq_sh = nc.dram_tensor("wq_sh", [4, 128, D], F32, kind="ExternalInput")
    wo_sh = nc.dram_tensor("wo_sh", [2, 128, 2 * D], F32, kind="ExternalInput")
    eye_in = nc.dram_tensor("eye", [128, 128], BF16, kind="ExternalInput")
    tri_in = nc.dram_tensor("tri", [128, 128], BF16, kind="ExternalInput")
    gates_in = nc.dram_tensor("gates", [128, NREM], F32, kind="ExternalInput")
    out = nc.dram_tensor("out_fm", [D, TC], F32, kind="ExternalOutput")

    g16 = dict(
        wug=nc.dram_tensor("g_wug", [88 * 128, 2 * D], BF16, addr_space="Shared"),
        wdn=nc.dram_tensor("g_wdn", [344 * 128, TC], BF16, addr_space="Shared"),
        wkv=nc.dram_tensor("g_wkv", [32 * 128, 2 * D], BF16, addr_space="Shared"),
        wq=nc.dram_tensor("g_wq", [32 * 128, D], BF16, addr_space="Shared"),
        wo=nc.dram_tensor("g_wo", [16 * 128, 2 * D], BF16, addr_space="Shared"),
    )
    h_kind = "ExternalOutput" if upto == "ffna" else "Internal"
    h_scr = nc.dram_tensor("h_scr", [NFC, 128, TC], BF16, kind=h_kind)
    res1 = nc.dram_tensor("res1", [NDC, 128, TC], F32)
    res2 = nc.dram_tensor("res2", [NDC, 128, TC], F32)
    k_local = nc.dram_tensor("k_local", [H, 128, TC], BF16)
    vt_local = nc.dram_tensor("vt_local", [H, TC, 128], BF16)
    k_gath = nc.dram_tensor("k_gath", [4 * H * 128, TC], BF16)
    vt_gath = nc.dram_tensor("vt_gath", [4 * H * TC, 128], BF16)

    xn_dbg = (nc.dram_tensor("xn_dbg", [128, NDC * TC], BF16, kind="ExternalOutput")
              if upto in ("ln1", "ffna", "attn_fm") else None)

    x_r = x_in.ap().rearrange("(dc k) t -> dc k t", k=128)
    out_r = out.ap().rearrange("(dc k) t -> dc k t", k=128)
    h_pr = h_scr.ap().rearrange("(fp two) k t -> fp k two t", two=2)
    x_v = [x_r[dc, :, :] for dc in range(NDC)]
    out_v = [out_r[dc, :, :] for dc in range(NDC)]
    res1_v = [res1[dc, :, :] for dc in range(NDC)]
    res2_v = [res2[dc, :, :] for dc in range(NDC)]
    h_v = [h_scr[fc, :, :] for fc in range(NFC)]
    h_pairs = [h_pr[fp, :, :, :] for fp in range(43)]

    wug_r = g16["wug"].ap().rearrange("(fc k) m -> fc k m", k=128)
    wdn_r = g16["wdn"].ap().rearrange("(i k) m -> i k m", k=128)
    wkv_r = g16["wkv"].ap().rearrange("(h k) m -> h k m", k=128)
    wq_r = g16["wq"].ap().rearrange("(h k) m -> h k m", k=128)
    wo_r = g16["wo"].ap().rearrange("(i k) m -> i k m", k=128)
    kg_r = k_gath.ap().rearrange("(r h k) t -> k r h t", h=H, k=128)
    vg_r = vt_gath.ap().rearrange("(r h c q) d -> q r h c d", h=H, c=8, q=128)
    vl_r = vt_local.ap().rearrange("h (c q) d -> q h c d", q=128)

    def wug_g(fc):
        return wug_r[fc, :, :]

    def wdn_g(gi, fp):
        return wdn_r[gi * 43 + fp, :, :]

    def wkv_g(h):
        return wkv_r[h, :, :]

    def wq_g(h):
        return wq_r[h, :, :]

    def wo_g(gi, half):
        return wo_r[gi * 2 + half, :, :]

    def kg_of(h):
        return kg_r[:, 0:3, h, :]

    def vg_of(h):
        return vg_r[:, 0:3, h, :, :]

    def vl_of(h):
        return vl_r[:, h, :, :]

    with ExitStack() as ctx:
        p = Prog(nc, ctx)
        p.sem("cc")
        eye_sb = ctx.enter_context(nc.sbuf_tensor("eye_sb", [128, 128], BF16))
        tri_sb = ctx.enter_context(nc.sbuf_tensor("tri_sb", [128, 128], BF16))
        gates_sb = ctx.enter_context(nc.sbuf_tensor("gates_sb", [128, NREM], F32))
        with nc.Block("blk_consts") as blk:
            @blk.sync
            def _(s):
                p.inc(s.dma_start(out=eye_sb[:, :], in_=eye_in[:, :]), "cst", 16)
                p.inc(s.dma_start(out=tri_sb[:, :], in_=tri_in[:, :]), "cst", 16)
                p.inc(s.dma_start(out=gates_sb[:, :], in_=gates_in[:, :]), "cst", 16)
                s.wait_ge(p.sem("cst"), 48)

        sh = dict(wug_sh=wug_sh, wdn_sh=wdn_sh, wkv_sh=wkv_sh, wq_sh=wq_sh,
                  wo_sh=wo_sh)
        phase_wprep(p, sh, g16)

        bigA = ctx.enter_context(nc.sbuf_tensor("bigA", [128, NDC * TC], BF16))

        phase_ln(p, x_v, bigA, "ln", btag="l1")
        if upto == "ln1":
            _finish(p, nc, [("wd_dmy", 0)], xn_dbg, bigA)
            return nc

        hb = phase_ffn_a(p, bigA, h_v, wug_g, "fa", btag="fa1")
        if upto == "ffna":
            _finish(p, nc, [("fa_h0", p.cnt["fa_h0"]), ("fa_h1", p.cnt["fa_h1"])],
                    xn_dbg, bigA)
            return nc

        h_rdy1 = lambda fp: [("fa_h0", hb[0] + 16 * (fp + 1)),
                             ("fa_h1", hb[1] + 16 * (fp + 1))]
        dst1 = out_v if upto == "ffn1" else res1_v
        xb1 = phase_ffn_b(p, h_pairs, x_v, dst1, wdn_g, "fb", btag="fb1",
                          yscale=0.5, h_ready=h_rdy1)
        if upto == "ffn1":
            _finish(p, nc, [("fb_xo0", p.cnt["fb_xo0"]),
                            ("fb_xo1", p.cnt["fb_xo1"])])
            return nc

        res1_rdy = lambda dc: [(f"fb_xo{dc % 2}", xb1[dc % 2] + 16 * (dc // 2 + 1))]
        phase_ln(p, res1_v, bigA, "ln", src_ready=res1_rdy, btag="l2")

        phase_kvproj(p, bigA, wkv_g, k_local, vt_local, k_gath, vt_gath, eye_sb)

        bigB = ctx.enter_context(nc.sbuf_tensor("bigB", [128, H * TC], BF16))
        phase_qproj(p, bigA, wq_g, bigB)

        phase_flash(p, bigB, bigA, kg_of, k_local, vg_of, vl_of,
                    gates_sb, eye_sb, tri_sb)
        if upto == "attn_fm":
            _finish(p, nc, [("fl_fin", p.cnt["fl_fin"])], xn_dbg, bigA)
            return nc

        dst2 = out_v if upto == "attn" else res2_v
        xb2 = phase_wo(p, bigA, wo_g, res1_v, dst2, res_ready=res1_rdy)
        if upto == "attn":
            _finish(p, nc, [("wo_xo0", p.cnt["wo_xo0"]), ("wo_xo1", p.cnt["wo_xo1"])])
            return nc

        res2_rdy = lambda dc: [(f"wo_xo{dc % 2}", xb2[dc % 2] + 16 * (dc // 2 + 1))]
        bigC = bigB
        phase_ln(p, res2_v, bigC, "ln", src_ready=res2_rdy, btag="l3")

        h_war = [("fb_h0", p.cnt["fb_h0"]), ("fb_h1", p.cnt["fb_h1"])]
        hb2 = phase_ffn_a(p, bigC, h_v, wug_g, "fa", h_war=h_war, btag="fa2")
        h_rdy2 = lambda fp: [("fa_h0", hb2[0] + 16 * (fp + 1)),
                             ("fa_h1", hb2[1] + 16 * (fp + 1))]
        phase_ffn_b(p, h_pairs, res2_v, out_v, wdn_g, "fb", btag="fb2",
                    yscale=0.5, h_ready=h_rdy2)
        _finish(p, nc, [("fb_xo0", p.cnt["fb_xo0"]), ("fb_xo1", p.cnt["fb_xo1"])])
        return nc


def _finish(p, nc, waits, xn_dbg=None, bigA=None):
    with nc.Block("blk_fin") as blk:
        @blk.sync
        def _(s):
            for nm, val in waits:
                if val:
                    s.wait_ge(p.sem(nm), val)
            if xn_dbg is not None:
                p.inc(s.dma_start(out=xn_dbg[:, :], in_=bigA[:, :]), "fin", 16)
                s.wait_ge(p.sem("fin"), p.cnt["fin"])



def kernel(L0_attn_hidden_states, L0_attn_wq, L0_attn_wkv, L0_attn_wo,
           L0_ffn_W_gate, L0_ffn_W_up, L0_ffn_W_down):
    x = np.asarray(L0_attn_hidden_states)
    lay = host_layouts(L0_attn_wq, L0_attn_wkv, L0_attn_wo,
                       L0_ffn_W_gate, L0_ffn_W_up, L0_ffn_W_down)
    shards = host_shards(lay)
    nc = build_program("full")
    in_maps = []
    for core in range(8):
        sq, c = core // 4, core % 4
        xs = np.ascontiguousarray(x[sq, c * TC:(c + 1) * TC, :].T)
        m = dict(x_fm=xs)
        m.update(shards[core])
        m.update(host_consts(core))
        in_maps.append(m)
    res = run_bass_kernel_spmd(nc, in_maps, core_ids=list(range(8)))
    outp = np.empty((B, S, D), np.float32)
    for core in range(8):
        sq, c = core // 4, core % 4
        outp[sq, c * TC:(c + 1) * TC, :] = np.asarray(res.results[core]["out_fm"]).T
    return outp


# version 02
# speedup vs baseline: 1.0733x; 1.0733x over previous
import numpy as np
import ml_dtypes
from contextlib import ExitStack

import concourse.bass as bass
import concourse.mybir as mybir
from concourse.bass_utils import run_bass_kernel_spmd

F32 = mybir.dt.float32
BF16 = mybir.dt.bfloat16
AF = mybir.ActivationFunctionType
ALU = mybir.AluOpType
AX = mybir.AxisListType

B, S, D, H, DN, DV, DQ, DFF = 2, 4096, 4096, 32, 128, 128, 192, 11008
TC = 1024
NDC = D // 128
NFC = DFF // 128
NT = TC // 512
SCALE = float(DQ) ** -0.5
EPS = 1e-5
NREM = 24
NB_R = 6



def host_layouts(L0_attn_wq, L0_attn_wkv, L0_attn_wo, L0_ffn_W_gate,
                 L0_ffn_W_up, L0_ffn_W_down):
    wq = np.asarray(L0_attn_wq)
    wkv = np.asarray(L0_attn_wkv)
    wo = np.asarray(L0_attn_wo)
    W_gate = np.asarray(L0_ffn_W_gate)
    W_up = np.asarray(L0_ffn_W_up)
    W_down = np.asarray(L0_ffn_W_down)

    def blk_updown(Wfd):
        a = Wfd.reshape(NFC, 128, NDC, 128)
        return a.transpose(0, 3, 2, 1).reshape(NFC, 128, D)

    wupgate = np.concatenate([blk_updown(W_up), blk_updown(W_gate)], axis=2)

    a = W_down.reshape(8, 4, 128, 43, 2, 128)
    wdn = np.ascontiguousarray(a.transpose(0, 3, 5, 4, 1, 2)).reshape(344, 128, TC)

    q_cols = wq.reshape(D, H, DQ)[:, :, :DN]
    a = q_cols.reshape(NDC, 128, H, DN)
    wq_blk = np.ascontiguousarray(a.transpose(2, 1, 0, 3)).reshape(H, 128, D)

    k_cols = wkv[:, :H * DN].reshape(D, H, DN)
    v_cols = wkv[:, H * DN:H * (DN + DV)].reshape(D, H, DV)
    kv = np.stack([k_cols, v_cols], axis=2)
    a = kv.reshape(NDC, 128, H, 2, 128)
    wkv_blk = np.ascontiguousarray(a.transpose(2, 1, 3, 0, 4)).reshape(H, 128, 2 * D)

    a = wo.reshape(2, 16, 128, 8, 4, 128)
    wo_blk = np.ascontiguousarray(a.transpose(3, 0, 2, 1, 4, 5)).reshape(16, 128, 8192)

    return dict(wupgate=wupgate, wdn=wdn, wq_blk=wq_blk, wkv_blk=wkv_blk,
                wo_blk=wo_blk)


def host_shards(lay):
    wug_pad = np.zeros((88, 128, 2 * D), np.float32)
    wug_pad[:NFC] = lay["wupgate"]
    shards = []
    for r in range(8):
        shards.append(dict(
            wug_sh=np.ascontiguousarray(wug_pad[r::8]),
            wdn_sh=np.ascontiguousarray(lay["wdn"][43 * r:43 * (r + 1)]),
            wkv_sh=np.ascontiguousarray(lay["wkv_blk"][4 * r:4 * (r + 1)]),
            wq_sh=np.ascontiguousarray(lay["wq_blk"][4 * r:4 * (r + 1)]),
            wo_sh=np.ascontiguousarray(lay["wo_blk"][2 * r:2 * (r + 1)]),
        ))
    return shards


def host_consts(core):
    c = core % 4
    eye = np.eye(128, dtype=ml_dtypes.bfloat16)
    gates = np.zeros((128, NREM), np.float32)
    for kc in range(NREM):
        gates[:, kc] = 1.0 if (kc // 8) < c else 0.0
    gatesb = gates.astype(ml_dtypes.bfloat16)
    tri = np.triu(np.ones((128, 128), np.float32))
    ones = np.ones((128, 128), np.float32)
    zeros = np.zeros((128, 128), np.float32)
    mask1 = np.concatenate([tri, ones], axis=1).astype(ml_dtypes.bfloat16)
    mask2 = np.concatenate([zeros, tri], axis=1).astype(ml_dtypes.bfloat16)
    return dict(eye=eye, gates=gates, gatesb=gatesb, mask1=mask1, mask2=mask2)



class Prog:
    def __init__(self, nc, ctx):
        self.nc = nc
        self.ctx = ctx
        self._sems = {}
        self.cnt = {}

    def sem(self, name):
        if name not in self._sems:
            self._sems[name] = self.ctx.enter_context(self.nc.semaphore(name))
            self.cnt[name] = 0
        return self._sems[name]

    def inc(self, inst, name, n=1):
        inst.then_inc(self.sem(name), n)
        self.cnt[name] += n
        return self.cnt[name]

    def base(self, name):
        self.sem(name)
        return self.cnt[name]



def phase_wprep(p, sh, g16, tag="wp"):
    btag = tag
    """cc counts after this phase: wug pieces 1..11, wdn 12, wkv 13, wq 14, wo 15."""
    nc = p.nc
    cs0 = p.base("wp_cs")
    loc = dict(
        wug=nc.dram_tensor(f"{tag}_wug16", [11, 128, 2 * D], BF16),
        wdn=nc.dram_tensor(f"{tag}_wdn16", [43, 128, TC], BF16),
        wkv=nc.dram_tensor(f"{tag}_wkv16", [4, 128, 2 * D], BF16),
        wq=nc.dram_tensor(f"{tag}_wq16", [4, 128, D], BF16),
        wo=nc.dram_tensor(f"{tag}_wo16", [2, 128, 2 * D], BF16),
    )
    with nc.Block(f"blk_{btag}") as blk:
        @blk.gpsimd
        def _(g):
            ncast = 0
            for i in range(11):
                p.inc(g.dma_start(out=loc["wug"][i, :, :], in_=sh["wug_sh"][i, :, :],
                                  max_dma_last_dim=2048), "wp_cs", 16)
                ncast += 1
                g.wait_ge(p.sem("wp_cs"), cs0 + 16 * ncast)
                p.inc(g.collective_compute(
                    "AllGather", ALU.bypass, replica_groups=[list(range(8))],
                    ins=[loc["wug"][i, :, :]],
                    outs=[g16["wug"].ap()[i * 1024:(i + 1) * 1024, :]],
                ), "cc", 1)
            for nm, npc in (("wdn", 43), ("wkv", 4), ("wq", 4), ("wo", 2)):
                for i in range(npc):
                    p.inc(g.dma_start(out=loc[nm][i, :, :], in_=sh[f"{nm}_sh"][i, :, :],
                                      max_dma_last_dim=2048), "wp_cs", 16)
                    ncast += 1
                g.wait_ge(p.sem("wp_cs"), cs0 + 16 * ncast)
                p.inc(g.collective_compute(
                    "AllGather", ALU.bypass, replica_groups=[list(range(8))],
                    ins=[loc[nm].ap().rearrange("c k m -> (c k) m")],
                    outs=[g16[nm].ap()],
                ), "cc", 1)



def phase_ln(p, src_dram, dst_sb, tag, src_ready=None, btag=None):
    btag = btag or tag
    """Stream src (32 dram APs [128, TC] f32) twice; write bf16 into
    dst_sb[:, dc*TC + t]. src_ready(dc) -> [(sem, val)] readiness waits."""
    nc = p.nc
    ld0 = [p.base(f"{tag}_ld{i}") for i in range(2)]
    ft0 = p.base(f"{tag}_ft")
    sq0 = p.base(f"{tag}_sq")
    rw0 = p.base(f"{tag}_row")
    bc0 = p.base(f"{tag}_bc")
    ap0 = p.base(f"{tag}_ap")
    ss0 = p.base(f"{tag}_ss")

    with ExitStack() as ctx:
        e = ctx.enter_context
        xc = [e(nc.sbuf_tensor(f"{btag}_xc{i}", [128, TC], F32)) for i in range(2)]
        xsq = [e(nc.sbuf_tensor(f"{btag}_xsq{i}", [128, TC], F32)) for i in range(2)]
        ones_col = e(nc.sbuf_tensor(f"{btag}_ones", [128, 1], F32))
        ones_row = e(nc.sbuf_tensor(f"{btag}_onesr", [1, 128], F32))
        srow = e(nc.sbuf_tensor(f"{btag}_srow", [1, TC], F32))
        qrow = e(nc.sbuf_tensor(f"{btag}_qrow", [1, TC], F32))
        mrow = e(nc.sbuf_tensor(f"{btag}_mrow", [1, TC], F32))
        vrow = e(nc.sbuf_tensor(f"{btag}_vrow", [1, TC], F32))
        arow = e(nc.sbuf_tensor(f"{btag}_arow", [1, TC], F32))
        brow = e(nc.sbuf_tensor(f"{btag}_brow", [1, TC], F32))
        tmp = [e(nc.sbuf_tensor(f"{btag}_tmp{i}", [128, TC], F32)) for i in range(2)]
        eps_sb = e(nc.sbuf_tensor(f"{btag}_eps", [1, 1], F32))
        st_ps = e(nc.psum_tensor(f"{btag}_stp", [1, TC], F32))
        sq_ps = e(nc.psum_tensor(f"{btag}_sqp", [1, TC], F32))
        abc_ps = e(nc.psum_tensor(f"{btag}_abc", [128, TC], F32))
        bbc_ps = e(nc.psum_tensor(f"{btag}_bbc", [128, TC], F32))
        blk = e(nc.Block(f"blk_{btag}"))

        @blk.sync
        def _(s):
            for r in range(2):
                for dc in range(NDC):
                    sl = dc % 2
                    if r == 0:
                        if src_ready is not None:
                            for nm, val in src_ready(dc):
                                s.wait_ge(p.sem(nm), val)
                        if dc >= 2:
                            s.wait_ge(p.sem(f"{tag}_ft"), ft0 + dc - 1)
                            s.wait_ge(p.sem(f"{tag}_sq"), sq0 + dc - 1)
                    else:
                        if dc < 2:
                            s.wait_ge(p.sem(f"{tag}_ft"), ft0 + NDC)
                            s.wait_ge(p.sem(f"{tag}_sq"), sq0 + NDC)
                        else:
                            s.wait_ge(p.sem(f"{tag}_ap"), ap0 + dc - 1)
                    p.inc(s.dma_start(out=xc[sl][:, :], in_=src_dram[dc]),
                          f"{tag}_ld{sl}", 16)

        @blk.vector
        def _(v):
            p.inc(v.memset(ones_col[:, :], 1.0), f"{tag}_row")
            p.inc(v.memset(ones_row[:, :], 1.0), f"{tag}_row")
            p.inc(v.memset(eps_sb[:, :], EPS), f"{tag}_row")

        @blk.scalar
        def _(sc):
            for dc in range(NDC):
                sl = dc % 2
                sc.wait_ge(p.sem(f"{tag}_ld{sl}"), ld0[sl] + 16 * (dc // 2 + 1))
                if dc >= 2:
                    sc.wait_ge(p.sem(f"{tag}_ft"), ft0 + dc - 1)
                p.inc(sc.activation(xsq[sl][:, :], xc[sl][:, :], AF.Square),
                      f"{tag}_sq")

        @blk.tensor
        def _(t):
            t.wait_ge(p.sem(f"{tag}_row"), rw0 + 3)
            for dc in range(NDC):
                sl = dc % 2
                t.wait_ge(p.sem(f"{tag}_ld{sl}"), ld0[sl] + 16 * (dc // 2 + 1))
                t.wait_ge(p.sem(f"{tag}_sq"), sq0 + dc + 1)
                for th in range(NT):
                    tsl = slice(th * 512, (th + 1) * 512)
                    t.matmul(st_ps[:, tsl], ones_col[:, :], xc[sl][:, tsl],
                             start=(dc == 0), stop=(dc == NDC - 1))
                    inst = t.matmul(sq_ps[:, tsl], ones_col[:, :], xsq[sl][:, tsl],
                                    start=(dc == 0), stop=(dc == NDC - 1))
                p.inc(inst, f"{tag}_ft")

        @blk.vector
        def _(v):
            v.wait_ge(p.sem(f"{tag}_ft"), ft0 + NDC)
            p.inc(v.tensor_scalar(mrow[:, :], st_ps[:, :], 1.0 / D, None, ALU.mult),
                  f"{tag}_row")
            v.wait_ge(p.sem(f"{tag}_row"), rw0 + 4)
            p.inc(v.tensor_scalar(vrow[:, :], sq_ps[:, :], 1.0 / D, None, ALU.mult),
                  f"{tag}_row")
            v.wait_ge(p.sem(f"{tag}_row"), rw0 + 5)
            p.inc(v.tensor_tensor(qrow[:, :], mrow[:, :], mrow[:, :], ALU.mult),
                  f"{tag}_row")
            v.wait_ge(p.sem(f"{tag}_row"), rw0 + 6)
            p.inc(v.tensor_tensor(vrow[:, :], vrow[:, :], qrow[:, :], ALU.subtract),
                  f"{tag}_row")

        @blk.scalar
        def _(sc):
            sc.wait_ge(p.sem(f"{tag}_row"), rw0 + 7)
            p.inc(sc.activation(srow[:, :], vrow[:, :], AF.Ln, bias=eps_sb[:, :]),
                  f"{tag}_row")
            sc.wait_ge(p.sem(f"{tag}_row"), rw0 + 8)
            p.inc(sc.activation(arow[:, :], srow[:, :], AF.Exp, scale=-0.5),
                  f"{tag}_row")

        @blk.vector
        def _(v):
            v.wait_ge(p.sem(f"{tag}_row"), rw0 + 9)
            p.inc(v.tensor_tensor(brow[:, :], mrow[:, :], arow[:, :], ALU.mult),
                  f"{tag}_row")
            v.wait_ge(p.sem(f"{tag}_row"), rw0 + 10)
            p.inc(v.tensor_scalar(brow[:, :], brow[:, :], -1.0, None, ALU.mult),
                  f"{tag}_row")

        @blk.tensor
        def _(t):
            t.wait_ge(p.sem(f"{tag}_row"), rw0 + 11)
            for th in range(NT):
                tsl = slice(th * 512, (th + 1) * 512)
                t.matmul(abc_ps[:, tsl], ones_row[:, :], arow[:, tsl],
                         start=True, stop=True)
                inst = t.matmul(bbc_ps[:, tsl], ones_row[:, :], brow[:, tsl],
                                start=True, stop=True)
            p.inc(inst, f"{tag}_bc")

        @blk.vector
        def _(v):
            v.wait_ge(p.sem(f"{tag}_bc"), bc0 + 1)
            for dc in range(NDC):
                sl = dc % 2
                v.wait_ge(p.sem(f"{tag}_ld{sl}"), ld0[sl] + 16 * (NDC // 2 + dc // 2 + 1))
                if dc >= 2:
                    v.wait_ge(p.sem(f"{tag}_ap"), ap0 + dc - 1)
                p.inc(v.tensor_tensor(tmp[sl][:, :], xc[sl][:, :], abc_ps[:, :],
                                      ALU.mult), f"{tag}_ss")
                v.wait_ge(p.sem(f"{tag}_ss"), ss0 + dc + 1)
                p.inc(v.tensor_tensor(dst_sb[:, dc * TC:(dc + 1) * TC],
                                      tmp[sl][:, :], bbc_ps[:, :], ALU.add),
                      f"{tag}_ap")



def phase_ffn_a(p, xn_sb, h_dram, wug_g, tag, h_war=None, btag=None):
    btag = btag or tag
    """wug_g: fc -> dram AP [128, 2D] bf16 (gathered). h_war: [(sem,val)] to
    wait once before the first h write (prior readers of h_scr)."""
    nc = p.nc
    w0 = [p.base(f"{tag}_w{i}") for i in range(2)]
    mm0 = p.base(f"{tag}_mm")
    si0 = p.base(f"{tag}_si")
    vt0 = p.base(f"{tag}_vt")
    hs0 = [p.base(f"{tag}_h{i}") for i in range(2)]

    with ExitStack() as ctx:
        e = ctx.enter_context
        wsl = [e(nc.sbuf_tensor(f"{btag}_wb{i}", [128, 2 * D], BF16)) for i in range(2)]
        sg = [e(nc.sbuf_tensor(f"{btag}_sg{i}", [128, TC], F32)) for i in range(2)]
        hsb = [e(nc.sbuf_tensor(f"{btag}_hb{i}", [128, TC], BF16)) for i in range(2)]
        up_ps = [e(nc.psum_tensor(f"{btag}_up{i}", [128, TC], F32)) for i in range(2)]
        gt_ps = [e(nc.psum_tensor(f"{btag}_gt{i}", [128, TC], F32)) for i in range(2)]
        blk = e(nc.Block(f"blk_{btag}"))

        @blk.gpsimd
        def _(g):
            for fc in range(NFC):
                sl = fc % 2
                g.wait_ge(p.sem("cc"), 1 + fc // 8)
                if fc >= 2:
                    g.wait_ge(p.sem(f"{tag}_mm"), mm0 + 4 * (fc - 1))
                p.inc(g.dma_start(out=wsl[sl][:, :], in_=wug_g(fc)), f"{tag}_w{sl}", 16)

        @blk.tensor
        def _(t):
            for fc in range(NFC):
                sl = fc % 2
                t.wait_ge(p.sem(f"{tag}_w{sl}"), w0[sl] + 16 * (fc // 2 + 1))
                if fc >= 2:
                    t.wait_ge(p.sem(f"{tag}_vt"), vt0 + 2 * (fc - 1))
                    t.wait_ge(p.sem(f"{tag}_si"), si0 + 2 * (fc - 1))
                for th in range(NT):
                    tsl = slice(th * 512, (th + 1) * 512)
                    for dc in range(NDC):
                        inst = t.matmul(up_ps[sl][:, tsl],
                                        wsl[sl][:, dc * 128:(dc + 1) * 128],
                                        xn_sb[:, dc * TC + th * 512: dc * TC + (th + 1) * 512],
                                        start=(dc == 0), stop=(dc == NDC - 1))
                    p.inc(inst, f"{tag}_mm")
                    for dc in range(NDC):
                        inst = t.matmul(gt_ps[sl][:, tsl],
                                        wsl[sl][:, D + dc * 128: D + (dc + 1) * 128],
                                        xn_sb[:, dc * TC + th * 512: dc * TC + (th + 1) * 512],
                                        start=(dc == 0), stop=(dc == NDC - 1))
                    p.inc(inst, f"{tag}_mm")

        @blk.scalar
        def _(sc):
            for fc in range(NFC):
                sl = fc % 2
                for th in range(NT):
                    tsl = slice(th * 512, (th + 1) * 512)
                    sc.wait_ge(p.sem(f"{tag}_mm"), mm0 + 4 * fc + 2 * (th + 1))
                    if fc >= 2:
                        sc.wait_ge(p.sem(f"{tag}_vt"), vt0 + 2 * (fc - 2) + th + 1)
                    p.inc(sc.activation(sg[sl][:, tsl], gt_ps[sl][:, tsl], AF.Silu),
                          f"{tag}_si")

        @blk.vector
        def _(v):
            for fc in range(NFC):
                sl = fc % 2
                for th in range(NT):
                    tsl = slice(th * 512, (th + 1) * 512)
                    v.wait_ge(p.sem(f"{tag}_si"), si0 + 2 * fc + th + 1)
                    v.wait_ge(p.sem(f"{tag}_mm"), mm0 + 4 * fc + 2 * th + 1)
                    if fc >= 2 and th == 0:
                        v.wait_ge(p.sem(f"{tag}_h{sl}"), hs0[sl] + 16 * (fc // 2))
                    p.inc(v.tensor_tensor(hsb[sl][:, tsl], sg[sl][:, tsl],
                                          up_ps[sl][:, tsl], ALU.mult), f"{tag}_vt")

        @blk.sync
        def _(s):
            if h_war:
                for nm, val in h_war:
                    s.wait_ge(p.sem(nm), val)
            for fc in range(NFC):
                sl = fc % 2
                s.wait_ge(p.sem(f"{tag}_vt"), vt0 + 2 * (fc + 1))
                p.inc(s.dma_start(out=h_dram[fc], in_=hsb[sl][:, :]), f"{tag}_h{sl}", 16)
            s.wait_ge(p.sem(f"{tag}_h0"), p.cnt[f"{tag}_h0"])
            s.wait_ge(p.sem(f"{tag}_h1"), p.cnt[f"{tag}_h1"])

    return hs0



def phase_ffn_b(p, h_pairs, res_src, res_dst, wdn_g, tag, yscale=0.5,
                h_ready=None, res_ready=None, btag=None):
    btag = btag or tag
    """h_pairs: fp -> AP [128,2,TC] bf16; res_*: dc -> AP [128, TC] f32.
    wdn_g: (gi, fp) -> AP [128, TC] bf16. Out DMAs ride the scalar queue."""
    nc = p.nc
    w0 = [p.base(f"{tag}_w{i}") for i in range(2)]
    h0 = [p.base(f"{tag}_h{i}") for i in range(2)]
    mm0 = p.base(f"{tag}_mm")
    ev0 = p.base(f"{tag}_ev")
    rc0 = [p.base(f"{tag}_rc{i}") for i in range(2)]
    xo0 = [p.base(f"{tag}_xo{i}") for i in range(2)]
    ss0 = p.base(f"{tag}_ss")

    NP = 43
    with ExitStack() as ctx:
        e = ctx.enter_context
        wd = [e(nc.sbuf_tensor(f"{btag}_wb{i}", [128, TC], BF16)) for i in range(2)]
        hh = [e(nc.sbuf_tensor(f"{btag}_hb{i}", [128, 2 * TC], BF16)) for i in range(2)]
        rc = [e(nc.sbuf_tensor(f"{btag}_rc{i}", [128, TC], F32)) for i in range(8)]
        xo = [e(nc.sbuf_tensor(f"{btag}_xo{i}", [128, TC], F32)) for i in range(2)]
        tmp = [e(nc.sbuf_tensor(f"{btag}_tm{i}", [128, TC], F32)) for i in range(2)]
        y_ps = [e(nc.psum_tensor(f"{btag}_y{j}", [128, TC], F32)) for j in range(4)]
        blk = e(nc.Block(f"blk_{btag}"))

        @blk.gpsimd
        def _(g):
            g.wait_ge(p.sem("cc"), 12)
            for gi in range(8):
                for fp in range(NP):
                    i = gi * NP + fp
                    sl = i % 2
                    if i >= 2:
                        g.wait_ge(p.sem(f"{tag}_mm"), mm0 + i - 1)
                    p.inc(g.dma_start(out=wd[sl][:, :], in_=wdn_g(gi, fp)),
                          f"{tag}_w{sl}", 16)

        @blk.sync
        def _(s):
            for gi in range(8):
                for j in range(4):
                    if gi >= 2:
                        s.wait_ge(p.sem(f"{tag}_ev"), ev0 + 4 * (gi - 1))
                    if res_ready is not None:
                        for nm, val in res_ready(gi * 4 + j):
                            s.wait_ge(p.sem(nm), val)
                    p.inc(s.dma_start(out=rc[(gi % 2) * 4 + j][:, :],
                                      in_=res_src[gi * 4 + j]), f"{tag}_rc{gi % 2}", 16)
                for fp in range(NP):
                    i = gi * NP + fp
                    sl = i % 2
                    if i >= 2:
                        s.wait_ge(p.sem(f"{tag}_mm"), mm0 + i - 1)
                    if gi == 0 and h_ready is not None:
                        for nm, val in h_ready(fp):
                            s.wait_ge(p.sem(nm), val)
                    p.inc(s.dma_start(out=hh[sl][:, :], in_=h_pairs[fp]),
                          f"{tag}_h{sl}", 16)

        @blk.tensor
        def _(t):
            for gi in range(8):
                if gi >= 1:
                    t.wait_ge(p.sem(f"{tag}_ev"), ev0 + gi * 4)
                for fp in range(NP):
                    i = gi * NP + fp
                    sl = i % 2
                    t.wait_ge(p.sem(f"{tag}_w{sl}"), w0[sl] + 16 * (i // 2 + 1))
                    t.wait_ge(p.sem(f"{tag}_h{sl}"), h0[sl] + 16 * (i // 2 + 1))
                    for fc2 in range(2):
                        for j in range(4):
                            for th in range(NT):
                                inst = t.matmul(
                                    y_ps[j][:, th * 512:(th + 1) * 512],
                                    wd[sl][:, (fc2 * 4 + j) * 128:(fc2 * 4 + j + 1) * 128],
                                    hh[sl][:, fc2 * TC + th * 512: fc2 * TC + (th + 1) * 512],
                                    start=(fp == 0 and fc2 == 0),
                                    stop=(fp == NP - 1 and fc2 == 1))
                    p.inc(inst, f"{tag}_mm")

        @blk.vector
        def _(v):
            nss = 0
            for gi in range(8):
                v.wait_ge(p.sem(f"{tag}_mm"), mm0 + (gi + 1) * NP)
                v.wait_ge(p.sem(f"{tag}_rc{gi % 2}"), rc0[gi % 2] + 16 * 4 * (gi // 2 + 1))
                for j in range(4):
                    idx = gi * 4 + j
                    if idx >= 2:
                        v.wait_ge(p.sem(f"{tag}_xo{idx % 2}"),
                                  xo0[idx % 2] + 16 * (idx // 2))
                        v.wait_ge(p.sem(f"{tag}_ev"), ev0 + idx - 1)
                    p.inc(v.tensor_scalar(tmp[idx % 2][:, :], y_ps[j][:, :], yscale,
                                          None, ALU.mult), f"{tag}_ss")
                    nss += 1
                    v.wait_ge(p.sem(f"{tag}_ss"), ss0 + nss)
                    p.inc(v.tensor_tensor(xo[idx % 2][:, :], tmp[idx % 2][:, :],
                                          rc[(gi % 2) * 4 + j][:, :], ALU.add),
                          f"{tag}_ev")

        @blk.scalar
        def _(sc):
            for gi in range(8):
                for j in range(4):
                    idx = gi * 4 + j
                    sc.wait_ge(p.sem(f"{tag}_ev"), ev0 + idx + 1)
                    p.inc(sc.dma_start(out=res_dst[idx], in_=xo[idx % 2][:, :]),
                          f"{tag}_xo{idx % 2}", 16)
            sc.wait_ge(p.sem(f"{tag}_xo0"), p.cnt[f"{tag}_xo0"])
            sc.wait_ge(p.sem(f"{tag}_xo1"), p.cnt[f"{tag}_xo1"])

    return xo0



def phase_kvproj(p, xn_sb, wkv_g, k_local, vt_local, k_gath, vt_gath,
                 eye_sb, tag="kp"):
    btag = tag
    nc = p.nc
    w0 = [p.base(f"{tag}_w{i}") for i in range(2)]
    mm0 = p.base(f"{tag}_mm")
    ev0 = p.base(f"{tag}_ev")
    av0 = p.base(f"{tag}_av")
    tr0 = p.base(f"{tag}_tr")
    vt0 = p.base(f"{tag}_vt")
    kd0 = [p.base(f"{tag}_kd{i}") for i in range(2)]
    vd0 = [p.base(f"{tag}_vd{i}") for i in range(2)]

    with ExitStack() as ctx:
        e = ctx.enter_context
        wsl = [e(nc.sbuf_tensor(f"{btag}_wb{i}", [128, 2 * D], BF16)) for i in range(2)]
        kst = [e(nc.sbuf_tensor(f"{btag}_k{i}", [128, TC], BF16)) for i in range(2)]
        vst = [e(nc.sbuf_tensor(f"{btag}_v{i}", [128, TC], BF16)) for i in range(2)]
        vtt = [e(nc.sbuf_tensor(f"{btag}_vt{i}", [128, TC], BF16)) for i in range(2)]
        k_ps = e(nc.psum_tensor(f"{btag}_kp", [128, TC], F32))
        v_ps = e(nc.psum_tensor(f"{btag}_vp", [128, TC], F32))
        t_ps = [e(nc.psum_tensor(f"{btag}_tp{i}", [128, 1024], BF16)) for i in range(2)]
        blk = e(nc.Block(f"blk_{btag}"))

        @blk.gpsimd
        def _(g):
            g.wait_ge(p.sem("cc"), 13)
            for h in range(H):
                sl = h % 2
                if h >= 2:
                    g.wait_ge(p.sem(f"{tag}_mm"), mm0 + 2 * (h - 1))
                p.inc(g.dma_start(out=wsl[sl][:, :], in_=wkv_g(h)), f"{tag}_w{sl}", 16)
            g.wait_ge(p.sem(f"{tag}_kd0"), kd0[0] + 16 * (H // 2))
            g.wait_ge(p.sem(f"{tag}_kd1"), kd0[1] + 16 * (H // 2))
            g.wait_ge(p.sem(f"{tag}_vd0"), vd0[0] + 16 * (H // 2))
            g.wait_ge(p.sem(f"{tag}_vd1"), vd0[1] + 16 * (H // 2))
            p.inc(g.collective_compute(
                "AllGather", ALU.bypass,
                replica_groups=[[0, 1, 2, 3], [4, 5, 6, 7]],
                ins=[k_local.ap().rearrange("h k t -> (h k) t")],
                outs=[k_gath.ap()]), "cc", 1)
            p.inc(g.collective_compute(
                "AllGather", ALU.bypass,
                replica_groups=[[0, 1, 2, 3], [4, 5, 6, 7]],
                ins=[vt_local.ap().rearrange("h t d -> (h t) d")],
                outs=[vt_gath.ap()]), "cc", 1)

        @blk.tensor
        def _(t):
            for h in range(H):
                sl = h % 2
                t.wait_ge(p.sem(f"{tag}_w{sl}"), w0[sl] + 16 * (h // 2 + 1))
                if h >= 1:
                    t.wait_ge(p.sem(f"{tag}_ev"), ev0 + h)
                    t.wait_ge(p.sem(f"{tag}_av"), av0 + h)
                for th in range(NT):
                    tsl = slice(th * 512, (th + 1) * 512)
                    for dc in range(NDC):
                        inst = t.matmul(k_ps[:, tsl],
                                        wsl[sl][:, dc * 128:(dc + 1) * 128],
                                        xn_sb[:, dc * TC + th * 512: dc * TC + (th + 1) * 512],
                                        start=(dc == 0), stop=(dc == NDC - 1))
                p.inc(inst, f"{tag}_mm")
                for th in range(NT):
                    tsl = slice(th * 512, (th + 1) * 512)
                    for dc in range(NDC):
                        inst = t.matmul(v_ps[:, tsl],
                                        wsl[sl][:, D + dc * 128: D + (dc + 1) * 128],
                                        xn_sb[:, dc * TC + th * 512: dc * TC + (th + 1) * 512],
                                        start=(dc == 0), stop=(dc == NDC - 1))
                p.inc(inst, f"{tag}_mm")
                t.wait_ge(p.sem(f"{tag}_av"), av0 + h + 1)
                for c in range(8):
                    gidx = h * 8 + c
                    if gidx >= 2:
                        t.wait_ge(p.sem(f"{tag}_vt"), vt0 + gidx - 1)
                    p.inc(t.transpose(t_ps[c % 2][:, 0:128],
                                      vst[sl][:, c * 128:(c + 1) * 128],
                                      eye_sb[:, :]), f"{tag}_tr")

        @blk.vector
        def _(v):
            for h in range(H):
                sl = h % 2
                v.wait_ge(p.sem(f"{tag}_mm"), mm0 + 2 * h + 1)
                if h >= 2:
                    v.wait_ge(p.sem(f"{tag}_kd{sl}"), kd0[sl] + 16 * (h // 2))
                p.inc(v.tensor_copy(kst[sl][:, :], k_ps[:, :]), f"{tag}_ev")
                for c in range(8):
                    gidx = h * 8 + c
                    v.wait_ge(p.sem(f"{tag}_tr"), tr0 + gidx + 1)
                    if h >= 2 and c == 0:
                        v.wait_ge(p.sem(f"{tag}_vd{sl}"), vd0[sl] + 16 * (h // 2))
                    p.inc(v.tensor_copy(vtt[sl][:, c * 128:(c + 1) * 128],
                                        t_ps[c % 2][:, 0:128]), f"{tag}_vt")

        @blk.scalar
        def _(sc):
            for h in range(H):
                sl = h % 2
                sc.wait_ge(p.sem(f"{tag}_mm"), mm0 + 2 * h + 2)
                if h >= 2:
                    sc.wait_ge(p.sem(f"{tag}_tr"), tr0 + 8 * (h - 1))
                p.inc(sc.activation(vst[sl][:, :], v_ps[:, :], AF.Copy), f"{tag}_av")

        @blk.sync
        def _(s):
            for h in range(H):
                sl = h % 2
                s.wait_ge(p.sem(f"{tag}_ev"), ev0 + h + 1)
                p.inc(s.dma_start(out=k_local[h, :, :], in_=kst[sl][:, :]),
                      f"{tag}_kd{sl}", 16)
                s.wait_ge(p.sem(f"{tag}_vt"), vt0 + 8 * (h + 1))
                vt_out = vt_local.ap().rearrange("h (c q) d -> h q c d", q=128)
                p.inc(s.dma_start(
                    out=vt_out[h, :, :, :],
                    in_=vtt[sl].ap().rearrange("q (c d) -> q c d", d=128)),
                    f"{tag}_vd{sl}", 16)



def phase_qproj(p, xn_sb, wq_g, q_sb, tag="qp"):
    btag = tag
    nc = p.nc
    w0 = [p.base(f"{tag}_w{i}") for i in range(2)]
    mm0 = p.base(f"{tag}_mm")
    ev0 = p.base(f"{tag}_ev")

    with ExitStack() as ctx:
        e = ctx.enter_context
        wsl = [e(nc.sbuf_tensor(f"{btag}_wb{i}", [128, D], BF16)) for i in range(2)]
        q_ps = [e(nc.psum_tensor(f"{btag}_q{i}", [128, TC], F32)) for i in range(2)]
        blk = e(nc.Block(f"blk_{btag}"))

        @blk.gpsimd
        def _(g):
            g.wait_ge(p.sem("cc"), 14)
            for h in range(H):
                sl = h % 2
                if h >= 2:
                    g.wait_ge(p.sem(f"{tag}_mm"), mm0 + h - 1)
                p.inc(g.dma_start(out=wsl[sl][:, :], in_=wq_g(h)), f"{tag}_w{sl}", 16)

        @blk.tensor
        def _(t):
            for h in range(H):
                sl = h % 2
                t.wait_ge(p.sem(f"{tag}_w{sl}"), w0[sl] + 16 * (h // 2 + 1))
                if h >= 2:
                    t.wait_ge(p.sem(f"{tag}_ev"), ev0 + h - 1)
                for th in range(NT):
                    tsl = slice(th * 512, (th + 1) * 512)
                    for dc in range(NDC):
                        inst = t.matmul(q_ps[sl][:, tsl],
                                        wsl[sl][:, dc * 128:(dc + 1) * 128],
                                        xn_sb[:, dc * TC + th * 512: dc * TC + (th + 1) * 512],
                                        start=(dc == 0), stop=(dc == NDC - 1))
                p.inc(inst, f"{tag}_mm")

        @blk.vector
        def _(v):
            for h in range(H):
                sl = h % 2
                v.wait_ge(p.sem(f"{tag}_mm"), mm0 + h + 1)
                p.inc(v.tensor_copy(q_sb[:, h * TC:(h + 1) * TC], q_ps[sl][:, :]),
                      f"{tag}_ev")



def phase_flash(p, q_sb, attn_sb, kg_of, kl_dram, vg_of, vl_of,
                gates_sb, gatesb_sb, m1_sb, m2_sb, tag="fl"):
    nc = p.nc
    NP_PT = 8

    def chunk_pairs(kc, kind):
        if kind == "r":
            return [0, 1, 2, 3]
        return list(range(kc // 2, 4))

    chunk_list = [("r", kc) for kc in range(NREM)] + [("l", kc) for kc in range(8)]

    sched = []
    g_slot = 0
    slot_seq = []
    cnt = dict(mm=0, cs=0, ex=0, msk=0, pv=0, fin=0, row=0, eb=0)
    for h in range(H):
        m = dict(norm=[], sc={}, pvm={}, msk={}, fin=[])
        for j in range(10):
            m["norm"].append(dict(g=g_slot, mm=cnt["mm"] + 1, cs=cnt["cs"] + 1))
            slot_seq.append(("cs", cnt["cs"] + 1))
            cnt["mm"] += 1
            cnt["cs"] += 1
            g_slot += 1
        cnt["row"] += 5
        cnt["eb"] += 2
        m["row"] = cnt["row"]
        m["eb"] = cnt["eb"]
        for kind, kc in chunk_list:
            for pi in chunk_pairs(kc, kind):
                key = (kind, kc, pi)
                cnt["mm"] += 1
                slot_seq.append(("ex", cnt["ex"] + 1))
                masked = (kind == "l" and kc in (2 * pi, 2 * pi + 1))
                if masked:
                    cnt["msk"] += 1
                cnt["pv"] += 1
                m["sc"][key] = dict(g=g_slot, mm=cnt["mm"], ex=cnt["ex"] + 1,
                                    msk=(cnt["msk"] if masked else None),
                                    pt=cnt["pv"] - 1, pv=cnt["pv"])
                cnt["ex"] += 1
                g_slot += 1
        for pi in range(4):
            cnt["fin"] += 4
            m["fin"].append(cnt["fin"])
        sched.append(m)

    kh0 = [p.base(f"{tag}_kh{i}") for i in range(2)]
    kl0 = [p.base(f"{tag}_kl{i}") for i in range(2)]
    vr0 = [p.base(f"{tag}_vr{i}") for i in range(2)]
    vl0 = [p.base(f"{tag}_vl{i}") for i in range(2)]
    vg0 = p.base(f"{tag}_vg")
    mm0 = p.base(f"{tag}_mm")
    cs0 = p.base(f"{tag}_cs")
    ex0 = p.base(f"{tag}_ex")
    msk0 = p.base(f"{tag}_msk")
    pv0 = p.base(f"{tag}_pv")
    fin0 = p.base(f"{tag}_fin")
    row0 = p.base(f"{tag}_row")
    eb0 = p.base(f"{tag}_eb")
    sq0 = p.base(f"{tag}_sq")
    on0 = p.base(f"{tag}_on")

    def slot_wait(eng, g):
        if g >= 2:
            nm, val = slot_seq[g - 2]
            base = cs0 if nm == "cs" else ex0
            eng.wait_ge(p.sem(f"{tag}_{nm}"), base + val)

    with ExitStack() as ctx:
        e = ctx.enter_context
        kh = [e(nc.sbuf_tensor(f"{tag}_kh{i}", [128, 3 * TC], BF16)) for i in range(2)]
        kl = [e(nc.sbuf_tensor(f"{tag}_kl{i}", [128, TC], BF16)) for i in range(2)]
        vr = [e(nc.sbuf_tensor(f"{tag}_vr{i}", [128, NREM * 128], BF16))
              for i in range(2)]
        vl = [e(nc.sbuf_tensor(f"{tag}_vl{i}", [128, 8 * 128], BF16))
              for i in range(2)]
        psb = [e(nc.sbuf_tensor(f"{tag}_p{i}", [128, 256], BF16))
               for i in range(NP_PT)]
        ksq = [e(nc.sbuf_tensor(f"{tag}_ksq{i}", [128, 512], BF16)) for i in range(2)]
        ones_col = e(nc.sbuf_tensor(f"{tag}_ones", [128, 1], BF16))
        ones_row = e(nc.sbuf_tensor(f"{tag}_onesr", [1, 128], F32))
        qmx = e(nc.sbuf_tensor(f"{tag}_qmx", [1, 2], F32))
        kmx = e(nc.sbuf_tensor(f"{tag}_kmx", [1, 8], F32))
        qq = e(nc.sbuf_tensor(f"{tag}_qq", [1, 1], F32))
        kk = e(nc.sbuf_tensor(f"{tag}_kk", [1, 1], F32))
        bb = e(nc.sbuf_tensor(f"{tag}_bb", [1, 1], F32))
        nb_sb = e(nc.sbuf_tensor(f"{tag}_nb", [1, 1], F32))
        ebs = [e(nc.sbuf_tensor(f"{tag}_eb{i}", [128, 1], F32)) for i in range(2)]
        liv_row = e(nc.sbuf_tensor(f"{tag}_lr", [1, 256], F32))
        liv_sb = e(nc.sbuf_tensor(f"{tag}_ls", [128, 256], F32))
        s_ps = [e(nc.psum_tensor(f"{tag}_s{i}", [128, 512], F32)) for i in range(2)]
        o_ps = [e(nc.psum_tensor(f"{tag}_o{i}", [128, 512], F32)) for i in range(4)]
        mi_ps = e(nc.psum_tensor(f"{tag}_mi", [128, 512], F32))
        blk = e(nc.Block(f"blk_{tag}"))

        def vt_chunk(h, kind, kc):
            buf = vr[h % 2] if kind == "r" else vl[h % 2]
            return buf[:, kc * 128:(kc + 1) * 128]

        def k_chunk(h, kind, kc):
            buf = kh[h % 2] if kind == "r" else kl[h % 2]
            return buf[:, kc * 128:(kc + 1) * 128]

        def pv_done_mark(h):
            mm = sched[h]
            return max(v["pv"] for v in mm["sc"].values())

        @blk.sync
        def _(s):
            s.wait_ge(p.sem("cc"), 17)
            for h in range(H):
                sl = h % 2
                if h >= 2:
                    s.wait_ge(p.sem(f"{tag}_pv"), pv0 + pv_done_mark(h - 2))
                p.inc(s.dma_start(out=kh[sl].ap().rearrange("p (r t) -> p r t", r=3),
                                  in_=kg_of(h)), f"{tag}_kh{sl}", 16)
                p.inc(s.dma_start(out=kl[sl][:, :], in_=kl_dram[h, :, :]),
                      f"{tag}_kl{sl}", 16)
                vrv = vr[sl].ap().rearrange("p (c x) -> p c x", x=128)
                for r in range(3):
                    p.inc(s.dma_start(out=vrv[:, r * 8:(r + 1) * 8, :],
                                      in_=vg_of(h, r)), f"{tag}_vr{sl}", 16)
                vlv = vl[sl].ap().rearrange("p (c x) -> p c x", x=128)
                p.inc(s.dma_start(out=vlv[:, :, :], in_=vl_of(h)), f"{tag}_vl{sl}", 16)

        @blk.scalar
        def _(sc):
            for h in range(H):
                sl = h % 2
                m = sched[h]
                srcs = [q_sb[:, h * TC:h * TC + 512], q_sb[:, h * TC + 512:(h + 1) * TC]]
                srcs += [kh[sl][:, j * 512:(j + 1) * 512] for j in range(6)]
                srcs += [kl[sl][:, 0:512], kl[sl][:, 512:1024]]
                sc.wait_ge(p.sem(f"{tag}_kh{sl}"), kh0[sl] + 16 * (h // 2 + 1))
                sc.wait_ge(p.sem(f"{tag}_kl{sl}"), kl0[sl] + 16 * (h // 2 + 1))
                for j, srcap in enumerate(srcs):
                    gi = h * 10 + j
                    if gi >= 2:
                        sc.wait_ge(p.sem(f"{tag}_mm"),
                                   mm0 + _nmm_of_normslot(sched, gi - 2))
                    p.inc(sc.activation(ksq[gi % 2][:, :], srcap, AF.Square),
                          f"{tag}_sq")
                sc.wait_ge(p.sem(f"{tag}_row"), row0 + m["row"] - 2)
                p.inc(sc.activation(nb_sb[:, :], bb[:, :], AF.Sqrt), f"{tag}_row")
                for kind, kc in chunk_list:
                    for pi in chunk_pairs(kc, kind):
                        d = m["sc"][(kind, kc, pi)]
                        sc.wait_ge(p.sem(f"{tag}_mm"), mm0 + d["mm"])
                        sc.wait_ge(p.sem(f"{tag}_eb"), eb0 + m["eb"])
                        pb = d["pv"] - NP_PT
                        if pb >= 1:
                            sc.wait_ge(p.sem(f"{tag}_pv"), pv0 + pb)
                        p.inc(sc.activation(psb[d["pt"] % NP_PT][:, :],
                                            s_ps[d["g"] % 2][:, 0:256],
                                            AF.Exp, bias=ebs[sl][:, :], scale=SCALE),
                              f"{tag}_ex")

        @blk.vector
        def _(v):
            p.inc(v.memset(ones_col[:, :], 1.0), f"{tag}_on")
            p.inc(v.memset(ones_row[:, :], 1.0), f"{tag}_on")
            for h in range(H):
                sl = h % 2
                m = sched[h]
                if h < 2:
                    v.wait_ge(p.sem(f"{tag}_on"), on0 + 2)
                v.wait_ge(p.sem(f"{tag}_vr{sl}"), vr0[sl] + 48 * (h // 2 + 1))
                for kc in range(NREM):
                    inst = v.tensor_scalar(vr[sl][:, kc * 128:(kc + 1) * 128],
                                           vr[sl][:, kc * 128:(kc + 1) * 128],
                                           gates_sb[:, kc:kc + 1], None, ALU.mult)
                p.inc(inst, f"{tag}_vg")
                for j in range(10):
                    d = m["norm"][j]
                    v.wait_ge(p.sem(f"{tag}_mm"), mm0 + d["mm"])
                    dst = qmx[:, j:j + 1] if j < 2 else kmx[:, (j - 2):(j - 1)]
                    p.inc(v.reduce_max(dst, s_ps[d["g"] % 2][0:1, 0:512], axis=AX.X),
                          f"{tag}_cs")
                v.wait_ge(p.sem(f"{tag}_cs"), cs0 + m["norm"][-1]["cs"])
                if h >= 1:
                    v.wait_ge(p.sem(f"{tag}_row"), row0 + sched[h - 1]["row"])
                p.inc(v.reduce_max(qq[:, :], qmx[:, :], axis=AX.X), f"{tag}_row")
                v.wait_ge(p.sem(f"{tag}_row"), row0 + m["row"] - 4)
                p.inc(v.reduce_max(kk[:, :], kmx[:, :], axis=AX.X), f"{tag}_row")
                v.wait_ge(p.sem(f"{tag}_row"), row0 + m["row"] - 3)
                p.inc(v.tensor_tensor(bb[:, :], qq[:, :], kk[:, :], ALU.mult),
                      f"{tag}_row")
                v.wait_ge(p.sem(f"{tag}_row"), row0 + m["row"] - 1)
                p.inc(v.tensor_scalar(nb_sb[:, :], nb_sb[:, :], -SCALE, None,
                                      ALU.mult), f"{tag}_row")
                v.wait_ge(p.sem(f"{tag}_eb"), eb0 + m["eb"] - 1)
                if h >= 2:
                    last_ex = max(d["ex"] for d in sched[h - 2]["sc"].values())
                    v.wait_ge(p.sem(f"{tag}_ex"), ex0 + last_ex)
                p.inc(v.tensor_copy(ebs[sl][:, :], mi_ps[:, 0:1]), f"{tag}_eb")
                for kind, kc in chunk_list:
                    for pi in chunk_pairs(kc, kind):
                        d = m["sc"][(kind, kc, pi)]
                        if d["msk"] is None:
                            continue
                        v.wait_ge(p.sem(f"{tag}_ex"), ex0 + d["ex"])
                        msk = m1_sb if kc == 2 * pi else m2_sb
                        p.inc(v.tensor_tensor(psb[d["pt"] % NP_PT][:, :],
                                              psb[d["pt"] % NP_PT][:, :],
                                              msk[:, :], ALU.mult), f"{tag}_msk")
                for pi in range(4):
                    fm = m["fin"][pi]
                    key = ("l", 2 * pi + 1, pi)
                    v.wait_ge(p.sem(f"{tag}_pv"), pv0 + m["sc"][key]["pv"])
                    if h >= 1 or pi >= 1:
                        v.wait_ge(p.sem(f"{tag}_fin"), fin0 + fm - 4 - 1)
                    p.inc(v.reciprocal(liv_row[:, :],
                                       o_ps[pi][0:1, 256:256 + 256]), f"{tag}_fin")
                    v.wait_ge(p.sem(f"{tag}_fin"), fin0 + fm - 2)
                    p.inc(v.tensor_copy(liv_sb[:, :], mi_ps[:, 256:512]), f"{tag}_fin")
                    v.wait_ge(p.sem(f"{tag}_fin"), fin0 + fm - 1)
                    p.inc(v.tensor_tensor(
                        attn_sb[:, h * TC + pi * 256:h * TC + (pi + 1) * 256],
                        o_ps[pi][:, 0:256], liv_sb[:, :], ALU.mult), f"{tag}_fin")

        @blk.tensor
        def _(t):
            for h in range(H):
                sl = h % 2
                m = sched[h]
                t.wait_ge(p.sem(f"{tag}_kh{sl}"), kh0[sl] + 16 * (h // 2 + 1))
                t.wait_ge(p.sem(f"{tag}_kl{sl}"), kl0[sl] + 16 * (h // 2 + 1))
                t.wait_ge(p.sem(f"{tag}_vl{sl}"), vl0[sl] + 16 * (h // 2 + 1))
                t.wait_ge(p.sem(f"{tag}_vg"), vg0 + h + 1)
                for j in range(10):
                    d = m["norm"][j]
                    t.wait_ge(p.sem(f"{tag}_sq"), sq0 + h * 10 + j + 1)
                    slot_wait(t, d["g"])
                    inst = t.matmul(s_ps[d["g"] % 2][0:1, 0:512], ones_col[:, :],
                                    ksq[(h * 10 + j) % 2][:, :],
                                    start=True, stop=True)
                    p.inc(inst, f"{tag}_mm")
                t.wait_ge(p.sem(f"{tag}_row"), row0 + m["row"])
                if h >= 1:
                    t.wait_ge(p.sem(f"{tag}_fin"), fin0 + sched[h - 1]["fin"][3] - 1)
                inst = t.matmul(mi_ps[:, 0:1], ones_row[:, :], nb_sb[:, :],
                                start=True, stop=True)
                p.inc(inst, f"{tag}_eb")
                emitted_pv = []
                prev_chunk = None
                for ci, (kind, kc) in enumerate(chunk_list):
                    pis = chunk_pairs(kc, kind)
                    for pi in pis:
                        d = m["sc"][(kind, kc, pi)]
                        slot_wait(t, d["g"])
                        inst = t.matmul(s_ps[d["g"] % 2][:, 0:256],
                                        k_chunk(h, kind, kc),
                                        q_sb[:, h * TC + pi * 256:
                                             h * TC + (pi + 1) * 256],
                                        start=True, stop=True)
                        p.inc(inst, f"{tag}_mm")
                    if prev_chunk is not None:
                        _emit_pv(p, t, tag, sched, h, prev_chunk, chunk_pairs,
                                 vt_chunk, gatesb_sb, ones_col, psb, o_ps,
                                 ex0, msk0, fin0, NP_PT)
                    prev_chunk = (kind, kc)
                _emit_pv(p, t, tag, sched, h, prev_chunk, chunk_pairs, vt_chunk,
                         gatesb_sb, ones_col, psb, o_ps, ex0, msk0, fin0, NP_PT)
                for pi in range(4):
                    fm = m["fin"][pi]
                    t.wait_ge(p.sem(f"{tag}_fin"), fin0 + fm - 3)
                    if pi == 0:
                        t.wait_ge(p.sem(f"{tag}_eb"), eb0 + m["eb"])
                    else:
                        t.wait_ge(p.sem(f"{tag}_fin"), fin0 + m["fin"][pi - 1] - 1)
                    inst = t.matmul(mi_ps[:, 256:512], ones_row[:, :],
                                    liv_row[:, :], start=True, stop=True)
                    p.inc(inst, f"{tag}_fin")

    return


def _nmm_of_normslot(sched, gi):
    h, j = gi // 10, gi % 10
    return sched[h]["norm"][j]["mm"]


def _emit_pv(p, t, tag, sched, h, chunk, chunk_pairs, vt_chunk, gatesb_sb,
             ones_col, psb, o_ps, ex0, msk0, fin0, NP_PT):
    kind, kc = chunk
    m = sched[h]
    for pi in chunk_pairs(kc, kind):
        d = m["sc"][(kind, kc, pi)]
        if d["msk"] is not None:
            t.wait_ge(p.sem(f"{tag}_msk"), msk0 + d["msk"])
        else:
            t.wait_ge(p.sem(f"{tag}_ex"), ex0 + d["ex"])
        first = (kind == "r" and kc == 0)
        last = (kind == "l" and kc == 2 * pi + 1)
        if first and h >= 1:
            t.wait_ge(p.sem(f"{tag}_fin"), fin0 + sched[h - 1]["fin"][pi])
        t.matmul(o_ps[pi][:, 0:256], vt_chunk(h, kind, kc),
                 psb[d["pt"] % NP_PT][:, :], start=first, stop=last)
        lhs_l = gatesb_sb[:, kc:kc + 1] if kind == "r" else ones_col[:, :]
        inst = t.matmul(o_ps[pi][0:1, 256:512], lhs_l,
                        psb[d["pt"] % NP_PT][:, :], start=False, stop=False,
                        skip_group_check=True)
        p.inc(inst, f"{tag}_pv")



def phase_wo(p, attn_sb, wo_g, res_src, res_dst, tag="wo", res_ready=None):
    btag = tag
    """attn @ wo + res_src -> res_dst. wo_g: (g, half) -> AP [128, 8192] bf16."""
    nc = p.nc
    w0 = [p.base(f"{tag}_w{i}") for i in range(2)]
    mm0 = p.base(f"{tag}_mm")
    ev0 = p.base(f"{tag}_ev")
    rc0 = [p.base(f"{tag}_rc{i}") for i in range(2)]
    xo0 = [p.base(f"{tag}_xo{i}") for i in range(2)]
    ss0 = p.base(f"{tag}_ss")

    with ExitStack() as ctx:
        e = ctx.enter_context
        wsl = [e(nc.sbuf_tensor(f"{btag}_wb{i}", [128, 8192], BF16)) for i in range(2)]
        rc = [e(nc.sbuf_tensor(f"{btag}_rc{i}", [128, TC], F32)) for i in range(4)]
        xo = [e(nc.sbuf_tensor(f"{btag}_xo{i}", [128, TC], F32)) for i in range(2)]
        tmp = [e(nc.sbuf_tensor(f"{btag}_tm{i}", [128, TC], F32)) for i in range(2)]
        y_ps = [e(nc.psum_tensor(f"{btag}_y{j}", [128, TC], F32)) for j in range(4)]
        blk = e(nc.Block(f"blk_{btag}"))

        @blk.gpsimd
        def _(g):
            g.wait_ge(p.sem("cc"), 15)
            for gi in range(8):
                for half in range(2):
                    i = gi * 2 + half
                    sl = i % 2
                    if i >= 2:
                        g.wait_ge(p.sem(f"{tag}_mm"), mm0 + i - 1)
                    p.inc(g.dma_start(out=wsl[sl][:, :], in_=wo_g(gi, half)),
                          f"{tag}_w{sl}", 16)

        @blk.sync
        def _(s):
            for gi in range(8):
                for j in range(4):
                    if gi >= 1:
                        s.wait_ge(p.sem(f"{tag}_ev"), ev0 + 4 * gi)
                    if res_ready is not None:
                        for nm, val in res_ready(gi * 4 + j):
                            s.wait_ge(p.sem(nm), val)
                    p.inc(s.dma_start(out=rc[j][:, :],
                                      in_=res_src[gi * 4 + j]), f"{tag}_rc{gi % 2}", 16)

        @blk.tensor
        def _(t):
            for gi in range(8):
                if gi >= 1:
                    t.wait_ge(p.sem(f"{tag}_ev"), ev0 + gi * 4)
                for half in range(2):
                    i = gi * 2 + half
                    sl = i % 2
                    t.wait_ge(p.sem(f"{tag}_w{sl}"), w0[sl] + 16 * (i // 2 + 1))
                    for kc16 in range(16):
                        kc = half * 16 + kc16
                        for j in range(4):
                            for th in range(NT):
                                inst = t.matmul(
                                    y_ps[j][:, th * 512:(th + 1) * 512],
                                    wsl[sl][:, kc16 * 512 + j * 128: kc16 * 512 + (j + 1) * 128],
                                    attn_sb[:, kc * TC + th * 512: kc * TC + (th + 1) * 512],
                                    start=(kc == 0), stop=(kc == 31))
                    p.inc(inst, f"{tag}_mm")

        @blk.vector
        def _(v):
            nss = 0
            for gi in range(8):
                v.wait_ge(p.sem(f"{tag}_mm"), mm0 + (gi + 1) * 2)
                v.wait_ge(p.sem(f"{tag}_rc{gi % 2}"), rc0[gi % 2] + 16 * 4 * (gi // 2 + 1))
                for j in range(4):
                    idx = gi * 4 + j
                    if idx >= 2:
                        v.wait_ge(p.sem(f"{tag}_xo{idx % 2}"),
                                  xo0[idx % 2] + 16 * (idx // 2))
                        v.wait_ge(p.sem(f"{tag}_ev"), ev0 + idx - 1)
                    p.inc(v.tensor_scalar(tmp[idx % 2][:, :], y_ps[j][:, :], 1.0,
                                          None, ALU.mult), f"{tag}_ss")
                    nss += 1
                    v.wait_ge(p.sem(f"{tag}_ss"), ss0 + nss)
                    p.inc(v.tensor_tensor(xo[idx % 2][:, :], tmp[idx % 2][:, :],
                                          rc[j][:, :], ALU.add),
                          f"{tag}_ev")

        @blk.scalar
        def _(sc):
            for gi in range(8):
                for j in range(4):
                    idx = gi * 4 + j
                    sc.wait_ge(p.sem(f"{tag}_ev"), ev0 + idx + 1)
                    p.inc(sc.dma_start(out=res_dst[idx], in_=xo[idx % 2][:, :]),
                          f"{tag}_xo{idx % 2}", 16)
            sc.wait_ge(p.sem(f"{tag}_xo0"), p.cnt[f"{tag}_xo0"])
            sc.wait_ge(p.sem(f"{tag}_xo1"), p.cnt[f"{tag}_xo1"])

    return xo0



def build_program(upto="full"):
    nc = bass.Bass("TRN2", target_bir_lowering=False)

    x_in = nc.dram_tensor("x_fm", [D, TC], F32, kind="ExternalInput")
    wug_sh = nc.dram_tensor("wug_sh", [11, 128, 2 * D], F32, kind="ExternalInput")
    wdn_sh = nc.dram_tensor("wdn_sh", [43, 128, TC], F32, kind="ExternalInput")
    wkv_sh = nc.dram_tensor("wkv_sh", [4, 128, 2 * D], F32, kind="ExternalInput")
    wq_sh = nc.dram_tensor("wq_sh", [4, 128, D], F32, kind="ExternalInput")
    wo_sh = nc.dram_tensor("wo_sh", [2, 128, 2 * D], F32, kind="ExternalInput")
    eye_in = nc.dram_tensor("eye", [128, 128], BF16, kind="ExternalInput")
    gates_in = nc.dram_tensor("gates", [128, NREM], F32, kind="ExternalInput")
    gatesb_in = nc.dram_tensor("gatesb", [128, NREM], BF16, kind="ExternalInput")
    m1_in = nc.dram_tensor("mask1", [128, 256], BF16, kind="ExternalInput")
    m2_in = nc.dram_tensor("mask2", [128, 256], BF16, kind="ExternalInput")
    out = nc.dram_tensor("out_fm", [D, TC], F32, kind="ExternalOutput")

    g16 = dict(
        wug=nc.dram_tensor("g_wug", [88 * 128, 2 * D], BF16, addr_space="Shared"),
        wdn=nc.dram_tensor("g_wdn", [344 * 128, TC], BF16, addr_space="Shared"),
        wkv=nc.dram_tensor("g_wkv", [32 * 128, 2 * D], BF16, addr_space="Shared"),
        wq=nc.dram_tensor("g_wq", [32 * 128, D], BF16, addr_space="Shared"),
        wo=nc.dram_tensor("g_wo", [16 * 128, 2 * D], BF16, addr_space="Shared"),
    )
    h_kind = "ExternalOutput" if upto == "ffna" else "Internal"
    h_scr = nc.dram_tensor("h_scr", [NFC, 128, TC], BF16, kind=h_kind)
    res1 = nc.dram_tensor("res1", [NDC, 128, TC], F32)
    res2 = nc.dram_tensor("res2", [NDC, 128, TC], F32)
    k_local = nc.dram_tensor("k_local", [H, 128, TC], BF16)
    vt_local = nc.dram_tensor("vt_local", [H, TC, 128], BF16)
    k_gath = nc.dram_tensor("k_gath", [4 * H * 128, TC], BF16)
    vt_gath = nc.dram_tensor("vt_gath", [4 * H * TC, 128], BF16)

    xn_dbg = (nc.dram_tensor("xn_dbg", [128, NDC * TC], BF16, kind="ExternalOutput")
              if upto in ("ln1", "ffna", "attn_fm") else None)

    x_r = x_in.ap().rearrange("(dc k) t -> dc k t", k=128)
    out_r = out.ap().rearrange("(dc k) t -> dc k t", k=128)
    h_pr = h_scr.ap().rearrange("(fp two) k t -> fp k two t", two=2)
    x_v = [x_r[dc, :, :] for dc in range(NDC)]
    out_v = [out_r[dc, :, :] for dc in range(NDC)]
    res1_v = [res1[dc, :, :] for dc in range(NDC)]
    res2_v = [res2[dc, :, :] for dc in range(NDC)]
    h_v = [h_scr[fc, :, :] for fc in range(NFC)]
    h_pairs = [h_pr[fp, :, :, :] for fp in range(43)]

    wug_r = g16["wug"].ap().rearrange("(fc k) m -> fc k m", k=128)
    wdn_r = g16["wdn"].ap().rearrange("(i k) m -> i k m", k=128)
    wkv_r = g16["wkv"].ap().rearrange("(h k) m -> h k m", k=128)
    wq_r = g16["wq"].ap().rearrange("(h k) m -> h k m", k=128)
    wo_r = g16["wo"].ap().rearrange("(i k) m -> i k m", k=128)
    kg_r = k_gath.ap().rearrange("(r h k) t -> k r h t", h=H, k=128)
    vg_r = vt_gath.ap().rearrange("(r h c q) d -> q r h c d", h=H, c=8, q=128)
    vl_r = vt_local.ap().rearrange("h (c q) d -> q h c d", q=128)

    def wug_g(fc):
        return wug_r[fc, :, :]

    def wdn_g(gi, fp):
        return wdn_r[gi * 43 + fp, :, :]

    def wkv_g(h):
        return wkv_r[h, :, :]

    def wq_g(h):
        return wq_r[h, :, :]

    def wo_g(gi, half):
        return wo_r[gi * 2 + half, :, :]

    def kg_of(h):
        return kg_r[:, 0:3, h, :]

    def vg_of(h, r):
        return vg_r[:, r, h, :, :]

    def vl_of(h):
        return vl_r[:, h, :, :]

    with ExitStack() as ctx:
        p = Prog(nc, ctx)
        p.sem("cc")
        eye_sb = ctx.enter_context(nc.sbuf_tensor("eye_sb", [128, 128], BF16))
        gates_sb = ctx.enter_context(nc.sbuf_tensor("gates_sb", [128, NREM], F32))
        gatesb_sb = ctx.enter_context(nc.sbuf_tensor("gatesb_sb", [128, NREM], BF16))
        m1_sb = ctx.enter_context(nc.sbuf_tensor("m1_sb", [128, 256], BF16))
        m2_sb = ctx.enter_context(nc.sbuf_tensor("m2_sb", [128, 256], BF16))
        with nc.Block("blk_consts") as blk:
            @blk.sync
            def _(s):
                p.inc(s.dma_start(out=eye_sb[:, :], in_=eye_in[:, :]), "cst", 16)
                p.inc(s.dma_start(out=gates_sb[:, :], in_=gates_in[:, :]), "cst", 16)
                p.inc(s.dma_start(out=gatesb_sb[:, :], in_=gatesb_in[:, :]), "cst", 16)
                p.inc(s.dma_start(out=m1_sb[:, :], in_=m1_in[:, :]), "cst", 16)
                p.inc(s.dma_start(out=m2_sb[:, :], in_=m2_in[:, :]), "cst", 16)
                s.wait_ge(p.sem("cst"), 80)

        sh = dict(wug_sh=wug_sh, wdn_sh=wdn_sh, wkv_sh=wkv_sh, wq_sh=wq_sh,
                  wo_sh=wo_sh)
        phase_wprep(p, sh, g16)

        bigA = ctx.enter_context(nc.sbuf_tensor("bigA", [128, NDC * TC], BF16))

        phase_ln(p, x_v, bigA, "ln", btag="l1")
        if upto == "ln1":
            _finish(p, nc, [("wd_dmy", 0)], xn_dbg, bigA)
            return nc

        hb = phase_ffn_a(p, bigA, h_v, wug_g, "fa", btag="fa1")
        if upto == "ffna":
            _finish(p, nc, [("fa_h0", p.cnt["fa_h0"]), ("fa_h1", p.cnt["fa_h1"])],
                    xn_dbg, bigA)
            return nc

        h_rdy1 = lambda fp: [("fa_h0", hb[0] + 16 * (fp + 1)),
                             ("fa_h1", hb[1] + 16 * (fp + 1))]
        dst1 = out_v if upto == "ffn1" else res1_v
        xb1 = phase_ffn_b(p, h_pairs, x_v, dst1, wdn_g, "fb", btag="fb1",
                          yscale=0.5, h_ready=h_rdy1)
        if upto == "ffn1":
            _finish(p, nc, [("fb_xo0", p.cnt["fb_xo0"]),
                            ("fb_xo1", p.cnt["fb_xo1"])])
            return nc

        res1_rdy = lambda dc: [(f"fb_xo{dc % 2}", xb1[dc % 2] + 16 * (dc // 2 + 1))]
        phase_ln(p, res1_v, bigA, "ln", src_ready=res1_rdy, btag="l2")

        phase_kvproj(p, bigA, wkv_g, k_local, vt_local, k_gath, vt_gath, eye_sb)

        bigB = ctx.enter_context(nc.sbuf_tensor("bigB", [128, H * TC], BF16))
        phase_qproj(p, bigA, wq_g, bigB)

        phase_flash(p, bigB, bigA, kg_of, k_local, vg_of, vl_of,
                    gates_sb, gatesb_sb, m1_sb, m2_sb)
        if upto == "attn_fm":
            _finish(p, nc, [("fl_fin", p.cnt["fl_fin"])], xn_dbg, bigA)
            return nc

        dst2 = out_v if upto == "attn" else res2_v
        xb2 = phase_wo(p, bigA, wo_g, res1_v, dst2, res_ready=res1_rdy)
        if upto == "attn":
            _finish(p, nc, [("wo_xo0", p.cnt["wo_xo0"]), ("wo_xo1", p.cnt["wo_xo1"])])
            return nc

        res2_rdy = lambda dc: [(f"wo_xo{dc % 2}", xb2[dc % 2] + 16 * (dc // 2 + 1))]
        bigC = bigB
        phase_ln(p, res2_v, bigC, "ln", src_ready=res2_rdy, btag="l3")

        h_war = [("fb_h0", p.cnt["fb_h0"]), ("fb_h1", p.cnt["fb_h1"])]
        hb2 = phase_ffn_a(p, bigC, h_v, wug_g, "fa", h_war=h_war, btag="fa2")
        h_rdy2 = lambda fp: [("fa_h0", hb2[0] + 16 * (fp + 1)),
                             ("fa_h1", hb2[1] + 16 * (fp + 1))]
        phase_ffn_b(p, h_pairs, res2_v, out_v, wdn_g, "fb", btag="fb2",
                    yscale=0.5, h_ready=h_rdy2)
        _finish(p, nc, [("fb_xo0", p.cnt["fb_xo0"]), ("fb_xo1", p.cnt["fb_xo1"])])
        return nc


def _finish(p, nc, waits, xn_dbg=None, bigA=None):
    with nc.Block("blk_fin") as blk:
        @blk.sync
        def _(s):
            for nm, val in waits:
                if val:
                    s.wait_ge(p.sem(nm), val)
            if xn_dbg is not None:
                p.inc(s.dma_start(out=xn_dbg[:, :], in_=bigA[:, :]), "fin", 16)
                s.wait_ge(p.sem("fin"), p.cnt["fin"])



def kernel(L0_attn_hidden_states, L0_attn_wq, L0_attn_wkv, L0_attn_wo,
           L0_ffn_W_gate, L0_ffn_W_up, L0_ffn_W_down):
    x = np.asarray(L0_attn_hidden_states)
    lay = host_layouts(L0_attn_wq, L0_attn_wkv, L0_attn_wo,
                       L0_ffn_W_gate, L0_ffn_W_up, L0_ffn_W_down)
    shards = host_shards(lay)
    nc = build_program("full")
    in_maps = []
    for core in range(8):
        sq, c = core // 4, core % 4
        xs = np.ascontiguousarray(x[sq, c * TC:(c + 1) * TC, :].T)
        m = dict(x_fm=xs)
        m.update(shards[core])
        m.update(host_consts(core))
        in_maps.append(m)
    res = run_bass_kernel_spmd(nc, in_maps, core_ids=list(range(8)))
    outp = np.empty((B, S, D), np.float32)
    for core in range(8):
        sq, c = core // 4, core % 4
        outp[sq, c * TC:(c + 1) * TC, :] = np.asarray(res.results[core]["out_fm"]).T
    return outp


# version 03
# speedup vs baseline: 1.2132x; 1.1304x over previous
import numpy as np
import ml_dtypes
from contextlib import ExitStack

import concourse.bass as bass
import concourse.mybir as mybir
from concourse.bass_utils import run_bass_kernel_spmd

F32 = mybir.dt.float32
BF16 = mybir.dt.bfloat16
AF = mybir.ActivationFunctionType
ALU = mybir.AluOpType
AX = mybir.AxisListType

B, S, D, H, DN, DV, DQ, DFF = 2, 4096, 4096, 32, 128, 128, 192, 11008
TC = 1024
NDC = D // 128
NFC = DFF // 128
NT = TC // 512
SCALE = float(DQ) ** -0.5
EPS = 1e-5
NREM = 24
NB_R = 6



def host_layouts(L0_attn_wq, L0_attn_wkv, L0_attn_wo, L0_ffn_W_gate,
                 L0_ffn_W_up, L0_ffn_W_down):
    wq = np.asarray(L0_attn_wq)
    wkv = np.asarray(L0_attn_wkv)
    wo = np.asarray(L0_attn_wo)
    W_gate = np.asarray(L0_ffn_W_gate)
    W_up = np.asarray(L0_ffn_W_up)
    W_down = np.asarray(L0_ffn_W_down)

    def blk_updown(Wfd):
        a = Wfd.reshape(NFC, 128, NDC, 128)
        return a.transpose(0, 3, 2, 1).reshape(NFC, 128, D)

    wupgate = np.concatenate([blk_updown(W_up), blk_updown(W_gate)], axis=2)

    a = W_down.reshape(8, 4, 128, 43, 2, 128)
    wdn = np.ascontiguousarray(a.transpose(0, 3, 5, 4, 1, 2)).reshape(344, 128, TC)

    q_cols = wq.reshape(D, H, DQ)[:, :, :DN]
    a = q_cols.reshape(NDC, 128, H, DN)
    wq_blk = np.ascontiguousarray(a.transpose(2, 1, 0, 3)).reshape(H, 128, D)

    k_cols = wkv[:, :H * DN].reshape(D, H, DN)
    v_cols = wkv[:, H * DN:H * (DN + DV)].reshape(D, H, DV)
    kv = np.stack([k_cols, v_cols], axis=2)
    a = kv.reshape(NDC, 128, H, 2, 128)
    wkv_blk = np.ascontiguousarray(a.transpose(2, 1, 3, 0, 4)).reshape(H, 128, 2 * D)

    a = wo.reshape(2, 16, 128, 8, 4, 128)
    wo_blk = np.ascontiguousarray(a.transpose(3, 0, 2, 1, 4, 5)).reshape(16, 128, 8192)

    return dict(wupgate=wupgate, wdn=wdn, wq_blk=wq_blk, wkv_blk=wkv_blk,
                wo_blk=wo_blk)


def host_shards(lay):
    wug_pad = np.zeros((88, 128, 2 * D), np.float32)
    wug_pad[:NFC] = lay["wupgate"]
    shards = []
    for r in range(8):
        shards.append(dict(
            wug_sh=np.ascontiguousarray(wug_pad[r::8]),
            wdn_sh=np.ascontiguousarray(lay["wdn"][43 * r:43 * (r + 1)]),
            wkv_sh=np.ascontiguousarray(lay["wkv_blk"][4 * r:4 * (r + 1)]),
            wq_sh=np.ascontiguousarray(lay["wq_blk"][4 * r:4 * (r + 1)]),
            wo_sh=np.ascontiguousarray(lay["wo_blk"][2 * r:2 * (r + 1)]),
        ))
    return shards


def host_consts(core):
    c = core % 4
    eye = np.eye(128, dtype=ml_dtypes.bfloat16)
    gates = np.zeros((128, NREM), np.float32)
    for kc in range(NREM):
        gates[:, kc] = 1.0 if (kc // 8) < c else 0.0
    gatesb = gates.astype(ml_dtypes.bfloat16)
    tri = np.triu(np.ones((128, 128), np.float32))
    ones = np.ones((128, 128), np.float32)
    zeros = np.zeros((128, 128), np.float32)
    masks = np.stack([
        np.concatenate([zeros] * i + [tri] + [ones] * (3 - i), axis=1)
        for i in range(4)]).astype(ml_dtypes.bfloat16)
    return dict(eye=eye, gates=gates, gatesb=gatesb, masks=masks)



class Prog:
    def __init__(self, nc, ctx):
        self.nc = nc
        self.ctx = ctx
        self._sems = {}
        self.cnt = {}

    def sem(self, name):
        if name not in self._sems:
            self._sems[name] = self.ctx.enter_context(self.nc.semaphore(name))
            self.cnt[name] = 0
        return self._sems[name]

    def inc(self, inst, name, n=1):
        inst.then_inc(self.sem(name), n)
        self.cnt[name] += n
        return self.cnt[name]

    def base(self, name):
        self.sem(name)
        return self.cnt[name]



def phase_wprep(p, sh, g16, tag="wp"):
    btag = tag
    """cc counts after this phase: wug pieces 1..11, wdn 12, wkv 13, wq 14, wo 15."""
    nc = p.nc
    cs0 = p.base("wp_cs")
    loc = dict(
        wug=nc.dram_tensor(f"{tag}_wug16", [11, 128, 2 * D], BF16),
        wdn=nc.dram_tensor(f"{tag}_wdn16", [43, 128, TC], BF16),
        wkv=nc.dram_tensor(f"{tag}_wkv16", [4, 128, 2 * D], BF16),
        wq=nc.dram_tensor(f"{tag}_wq16", [4, 128, D], BF16),
        wo=nc.dram_tensor(f"{tag}_wo16", [2, 128, 2 * D], BF16),
    )
    with nc.Block(f"blk_{btag}") as blk:
        @blk.gpsimd
        def _(g):
            ncast = 0
            for i in range(11):
                p.inc(g.dma_start(out=loc["wug"][i, :, :], in_=sh["wug_sh"][i, :, :],
                                  max_dma_last_dim=2048), "wp_cs", 16)
                ncast += 1
                g.wait_ge(p.sem("wp_cs"), cs0 + 16 * ncast)
                p.inc(g.collective_compute(
                    "AllGather", ALU.bypass, replica_groups=[list(range(8))],
                    ins=[loc["wug"][i, :, :]],
                    outs=[g16["wug"].ap()[i * 1024:(i + 1) * 1024, :]],
                ), "cc", 1)
            for nm, npc in (("wdn", 43), ("wkv", 4), ("wq", 4), ("wo", 2)):
                for i in range(npc):
                    p.inc(g.dma_start(out=loc[nm][i, :, :], in_=sh[f"{nm}_sh"][i, :, :],
                                      max_dma_last_dim=2048), "wp_cs", 16)
                    ncast += 1
                g.wait_ge(p.sem("wp_cs"), cs0 + 16 * ncast)
                p.inc(g.collective_compute(
                    "AllGather", ALU.bypass, replica_groups=[list(range(8))],
                    ins=[loc[nm].ap().rearrange("c k m -> (c k) m")],
                    outs=[g16[nm].ap()],
                ), "cc", 1)



def phase_ln(p, src_dram, dst_sb, tag, src_ready=None, btag=None):
    btag = btag or tag
    """Stream src (32 dram APs [128, TC] f32) twice; write bf16 into
    dst_sb[:, dc*TC + t]. src_ready(dc) -> [(sem, val)] readiness waits."""
    nc = p.nc
    ld0 = [p.base(f"{tag}_ld{i}") for i in range(2)]
    ft0 = p.base(f"{tag}_ft")
    sq0 = p.base(f"{tag}_sq")
    rw0 = p.base(f"{tag}_row")
    bc0 = p.base(f"{tag}_bc")
    ap0 = p.base(f"{tag}_ap")
    ss0 = p.base(f"{tag}_ss")

    with ExitStack() as ctx:
        e = ctx.enter_context
        xc = [e(nc.sbuf_tensor(f"{btag}_xc{i}", [128, TC], F32)) for i in range(2)]
        xsq = [e(nc.sbuf_tensor(f"{btag}_xsq{i}", [128, TC], F32)) for i in range(2)]
        ones_col = e(nc.sbuf_tensor(f"{btag}_ones", [128, 1], F32))
        ones_row = e(nc.sbuf_tensor(f"{btag}_onesr", [1, 128], F32))
        srow = e(nc.sbuf_tensor(f"{btag}_srow", [1, TC], F32))
        qrow = e(nc.sbuf_tensor(f"{btag}_qrow", [1, TC], F32))
        mrow = e(nc.sbuf_tensor(f"{btag}_mrow", [1, TC], F32))
        vrow = e(nc.sbuf_tensor(f"{btag}_vrow", [1, TC], F32))
        arow = e(nc.sbuf_tensor(f"{btag}_arow", [1, TC], F32))
        brow = e(nc.sbuf_tensor(f"{btag}_brow", [1, TC], F32))
        tmp = [e(nc.sbuf_tensor(f"{btag}_tmp{i}", [128, TC], F32)) for i in range(2)]
        eps_sb = e(nc.sbuf_tensor(f"{btag}_eps", [1, 1], F32))
        st_ps = e(nc.psum_tensor(f"{btag}_stp", [1, TC], F32))
        sq_ps = e(nc.psum_tensor(f"{btag}_sqp", [1, TC], F32))
        abc_ps = e(nc.psum_tensor(f"{btag}_abc", [128, TC], F32))
        bbc_ps = e(nc.psum_tensor(f"{btag}_bbc", [128, TC], F32))
        blk = e(nc.Block(f"blk_{btag}"))

        @blk.sync
        def _(s):
            for r in range(2):
                for dc in range(NDC):
                    sl = dc % 2
                    if r == 0:
                        if src_ready is not None:
                            for nm, val in src_ready(dc):
                                s.wait_ge(p.sem(nm), val)
                        if dc >= 2:
                            s.wait_ge(p.sem(f"{tag}_ft"), ft0 + dc - 1)
                            s.wait_ge(p.sem(f"{tag}_sq"), sq0 + dc - 1)
                    else:
                        if dc < 2:
                            s.wait_ge(p.sem(f"{tag}_ft"), ft0 + NDC)
                            s.wait_ge(p.sem(f"{tag}_sq"), sq0 + NDC)
                        else:
                            s.wait_ge(p.sem(f"{tag}_ap"), ap0 + dc - 1)
                    p.inc(s.dma_start(out=xc[sl][:, :], in_=src_dram[dc]),
                          f"{tag}_ld{sl}", 16)

        @blk.vector
        def _(v):
            p.inc(v.memset(ones_col[:, :], 1.0), f"{tag}_row")
            p.inc(v.memset(ones_row[:, :], 1.0), f"{tag}_row")
            p.inc(v.memset(eps_sb[:, :], EPS), f"{tag}_row")

        @blk.scalar
        def _(sc):
            for dc in range(NDC):
                sl = dc % 2
                sc.wait_ge(p.sem(f"{tag}_ld{sl}"), ld0[sl] + 16 * (dc // 2 + 1))
                if dc >= 2:
                    sc.wait_ge(p.sem(f"{tag}_ft"), ft0 + dc - 1)
                p.inc(sc.activation(xsq[sl][:, :], xc[sl][:, :], AF.Square),
                      f"{tag}_sq")

        @blk.tensor
        def _(t):
            t.wait_ge(p.sem(f"{tag}_row"), rw0 + 3)
            for dc in range(NDC):
                sl = dc % 2
                t.wait_ge(p.sem(f"{tag}_ld{sl}"), ld0[sl] + 16 * (dc // 2 + 1))
                t.wait_ge(p.sem(f"{tag}_sq"), sq0 + dc + 1)
                for th in range(NT):
                    tsl = slice(th * 512, (th + 1) * 512)
                    t.matmul(st_ps[:, tsl], ones_col[:, :], xc[sl][:, tsl],
                             start=(dc == 0), stop=(dc == NDC - 1))
                    inst = t.matmul(sq_ps[:, tsl], ones_col[:, :], xsq[sl][:, tsl],
                                    start=(dc == 0), stop=(dc == NDC - 1))
                p.inc(inst, f"{tag}_ft")

        @blk.vector
        def _(v):
            v.wait_ge(p.sem(f"{tag}_ft"), ft0 + NDC)
            p.inc(v.tensor_scalar(mrow[:, :], st_ps[:, :], 1.0 / D, None, ALU.mult),
                  f"{tag}_row")
            v.wait_ge(p.sem(f"{tag}_row"), rw0 + 4)
            p.inc(v.tensor_scalar(vrow[:, :], sq_ps[:, :], 1.0 / D, None, ALU.mult),
                  f"{tag}_row")
            v.wait_ge(p.sem(f"{tag}_row"), rw0 + 5)
            p.inc(v.tensor_tensor(qrow[:, :], mrow[:, :], mrow[:, :], ALU.mult),
                  f"{tag}_row")
            v.wait_ge(p.sem(f"{tag}_row"), rw0 + 6)
            p.inc(v.tensor_tensor(vrow[:, :], vrow[:, :], qrow[:, :], ALU.subtract),
                  f"{tag}_row")

        @blk.scalar
        def _(sc):
            sc.wait_ge(p.sem(f"{tag}_row"), rw0 + 7)
            p.inc(sc.activation(srow[:, :], vrow[:, :], AF.Ln, bias=eps_sb[:, :]),
                  f"{tag}_row")
            sc.wait_ge(p.sem(f"{tag}_row"), rw0 + 8)
            p.inc(sc.activation(arow[:, :], srow[:, :], AF.Exp, scale=-0.5),
                  f"{tag}_row")

        @blk.vector
        def _(v):
            v.wait_ge(p.sem(f"{tag}_row"), rw0 + 9)
            p.inc(v.tensor_tensor(brow[:, :], mrow[:, :], arow[:, :], ALU.mult),
                  f"{tag}_row")
            v.wait_ge(p.sem(f"{tag}_row"), rw0 + 10)
            p.inc(v.tensor_scalar(brow[:, :], brow[:, :], -1.0, None, ALU.mult),
                  f"{tag}_row")

        @blk.tensor
        def _(t):
            t.wait_ge(p.sem(f"{tag}_row"), rw0 + 11)
            for th in range(NT):
                tsl = slice(th * 512, (th + 1) * 512)
                t.matmul(abc_ps[:, tsl], ones_row[:, :], arow[:, tsl],
                         start=True, stop=True)
                inst = t.matmul(bbc_ps[:, tsl], ones_row[:, :], brow[:, tsl],
                                start=True, stop=True)
            p.inc(inst, f"{tag}_bc")

        @blk.vector
        def _(v):
            v.wait_ge(p.sem(f"{tag}_bc"), bc0 + 1)
            for dc in range(NDC):
                sl = dc % 2
                v.wait_ge(p.sem(f"{tag}_ld{sl}"), ld0[sl] + 16 * (NDC // 2 + dc // 2 + 1))
                if dc >= 2:
                    v.wait_ge(p.sem(f"{tag}_ap"), ap0 + dc - 1)
                p.inc(v.tensor_tensor(tmp[sl][:, :], xc[sl][:, :], abc_ps[:, :],
                                      ALU.mult), f"{tag}_ss")
                v.wait_ge(p.sem(f"{tag}_ss"), ss0 + dc + 1)
                p.inc(v.tensor_tensor(dst_sb[:, dc * TC:(dc + 1) * TC],
                                      tmp[sl][:, :], bbc_ps[:, :], ALU.add),
                      f"{tag}_ap")



def phase_ffn_a(p, xn_sb, h_dram, wug_g, tag, h_war=None, btag=None):
    btag = btag or tag
    """wug_g: fc -> dram AP [128, 2D] bf16 (gathered). h_war: [(sem,val)] to
    wait once before the first h write (prior readers of h_scr)."""
    nc = p.nc
    w0 = [p.base(f"{tag}_w{i}") for i in range(2)]
    mm0 = p.base(f"{tag}_mm")
    si0 = p.base(f"{tag}_si")
    vt0 = p.base(f"{tag}_vt")
    hs0 = [p.base(f"{tag}_h{i}") for i in range(2)]

    with ExitStack() as ctx:
        e = ctx.enter_context
        wsl = [e(nc.sbuf_tensor(f"{btag}_wb{i}", [128, 2 * D], BF16)) for i in range(2)]
        sg = [e(nc.sbuf_tensor(f"{btag}_sg{i}", [128, TC], F32)) for i in range(2)]
        hsb = [e(nc.sbuf_tensor(f"{btag}_hb{i}", [128, TC], BF16)) for i in range(2)]
        up_ps = [e(nc.psum_tensor(f"{btag}_up{i}", [128, TC], F32)) for i in range(2)]
        gt_ps = [e(nc.psum_tensor(f"{btag}_gt{i}", [128, TC], F32)) for i in range(2)]
        blk = e(nc.Block(f"blk_{btag}"))

        @blk.gpsimd
        def _(g):
            for fc in range(NFC):
                sl = fc % 2
                g.wait_ge(p.sem("cc"), 1 + fc // 8)
                if fc >= 2:
                    g.wait_ge(p.sem(f"{tag}_mm"), mm0 + 4 * (fc - 1))
                p.inc(g.dma_start(out=wsl[sl][:, :], in_=wug_g(fc)), f"{tag}_w{sl}", 16)

        @blk.tensor
        def _(t):
            for fc in range(NFC):
                sl = fc % 2
                t.wait_ge(p.sem(f"{tag}_w{sl}"), w0[sl] + 16 * (fc // 2 + 1))
                if fc >= 2:
                    t.wait_ge(p.sem(f"{tag}_vt"), vt0 + 2 * (fc - 1))
                    t.wait_ge(p.sem(f"{tag}_si"), si0 + 2 * (fc - 1))
                for th in range(NT):
                    tsl = slice(th * 512, (th + 1) * 512)
                    for dc in range(NDC):
                        inst = t.matmul(up_ps[sl][:, tsl],
                                        wsl[sl][:, dc * 128:(dc + 1) * 128],
                                        xn_sb[:, dc * TC + th * 512: dc * TC + (th + 1) * 512],
                                        start=(dc == 0), stop=(dc == NDC - 1))
                    p.inc(inst, f"{tag}_mm")
                    for dc in range(NDC):
                        inst = t.matmul(gt_ps[sl][:, tsl],
                                        wsl[sl][:, D + dc * 128: D + (dc + 1) * 128],
                                        xn_sb[:, dc * TC + th * 512: dc * TC + (th + 1) * 512],
                                        start=(dc == 0), stop=(dc == NDC - 1))
                    p.inc(inst, f"{tag}_mm")

        @blk.scalar
        def _(sc):
            for fc in range(NFC):
                sl = fc % 2
                for th in range(NT):
                    tsl = slice(th * 512, (th + 1) * 512)
                    sc.wait_ge(p.sem(f"{tag}_mm"), mm0 + 4 * fc + 2 * (th + 1))
                    if fc >= 2:
                        sc.wait_ge(p.sem(f"{tag}_vt"), vt0 + 2 * (fc - 2) + th + 1)
                    p.inc(sc.activation(sg[sl][:, tsl], gt_ps[sl][:, tsl], AF.Silu),
                          f"{tag}_si")

        @blk.vector
        def _(v):
            for fc in range(NFC):
                sl = fc % 2
                for th in range(NT):
                    tsl = slice(th * 512, (th + 1) * 512)
                    v.wait_ge(p.sem(f"{tag}_si"), si0 + 2 * fc + th + 1)
                    v.wait_ge(p.sem(f"{tag}_mm"), mm0 + 4 * fc + 2 * th + 1)
                    if fc >= 2 and th == 0:
                        v.wait_ge(p.sem(f"{tag}_h{sl}"), hs0[sl] + 16 * (fc // 2))
                    p.inc(v.tensor_tensor(hsb[sl][:, tsl], sg[sl][:, tsl],
                                          up_ps[sl][:, tsl], ALU.mult), f"{tag}_vt")

        @blk.sync
        def _(s):
            if h_war:
                for nm, val in h_war:
                    s.wait_ge(p.sem(nm), val)
            for fc in range(NFC):
                sl = fc % 2
                s.wait_ge(p.sem(f"{tag}_vt"), vt0 + 2 * (fc + 1))
                p.inc(s.dma_start(out=h_dram[fc], in_=hsb[sl][:, :]), f"{tag}_h{sl}", 16)
            s.wait_ge(p.sem(f"{tag}_h0"), p.cnt[f"{tag}_h0"])
            s.wait_ge(p.sem(f"{tag}_h1"), p.cnt[f"{tag}_h1"])

    return hs0



def phase_ffn_b(p, h_pairs, res_src, res_dst, wdn_g, tag, yscale=0.5,
                h_ready=None, res_ready=None, btag=None):
    btag = btag or tag
    """h_pairs: fp -> AP [128,2,TC] bf16; res_*: dc -> AP [128, TC] f32.
    wdn_g: (gi, fp) -> AP [128, TC] bf16. Out DMAs ride the scalar queue."""
    nc = p.nc
    w0 = [p.base(f"{tag}_w{i}") for i in range(2)]
    h0 = [p.base(f"{tag}_h{i}") for i in range(2)]
    mm0 = p.base(f"{tag}_mm")
    ev0 = p.base(f"{tag}_ev")
    rc0 = [p.base(f"{tag}_rc{i}") for i in range(2)]
    xo0 = [p.base(f"{tag}_xo{i}") for i in range(2)]
    ss0 = p.base(f"{tag}_ss")

    NP = 43
    with ExitStack() as ctx:
        e = ctx.enter_context
        wd = [e(nc.sbuf_tensor(f"{btag}_wb{i}", [128, TC], BF16)) for i in range(2)]
        hh = [e(nc.sbuf_tensor(f"{btag}_hb{i}", [128, 2 * TC], BF16)) for i in range(2)]
        rc = [e(nc.sbuf_tensor(f"{btag}_rc{i}", [128, TC], F32)) for i in range(8)]
        xo = [e(nc.sbuf_tensor(f"{btag}_xo{i}", [128, TC], F32)) for i in range(2)]
        tmp = [e(nc.sbuf_tensor(f"{btag}_tm{i}", [128, TC], F32)) for i in range(2)]
        y_ps = [e(nc.psum_tensor(f"{btag}_y{j}", [128, TC], F32)) for j in range(4)]
        blk = e(nc.Block(f"blk_{btag}"))

        @blk.gpsimd
        def _(g):
            g.wait_ge(p.sem("cc"), 12)
            for gi in range(8):
                for fp in range(NP):
                    i = gi * NP + fp
                    sl = i % 2
                    if i >= 2:
                        g.wait_ge(p.sem(f"{tag}_mm"), mm0 + i - 1)
                    p.inc(g.dma_start(out=wd[sl][:, :], in_=wdn_g(gi, fp)),
                          f"{tag}_w{sl}", 16)

        @blk.sync
        def _(s):
            for gi in range(8):
                for j in range(4):
                    if gi >= 2:
                        s.wait_ge(p.sem(f"{tag}_ev"), ev0 + 4 * (gi - 1))
                    if res_ready is not None:
                        for nm, val in res_ready(gi * 4 + j):
                            s.wait_ge(p.sem(nm), val)
                    p.inc(s.dma_start(out=rc[(gi % 2) * 4 + j][:, :],
                                      in_=res_src[gi * 4 + j]), f"{tag}_rc{gi % 2}", 16)
                for fp in range(NP):
                    i = gi * NP + fp
                    sl = i % 2
                    if i >= 2:
                        s.wait_ge(p.sem(f"{tag}_mm"), mm0 + i - 1)
                    if gi == 0 and h_ready is not None:
                        for nm, val in h_ready(fp):
                            s.wait_ge(p.sem(nm), val)
                    p.inc(s.dma_start(out=hh[sl][:, :], in_=h_pairs[fp]),
                          f"{tag}_h{sl}", 16)

        @blk.tensor
        def _(t):
            for gi in range(8):
                if gi >= 1:
                    t.wait_ge(p.sem(f"{tag}_ev"), ev0 + gi * 4)
                for fp in range(NP):
                    i = gi * NP + fp
                    sl = i % 2
                    t.wait_ge(p.sem(f"{tag}_w{sl}"), w0[sl] + 16 * (i // 2 + 1))
                    t.wait_ge(p.sem(f"{tag}_h{sl}"), h0[sl] + 16 * (i // 2 + 1))
                    for fc2 in range(2):
                        for j in range(4):
                            for th in range(NT):
                                inst = t.matmul(
                                    y_ps[j][:, th * 512:(th + 1) * 512],
                                    wd[sl][:, (fc2 * 4 + j) * 128:(fc2 * 4 + j + 1) * 128],
                                    hh[sl][:, fc2 * TC + th * 512: fc2 * TC + (th + 1) * 512],
                                    start=(fp == 0 and fc2 == 0),
                                    stop=(fp == NP - 1 and fc2 == 1))
                    p.inc(inst, f"{tag}_mm")

        @blk.vector
        def _(v):
            nss = 0
            for gi in range(8):
                v.wait_ge(p.sem(f"{tag}_mm"), mm0 + (gi + 1) * NP)
                v.wait_ge(p.sem(f"{tag}_rc{gi % 2}"), rc0[gi % 2] + 16 * 4 * (gi // 2 + 1))
                for j in range(4):
                    idx = gi * 4 + j
                    if idx >= 2:
                        v.wait_ge(p.sem(f"{tag}_xo{idx % 2}"),
                                  xo0[idx % 2] + 16 * (idx // 2))
                        v.wait_ge(p.sem(f"{tag}_ev"), ev0 + idx - 1)
                    p.inc(v.tensor_scalar(tmp[idx % 2][:, :], y_ps[j][:, :], yscale,
                                          None, ALU.mult), f"{tag}_ss")
                    nss += 1
                    v.wait_ge(p.sem(f"{tag}_ss"), ss0 + nss)
                    p.inc(v.tensor_tensor(xo[idx % 2][:, :], tmp[idx % 2][:, :],
                                          rc[(gi % 2) * 4 + j][:, :], ALU.add),
                          f"{tag}_ev")

        @blk.scalar
        def _(sc):
            for gi in range(8):
                for j in range(4):
                    idx = gi * 4 + j
                    sc.wait_ge(p.sem(f"{tag}_ev"), ev0 + idx + 1)
                    p.inc(sc.dma_start(out=res_dst[idx], in_=xo[idx % 2][:, :]),
                          f"{tag}_xo{idx % 2}", 16)
            sc.wait_ge(p.sem(f"{tag}_xo0"), p.cnt[f"{tag}_xo0"])
            sc.wait_ge(p.sem(f"{tag}_xo1"), p.cnt[f"{tag}_xo1"])

    return xo0



def phase_kvproj(p, xn_sb, wkv_g, k_local, vt_local, k_gath, vt_gath,
                 eye_sb, tag="kp"):
    btag = tag
    nc = p.nc
    w0 = [p.base(f"{tag}_w{i}") for i in range(2)]
    mm0 = p.base(f"{tag}_mm")
    ev0 = p.base(f"{tag}_ev")
    av0 = p.base(f"{tag}_av")
    tr0 = p.base(f"{tag}_tr")
    vt0 = p.base(f"{tag}_vt")
    kd0 = [p.base(f"{tag}_kd{i}") for i in range(2)]
    vd0 = [p.base(f"{tag}_vd{i}") for i in range(2)]

    with ExitStack() as ctx:
        e = ctx.enter_context
        wsl = [e(nc.sbuf_tensor(f"{btag}_wb{i}", [128, 2 * D], BF16)) for i in range(2)]
        kst = [e(nc.sbuf_tensor(f"{btag}_k{i}", [128, TC], BF16)) for i in range(2)]
        vst = [e(nc.sbuf_tensor(f"{btag}_v{i}", [128, TC], BF16)) for i in range(2)]
        vtt = [e(nc.sbuf_tensor(f"{btag}_vt{i}", [128, TC], BF16)) for i in range(2)]
        k_ps = e(nc.psum_tensor(f"{btag}_kp", [128, TC], F32))
        v_ps = e(nc.psum_tensor(f"{btag}_vp", [128, TC], F32))
        t_ps = [e(nc.psum_tensor(f"{btag}_tp{i}", [128, 1024], BF16)) for i in range(2)]
        blk = e(nc.Block(f"blk_{btag}"))

        @blk.gpsimd
        def _(g):
            g.wait_ge(p.sem("cc"), 13)
            for h in range(H):
                sl = h % 2
                if h >= 2:
                    g.wait_ge(p.sem(f"{tag}_mm"), mm0 + 2 * (h - 1))
                p.inc(g.dma_start(out=wsl[sl][:, :], in_=wkv_g(h)), f"{tag}_w{sl}", 16)
            g.wait_ge(p.sem(f"{tag}_kd0"), kd0[0] + 16 * (H // 2))
            g.wait_ge(p.sem(f"{tag}_kd1"), kd0[1] + 16 * (H // 2))
            g.wait_ge(p.sem(f"{tag}_vd0"), vd0[0] + 16 * (H // 2))
            g.wait_ge(p.sem(f"{tag}_vd1"), vd0[1] + 16 * (H // 2))
            p.inc(g.collective_compute(
                "AllGather", ALU.bypass,
                replica_groups=[[0, 1, 2, 3], [4, 5, 6, 7]],
                ins=[k_local.ap().rearrange("h k t -> (h k) t")],
                outs=[k_gath.ap()]), "cc", 1)
            p.inc(g.collective_compute(
                "AllGather", ALU.bypass,
                replica_groups=[[0, 1, 2, 3], [4, 5, 6, 7]],
                ins=[vt_local.ap().rearrange("h t d -> (h t) d")],
                outs=[vt_gath.ap()]), "cc", 1)

        @blk.tensor
        def _(t):
            for h in range(H):
                sl = h % 2
                t.wait_ge(p.sem(f"{tag}_w{sl}"), w0[sl] + 16 * (h // 2 + 1))
                if h >= 1:
                    t.wait_ge(p.sem(f"{tag}_ev"), ev0 + h)
                    t.wait_ge(p.sem(f"{tag}_av"), av0 + h)
                for th in range(NT):
                    tsl = slice(th * 512, (th + 1) * 512)
                    for dc in range(NDC):
                        inst = t.matmul(k_ps[:, tsl],
                                        wsl[sl][:, dc * 128:(dc + 1) * 128],
                                        xn_sb[:, dc * TC + th * 512: dc * TC + (th + 1) * 512],
                                        start=(dc == 0), stop=(dc == NDC - 1))
                p.inc(inst, f"{tag}_mm")
                for th in range(NT):
                    tsl = slice(th * 512, (th + 1) * 512)
                    for dc in range(NDC):
                        inst = t.matmul(v_ps[:, tsl],
                                        wsl[sl][:, D + dc * 128: D + (dc + 1) * 128],
                                        xn_sb[:, dc * TC + th * 512: dc * TC + (th + 1) * 512],
                                        start=(dc == 0), stop=(dc == NDC - 1))
                p.inc(inst, f"{tag}_mm")
                t.wait_ge(p.sem(f"{tag}_av"), av0 + h + 1)
                for c in range(8):
                    gidx = h * 8 + c
                    if gidx >= 2:
                        t.wait_ge(p.sem(f"{tag}_vt"), vt0 + gidx - 1)
                    p.inc(t.transpose(t_ps[c % 2][:, 0:128],
                                      vst[sl][:, c * 128:(c + 1) * 128],
                                      eye_sb[:, :]), f"{tag}_tr")

        @blk.vector
        def _(v):
            for h in range(H):
                sl = h % 2
                v.wait_ge(p.sem(f"{tag}_mm"), mm0 + 2 * h + 1)
                if h >= 2:
                    v.wait_ge(p.sem(f"{tag}_kd{sl}"), kd0[sl] + 16 * (h // 2))
                p.inc(v.tensor_copy(kst[sl][:, :], k_ps[:, :]), f"{tag}_ev")
                for c in range(8):
                    gidx = h * 8 + c
                    v.wait_ge(p.sem(f"{tag}_tr"), tr0 + gidx + 1)
                    if h >= 2 and c == 0:
                        v.wait_ge(p.sem(f"{tag}_vd{sl}"), vd0[sl] + 16 * (h // 2))
                    p.inc(v.tensor_copy(vtt[sl][:, c * 128:(c + 1) * 128],
                                        t_ps[c % 2][:, 0:128]), f"{tag}_vt")

        @blk.scalar
        def _(sc):
            for h in range(H):
                sl = h % 2
                sc.wait_ge(p.sem(f"{tag}_mm"), mm0 + 2 * h + 2)
                if h >= 2:
                    sc.wait_ge(p.sem(f"{tag}_tr"), tr0 + 8 * (h - 1))
                p.inc(sc.activation(vst[sl][:, :], v_ps[:, :], AF.Copy), f"{tag}_av")

        @blk.sync
        def _(s):
            for h in range(H):
                sl = h % 2
                s.wait_ge(p.sem(f"{tag}_ev"), ev0 + h + 1)
                p.inc(s.dma_start(out=k_local[h, :, :], in_=kst[sl][:, :]),
                      f"{tag}_kd{sl}", 16)
                s.wait_ge(p.sem(f"{tag}_vt"), vt0 + 8 * (h + 1))
                vt_out = vt_local.ap().rearrange("h (c q) d -> h q c d", q=128)
                p.inc(s.dma_start(
                    out=vt_out[h, :, :, :],
                    in_=vtt[sl].ap().rearrange("q (c d) -> q c d", d=128)),
                    f"{tag}_vd{sl}", 16)



def phase_qproj(p, xn_sb, wq_g, q_sb, tag="qp"):
    btag = tag
    nc = p.nc
    w0 = [p.base(f"{tag}_w{i}") for i in range(2)]
    mm0 = p.base(f"{tag}_mm")
    ev0 = p.base(f"{tag}_ev")

    with ExitStack() as ctx:
        e = ctx.enter_context
        wsl = [e(nc.sbuf_tensor(f"{btag}_wb{i}", [128, D], BF16)) for i in range(2)]
        q_ps = [e(nc.psum_tensor(f"{btag}_q{i}", [128, TC], F32)) for i in range(2)]
        blk = e(nc.Block(f"blk_{btag}"))

        @blk.gpsimd
        def _(g):
            g.wait_ge(p.sem("cc"), 14)
            for h in range(H):
                sl = h % 2
                if h >= 2:
                    g.wait_ge(p.sem(f"{tag}_mm"), mm0 + h - 1)
                p.inc(g.dma_start(out=wsl[sl][:, :], in_=wq_g(h)), f"{tag}_w{sl}", 16)

        @blk.tensor
        def _(t):
            for h in range(H):
                sl = h % 2
                t.wait_ge(p.sem(f"{tag}_w{sl}"), w0[sl] + 16 * (h // 2 + 1))
                if h >= 2:
                    t.wait_ge(p.sem(f"{tag}_ev"), ev0 + h - 1)
                for th in range(NT):
                    tsl = slice(th * 512, (th + 1) * 512)
                    for dc in range(NDC):
                        inst = t.matmul(q_ps[sl][:, tsl],
                                        wsl[sl][:, dc * 128:(dc + 1) * 128],
                                        xn_sb[:, dc * TC + th * 512: dc * TC + (th + 1) * 512],
                                        start=(dc == 0), stop=(dc == NDC - 1))
                p.inc(inst, f"{tag}_mm")

        @blk.vector
        def _(v):
            for h in range(H):
                sl = h % 2
                v.wait_ge(p.sem(f"{tag}_mm"), mm0 + h + 1)
                p.inc(v.tensor_copy(q_sb[:, h * TC:(h + 1) * TC], q_ps[sl][:, :]),
                      f"{tag}_ev")



def phase_flash(p, q_sb, attn_sb, kg_of, kl_dram, vg_of, vl_of,
                gates_sb, gatesb_sb, masks_sb, tag="fl"):
    nc = p.nc
    NP_PT = 8

    def chunk_quads(kc, kind):
        if kind == "r":
            return [0, 1]
        return [0, 1] if kc < 4 else [1]

    chunk_list = [("r", kc) for kc in range(NREM)] + [("l", kc) for kc in range(8)]

    sched = []
    g_slot = 0
    slot_seq = []
    cnt = dict(mm=0, cs=0, ex=0, msk=0, pv=0, fin=0, row=0, eb=0)
    for h in range(H):
        m = dict(norm=[], sc={}, fin=[])
        for j in range(10):
            m["norm"].append(dict(g=g_slot, mm=cnt["mm"] + 1, cs=cnt["cs"] + 1))
            slot_seq.append(("cs", cnt["cs"] + 1))
            cnt["mm"] += 1
            cnt["cs"] += 1
            g_slot += 1
        cnt["row"] += 5
        cnt["eb"] += 2
        m["row"] = cnt["row"]
        m["eb"] = cnt["eb"]
        for kind, kc in chunk_list:
            for qd in chunk_quads(kc, kind):
                key = (kind, kc, qd)
                cnt["mm"] += 1
                slot_seq.append(("ex", cnt["ex"] + 1))
                masked = (kind == "l" and 0 <= kc - 4 * qd <= 3)
                if masked:
                    cnt["msk"] += 1
                cnt["pv"] += 1
                m["sc"][key] = dict(g=g_slot, mm=cnt["mm"], ex=cnt["ex"] + 1,
                                    msk=(cnt["msk"] if masked else None),
                                    pt=cnt["pv"] - 1, pv=cnt["pv"])
                cnt["ex"] += 1
                g_slot += 1
        for qd in range(2):
            cnt["fin"] += 4
            m["fin"].append(cnt["fin"])
        sched.append(m)

    kh0 = [p.base(f"{tag}_kh{i}") for i in range(2)]
    kl0 = [p.base(f"{tag}_kl{i}") for i in range(2)]
    vr0 = [p.base(f"{tag}_vr{i}") for i in range(2)]
    vl0 = [p.base(f"{tag}_vl{i}") for i in range(2)]
    vg0 = p.base(f"{tag}_vg")
    mm0 = p.base(f"{tag}_mm")
    cs0 = p.base(f"{tag}_cs")
    ex0 = p.base(f"{tag}_ex")
    msk0 = p.base(f"{tag}_msk")
    pv0 = p.base(f"{tag}_pv")
    fin0 = p.base(f"{tag}_fin")
    row0 = p.base(f"{tag}_row")
    eb0 = p.base(f"{tag}_eb")
    sq0 = p.base(f"{tag}_sq")
    on0 = p.base(f"{tag}_on")

    def slot_wait(eng, g):
        if g >= 3:
            nm, val = slot_seq[g - 3]
            base = cs0 if nm == "cs" else ex0
            eng.wait_ge(p.sem(f"{tag}_{nm}"), base + val)

    with ExitStack() as ctx:
        e = ctx.enter_context
        kh = [e(nc.sbuf_tensor(f"{tag}_kh{i}", [128, 3 * TC], BF16)) for i in range(2)]
        kl = [e(nc.sbuf_tensor(f"{tag}_kl{i}", [128, TC], BF16)) for i in range(2)]
        vr = [e(nc.sbuf_tensor(f"{tag}_vr{i}", [128, NREM * 128], BF16))
              for i in range(2)]
        vl = [e(nc.sbuf_tensor(f"{tag}_vl{i}", [128, 8 * 128], BF16))
              for i in range(2)]
        psb = [e(nc.sbuf_tensor(f"{tag}_p{i}", [128, 512], BF16))
               for i in range(NP_PT)]
        ksq = [e(nc.sbuf_tensor(f"{tag}_ksq{i}", [128, 512], BF16)) for i in range(2)]
        ones_col = e(nc.sbuf_tensor(f"{tag}_ones", [128, 1], BF16))
        ones_row = e(nc.sbuf_tensor(f"{tag}_onesr", [1, 128], F32))
        qmx = e(nc.sbuf_tensor(f"{tag}_qmx", [1, 2], F32))
        kmx = e(nc.sbuf_tensor(f"{tag}_kmx", [1, 8], F32))
        qq = e(nc.sbuf_tensor(f"{tag}_qq", [1, 1], F32))
        kk = e(nc.sbuf_tensor(f"{tag}_kk", [1, 1], F32))
        bb = e(nc.sbuf_tensor(f"{tag}_bb", [1, 1], F32))
        nb_sb = e(nc.sbuf_tensor(f"{tag}_nb", [1, 1], F32))
        ebs = [e(nc.sbuf_tensor(f"{tag}_eb{i}", [128, 1], F32)) for i in range(2)]
        liv_row = e(nc.sbuf_tensor(f"{tag}_lr", [1, 512], F32))
        liv_sb = e(nc.sbuf_tensor(f"{tag}_ls", [128, 512], F32))
        s_ps = [e(nc.psum_tensor(f"{tag}_s{i}", [128, 512], F32)) for i in range(3)]
        o_ps = [e(nc.psum_tensor(f"{tag}_o{i}", [128, 512], F32)) for i in range(2)]
        l_ps = [e(nc.psum_tensor(f"{tag}_l{i}", [1, 512], F32)) for i in range(2)]
        mi_ps = e(nc.psum_tensor(f"{tag}_mi", [128, 512], F32))
        blk = e(nc.Block(f"blk_{tag}"))

        def vt_chunk(h, kind, kc):
            buf = vr[h % 2] if kind == "r" else vl[h % 2]
            return buf[:, kc * 128:(kc + 1) * 128]

        def k_chunk(h, kind, kc):
            buf = kh[h % 2] if kind == "r" else kl[h % 2]
            return buf[:, kc * 128:(kc + 1) * 128]

        def pv_done_mark(h):
            return max(v["pv"] for v in sched[h]["sc"].values())

        @blk.sync
        def _(s):
            s.wait_ge(p.sem("cc"), 17)
            for h in range(H):
                sl = h % 2
                if h >= 2:
                    s.wait_ge(p.sem(f"{tag}_pv"), pv0 + pv_done_mark(h - 2))
                p.inc(s.dma_start(out=kh[sl].ap().rearrange("p (r t) -> p r t", r=3),
                                  in_=kg_of(h)), f"{tag}_kh{sl}", 16)
                p.inc(s.dma_start(out=kl[sl][:, :], in_=kl_dram[h, :, :]),
                      f"{tag}_kl{sl}", 16)
                vrv = vr[sl].ap().rearrange("p (c x) -> p c x", x=128)
                for r in range(3):
                    p.inc(s.dma_start(out=vrv[:, r * 8:(r + 1) * 8, :],
                                      in_=vg_of(h, r)), f"{tag}_vr{sl}", 16)
                vlv = vl[sl].ap().rearrange("p (c x) -> p c x", x=128)
                p.inc(s.dma_start(out=vlv[:, :, :], in_=vl_of(h)), f"{tag}_vl{sl}", 16)

        @blk.scalar
        def _(sc):
            for h in range(H):
                sl = h % 2
                m = sched[h]
                srcs = [q_sb[:, h * TC:h * TC + 512], q_sb[:, h * TC + 512:(h + 1) * TC]]
                srcs += [kh[sl][:, j * 512:(j + 1) * 512] for j in range(6)]
                srcs += [kl[sl][:, 0:512], kl[sl][:, 512:1024]]
                sc.wait_ge(p.sem(f"{tag}_kh{sl}"), kh0[sl] + 16 * (h // 2 + 1))
                sc.wait_ge(p.sem(f"{tag}_kl{sl}"), kl0[sl] + 16 * (h // 2 + 1))
                for j, srcap in enumerate(srcs):
                    gi = h * 10 + j
                    if gi >= 2:
                        sc.wait_ge(p.sem(f"{tag}_mm"),
                                   mm0 + _nmm_of_normslot(sched, gi - 2))
                    p.inc(sc.activation(ksq[gi % 2][:, :], srcap, AF.Square),
                          f"{tag}_sq")
                sc.wait_ge(p.sem(f"{tag}_row"), row0 + m["row"] - 2)
                p.inc(sc.activation(nb_sb[:, :], bb[:, :], AF.Sqrt), f"{tag}_row")
                for kind, kc in chunk_list:
                    for qd in chunk_quads(kc, kind):
                        d = m["sc"][(kind, kc, qd)]
                        sc.wait_ge(p.sem(f"{tag}_mm"), mm0 + d["mm"])
                        sc.wait_ge(p.sem(f"{tag}_eb"), eb0 + m["eb"])
                        pb = d["pv"] - NP_PT
                        if pb >= 1:
                            sc.wait_ge(p.sem(f"{tag}_pv"), pv0 + pb)
                        p.inc(sc.activation(psb[d["pt"] % NP_PT][:, :],
                                            s_ps[d["g"] % 3][:, :],
                                            AF.Exp, bias=ebs[sl][:, :], scale=SCALE),
                              f"{tag}_ex")

        @blk.vector
        def _(v):
            p.inc(v.memset(ones_col[:, :], 1.0), f"{tag}_on")
            p.inc(v.memset(ones_row[:, :], 1.0), f"{tag}_on")
            for h in range(H):
                sl = h % 2
                m = sched[h]
                if h < 2:
                    v.wait_ge(p.sem(f"{tag}_on"), on0 + 2)
                v.wait_ge(p.sem(f"{tag}_vr{sl}"), vr0[sl] + 48 * (h // 2 + 1))
                for kc in range(NREM):
                    inst = v.tensor_scalar(vr[sl][:, kc * 128:(kc + 1) * 128],
                                           vr[sl][:, kc * 128:(kc + 1) * 128],
                                           gates_sb[:, kc:kc + 1], None, ALU.mult)
                p.inc(inst, f"{tag}_vg")
                for j in range(10):
                    d = m["norm"][j]
                    v.wait_ge(p.sem(f"{tag}_mm"), mm0 + d["mm"])
                    dst = qmx[:, j:j + 1] if j < 2 else kmx[:, (j - 2):(j - 1)]
                    p.inc(v.reduce_max(dst, s_ps[d["g"] % 3][0:1, 0:512], axis=AX.X),
                          f"{tag}_cs")
                v.wait_ge(p.sem(f"{tag}_cs"), cs0 + m["norm"][-1]["cs"])
                if h >= 1:
                    v.wait_ge(p.sem(f"{tag}_row"), row0 + sched[h - 1]["row"])
                p.inc(v.reduce_max(qq[:, :], qmx[:, :], axis=AX.X), f"{tag}_row")
                v.wait_ge(p.sem(f"{tag}_row"), row0 + m["row"] - 4)
                p.inc(v.reduce_max(kk[:, :], kmx[:, :], axis=AX.X), f"{tag}_row")
                v.wait_ge(p.sem(f"{tag}_row"), row0 + m["row"] - 3)
                p.inc(v.tensor_tensor(bb[:, :], qq[:, :], kk[:, :], ALU.mult),
                      f"{tag}_row")
                v.wait_ge(p.sem(f"{tag}_row"), row0 + m["row"] - 1)
                p.inc(v.tensor_scalar(nb_sb[:, :], nb_sb[:, :], -SCALE, None,
                                      ALU.mult), f"{tag}_row")
                v.wait_ge(p.sem(f"{tag}_eb"), eb0 + m["eb"] - 1)
                if h >= 2:
                    last_ex = max(d["ex"] for d in sched[h - 2]["sc"].values())
                    v.wait_ge(p.sem(f"{tag}_ex"), ex0 + last_ex)
                p.inc(v.tensor_copy(ebs[sl][:, :], mi_ps[:, 0:1]), f"{tag}_eb")
                for kind, kc in chunk_list:
                    for qd in chunk_quads(kc, kind):
                        d = m["sc"][(kind, kc, qd)]
                        if d["msk"] is None:
                            continue
                        v.wait_ge(p.sem(f"{tag}_ex"), ex0 + d["ex"])
                        p.inc(v.tensor_tensor(psb[d["pt"] % NP_PT][:, :],
                                              psb[d["pt"] % NP_PT][:, :],
                                              masks_sb[kc - 4 * qd][:, :], ALU.mult),
                              f"{tag}_msk")
                for qd in range(2):
                    fm = m["fin"][qd]
                    key = ("l", 4 * qd + 3, qd)
                    v.wait_ge(p.sem(f"{tag}_pv"), pv0 + m["sc"][key]["pv"])
                    if h >= 1 or qd >= 1:
                        v.wait_ge(p.sem(f"{tag}_fin"), fin0 + fm - 5)
                    p.inc(v.reciprocal(liv_row[:, :], l_ps[qd][0:1, :]), f"{tag}_fin")
                    v.wait_ge(p.sem(f"{tag}_fin"), fin0 + fm - 2)
                    p.inc(v.tensor_copy(liv_sb[:, :], mi_ps[:, :]), f"{tag}_fin")
                    v.wait_ge(p.sem(f"{tag}_fin"), fin0 + fm - 1)
                    p.inc(v.tensor_tensor(
                        attn_sb[:, h * TC + qd * 512:h * TC + (qd + 1) * 512],
                        o_ps[qd][:, :], liv_sb[:, :], ALU.mult), f"{tag}_fin")

        @blk.tensor
        def _(t):
            for h in range(H):
                sl = h % 2
                m = sched[h]
                t.wait_ge(p.sem(f"{tag}_kh{sl}"), kh0[sl] + 16 * (h // 2 + 1))
                t.wait_ge(p.sem(f"{tag}_kl{sl}"), kl0[sl] + 16 * (h // 2 + 1))
                t.wait_ge(p.sem(f"{tag}_vl{sl}"), vl0[sl] + 16 * (h // 2 + 1))
                t.wait_ge(p.sem(f"{tag}_vg"), vg0 + h + 1)
                for j in range(10):
                    d = m["norm"][j]
                    t.wait_ge(p.sem(f"{tag}_sq"), sq0 + h * 10 + j + 1)
                    slot_wait(t, d["g"])
                    p.inc(t.matmul(s_ps[d["g"] % 3][0:1, 0:512], ones_col[:, :],
                                   ksq[(h * 10 + j) % 2][:, :],
                                   start=True, stop=True), f"{tag}_mm")
                t.wait_ge(p.sem(f"{tag}_row"), row0 + m["row"])
                if h >= 1:
                    t.wait_ge(p.sem(f"{tag}_fin"), fin0 + sched[h - 1]["fin"][1] - 1)
                p.inc(t.matmul(mi_ps[:, 0:1], ones_row[:, :], nb_sb[:, :],
                               start=True, stop=True), f"{tag}_eb")
                prev_chunk = None
                for kind, kc in chunk_list:
                    for qd in chunk_quads(kc, kind):
                        d = m["sc"][(kind, kc, qd)]
                        slot_wait(t, d["g"])
                        p.inc(t.matmul(s_ps[d["g"] % 3][:, :],
                                       k_chunk(h, kind, kc),
                                       q_sb[:, h * TC + qd * 512:
                                            h * TC + (qd + 1) * 512],
                                       start=True, stop=True), f"{tag}_mm")
                    if prev_chunk is not None:
                        _emit_pv(p, t, tag, sched, h, prev_chunk, chunk_quads,
                                 vt_chunk, gatesb_sb, ones_col, psb, o_ps, l_ps,
                                 ex0, msk0, fin0, eb0, NP_PT)
                    prev_chunk = (kind, kc)
                _emit_pv(p, t, tag, sched, h, prev_chunk, chunk_quads, vt_chunk,
                         gatesb_sb, ones_col, psb, o_ps, l_ps, ex0, msk0, fin0,
                         eb0, NP_PT)
                for qd in range(2):
                    fm = m["fin"][qd]
                    t.wait_ge(p.sem(f"{tag}_fin"), fin0 + fm - 3)
                    if qd == 0:
                        t.wait_ge(p.sem(f"{tag}_eb"), eb0 + m["eb"])
                    else:
                        t.wait_ge(p.sem(f"{tag}_fin"), fin0 + m["fin"][0] - 1)
                    p.inc(t.matmul(mi_ps[:, :], ones_row[:, :],
                                   liv_row[:, :], start=True, stop=True),
                          f"{tag}_fin")

    return


def _nmm_of_normslot(sched, gi):
    h, j = gi // 10, gi % 10
    return sched[h]["norm"][j]["mm"]


def _emit_pv(p, t, tag, sched, h, chunk, chunk_quads, vt_chunk, gatesb_sb,
             ones_col, psb, o_ps, l_ps, ex0, msk0, fin0, eb0, NP_PT):
    kind, kc = chunk
    m = sched[h]
    for qd in chunk_quads(kc, kind):
        d = m["sc"][(kind, kc, qd)]
        if d["msk"] is not None:
            t.wait_ge(p.sem(f"{tag}_msk"), msk0 + d["msk"])
        else:
            t.wait_ge(p.sem(f"{tag}_ex"), ex0 + d["ex"])
        first = (kind == "r" and kc == 0)
        last = (kind == "l" and kc == 4 * qd + 3)
        if first:
            if h >= 1:
                t.wait_ge(p.sem(f"{tag}_fin"), fin0 + sched[h - 1]["fin"][qd])
            else:
                pass
        t.matmul(o_ps[qd][:, :], vt_chunk(h, kind, kc),
                 psb[d["pt"] % NP_PT][:, :], start=first, stop=last)
        lhs_l = gatesb_sb[:, kc:kc + 1] if kind == "r" else ones_col[:, :]
        inst = t.matmul(l_ps[qd][0:1, :], lhs_l,
                        psb[d["pt"] % NP_PT][:, :], start=first, stop=last)
        p.inc(inst, f"{tag}_pv")



def phase_wo(p, attn_sb, wo_g, res_src, res_dst, tag="wo", res_ready=None):
    btag = tag
    """attn @ wo + res_src -> res_dst. wo_g: (g, half) -> AP [128, 8192] bf16."""
    nc = p.nc
    w0 = [p.base(f"{tag}_w{i}") for i in range(2)]
    mm0 = p.base(f"{tag}_mm")
    ev0 = p.base(f"{tag}_ev")
    rc0 = [p.base(f"{tag}_rc{i}") for i in range(2)]
    xo0 = [p.base(f"{tag}_xo{i}") for i in range(2)]
    ss0 = p.base(f"{tag}_ss")

    with ExitStack() as ctx:
        e = ctx.enter_context
        wsl = [e(nc.sbuf_tensor(f"{btag}_wb{i}", [128, 8192], BF16)) for i in range(2)]
        rc = [e(nc.sbuf_tensor(f"{btag}_rc{i}", [128, TC], F32)) for i in range(4)]
        xo = [e(nc.sbuf_tensor(f"{btag}_xo{i}", [128, TC], F32)) for i in range(2)]
        tmp = [e(nc.sbuf_tensor(f"{btag}_tm{i}", [128, TC], F32)) for i in range(2)]
        y_ps = [e(nc.psum_tensor(f"{btag}_y{j}", [128, TC], F32)) for j in range(4)]
        blk = e(nc.Block(f"blk_{btag}"))

        @blk.gpsimd
        def _(g):
            g.wait_ge(p.sem("cc"), 15)
            for gi in range(8):
                for half in range(2):
                    i = gi * 2 + half
                    sl = i % 2
                    if i >= 2:
                        g.wait_ge(p.sem(f"{tag}_mm"), mm0 + i - 1)
                    p.inc(g.dma_start(out=wsl[sl][:, :], in_=wo_g(gi, half)),
                          f"{tag}_w{sl}", 16)

        @blk.sync
        def _(s):
            for gi in range(8):
                for j in range(4):
                    if gi >= 1:
                        s.wait_ge(p.sem(f"{tag}_ev"), ev0 + 4 * gi)
                    if res_ready is not None:
                        for nm, val in res_ready(gi * 4 + j):
                            s.wait_ge(p.sem(nm), val)
                    p.inc(s.dma_start(out=rc[j][:, :],
                                      in_=res_src[gi * 4 + j]), f"{tag}_rc{gi % 2}", 16)

        @blk.tensor
        def _(t):
            for gi in range(8):
                if gi >= 1:
                    t.wait_ge(p.sem(f"{tag}_ev"), ev0 + gi * 4)
                for half in range(2):
                    i = gi * 2 + half
                    sl = i % 2
                    t.wait_ge(p.sem(f"{tag}_w{sl}"), w0[sl] + 16 * (i // 2 + 1))
                    for kc16 in range(16):
                        kc = half * 16 + kc16
                        for j in range(4):
                            for th in range(NT):
                                inst = t.matmul(
                                    y_ps[j][:, th * 512:(th + 1) * 512],
                                    wsl[sl][:, kc16 * 512 + j * 128: kc16 * 512 + (j + 1) * 128],
                                    attn_sb[:, kc * TC + th * 512: kc * TC + (th + 1) * 512],
                                    start=(kc == 0), stop=(kc == 31))
                    p.inc(inst, f"{tag}_mm")

        @blk.vector
        def _(v):
            nss = 0
            for gi in range(8):
                v.wait_ge(p.sem(f"{tag}_mm"), mm0 + (gi + 1) * 2)
                v.wait_ge(p.sem(f"{tag}_rc{gi % 2}"), rc0[gi % 2] + 16 * 4 * (gi // 2 + 1))
                for j in range(4):
                    idx = gi * 4 + j
                    if idx >= 2:
                        v.wait_ge(p.sem(f"{tag}_xo{idx % 2}"),
                                  xo0[idx % 2] + 16 * (idx // 2))
                        v.wait_ge(p.sem(f"{tag}_ev"), ev0 + idx - 1)
                    p.inc(v.tensor_scalar(tmp[idx % 2][:, :], y_ps[j][:, :], 1.0,
                                          None, ALU.mult), f"{tag}_ss")
                    nss += 1
                    v.wait_ge(p.sem(f"{tag}_ss"), ss0 + nss)
                    p.inc(v.tensor_tensor(xo[idx % 2][:, :], tmp[idx % 2][:, :],
                                          rc[j][:, :], ALU.add),
                          f"{tag}_ev")

        @blk.scalar
        def _(sc):
            for gi in range(8):
                for j in range(4):
                    idx = gi * 4 + j
                    sc.wait_ge(p.sem(f"{tag}_ev"), ev0 + idx + 1)
                    p.inc(sc.dma_start(out=res_dst[idx], in_=xo[idx % 2][:, :]),
                          f"{tag}_xo{idx % 2}", 16)
            sc.wait_ge(p.sem(f"{tag}_xo0"), p.cnt[f"{tag}_xo0"])
            sc.wait_ge(p.sem(f"{tag}_xo1"), p.cnt[f"{tag}_xo1"])

    return xo0



def build_program(upto="full"):
    nc = bass.Bass("TRN2", target_bir_lowering=False)

    x_in = nc.dram_tensor("x_fm", [D, TC], F32, kind="ExternalInput")
    wug_sh = nc.dram_tensor("wug_sh", [11, 128, 2 * D], F32, kind="ExternalInput")
    wdn_sh = nc.dram_tensor("wdn_sh", [43, 128, TC], F32, kind="ExternalInput")
    wkv_sh = nc.dram_tensor("wkv_sh", [4, 128, 2 * D], F32, kind="ExternalInput")
    wq_sh = nc.dram_tensor("wq_sh", [4, 128, D], F32, kind="ExternalInput")
    wo_sh = nc.dram_tensor("wo_sh", [2, 128, 2 * D], F32, kind="ExternalInput")
    eye_in = nc.dram_tensor("eye", [128, 128], BF16, kind="ExternalInput")
    gates_in = nc.dram_tensor("gates", [128, NREM], F32, kind="ExternalInput")
    gatesb_in = nc.dram_tensor("gatesb", [128, NREM], BF16, kind="ExternalInput")
    masks_in = nc.dram_tensor("masks", [4, 128, 512], BF16, kind="ExternalInput")
    out = nc.dram_tensor("out_fm", [D, TC], F32, kind="ExternalOutput")

    g16 = dict(
        wug=nc.dram_tensor("g_wug", [88 * 128, 2 * D], BF16, addr_space="Shared"),
        wdn=nc.dram_tensor("g_wdn", [344 * 128, TC], BF16, addr_space="Shared"),
        wkv=nc.dram_tensor("g_wkv", [32 * 128, 2 * D], BF16, addr_space="Shared"),
        wq=nc.dram_tensor("g_wq", [32 * 128, D], BF16, addr_space="Shared"),
        wo=nc.dram_tensor("g_wo", [16 * 128, 2 * D], BF16, addr_space="Shared"),
    )
    h_kind = "ExternalOutput" if upto == "ffna" else "Internal"
    h_scr = nc.dram_tensor("h_scr", [NFC, 128, TC], BF16, kind=h_kind)
    res1 = nc.dram_tensor("res1", [NDC, 128, TC], F32)
    res2 = nc.dram_tensor("res2", [NDC, 128, TC], F32)
    k_local = nc.dram_tensor("k_local", [H, 128, TC], BF16)
    vt_local = nc.dram_tensor("vt_local", [H, TC, 128], BF16)
    k_gath = nc.dram_tensor("k_gath", [4 * H * 128, TC], BF16)
    vt_gath = nc.dram_tensor("vt_gath", [4 * H * TC, 128], BF16)

    xn_dbg = (nc.dram_tensor("xn_dbg", [128, NDC * TC], BF16, kind="ExternalOutput")
              if upto in ("ln1", "ffna", "attn_fm") else None)

    x_r = x_in.ap().rearrange("(dc k) t -> dc k t", k=128)
    out_r = out.ap().rearrange("(dc k) t -> dc k t", k=128)
    h_pr = h_scr.ap().rearrange("(fp two) k t -> fp k two t", two=2)
    x_v = [x_r[dc, :, :] for dc in range(NDC)]
    out_v = [out_r[dc, :, :] for dc in range(NDC)]
    res1_v = [res1[dc, :, :] for dc in range(NDC)]
    res2_v = [res2[dc, :, :] for dc in range(NDC)]
    h_v = [h_scr[fc, :, :] for fc in range(NFC)]
    h_pairs = [h_pr[fp, :, :, :] for fp in range(43)]

    wug_r = g16["wug"].ap().rearrange("(fc k) m -> fc k m", k=128)
    wdn_r = g16["wdn"].ap().rearrange("(i k) m -> i k m", k=128)
    wkv_r = g16["wkv"].ap().rearrange("(h k) m -> h k m", k=128)
    wq_r = g16["wq"].ap().rearrange("(h k) m -> h k m", k=128)
    wo_r = g16["wo"].ap().rearrange("(i k) m -> i k m", k=128)
    kg_r = k_gath.ap().rearrange("(r h k) t -> k r h t", h=H, k=128)
    vg_r = vt_gath.ap().rearrange("(r h c q) d -> q r h c d", h=H, c=8, q=128)
    vl_r = vt_local.ap().rearrange("h (c q) d -> q h c d", q=128)

    def wug_g(fc):
        return wug_r[fc, :, :]

    def wdn_g(gi, fp):
        return wdn_r[gi * 43 + fp, :, :]

    def wkv_g(h):
        return wkv_r[h, :, :]

    def wq_g(h):
        return wq_r[h, :, :]

    def wo_g(gi, half):
        return wo_r[gi * 2 + half, :, :]

    def kg_of(h):
        return kg_r[:, 0:3, h, :]

    def vg_of(h, r):
        return vg_r[:, r, h, :, :]

    def vl_of(h):
        return vl_r[:, h, :, :]

    with ExitStack() as ctx:
        p = Prog(nc, ctx)
        p.sem("cc")
        eye_sb = ctx.enter_context(nc.sbuf_tensor("eye_sb", [128, 128], BF16))
        gates_sb = ctx.enter_context(nc.sbuf_tensor("gates_sb", [128, NREM], F32))
        gatesb_sb = ctx.enter_context(nc.sbuf_tensor("gatesb_sb", [128, NREM], BF16))
        masks_sb = [ctx.enter_context(nc.sbuf_tensor(f"mk{i}_sb", [128, 512], BF16))
                    for i in range(4)]
        with nc.Block("blk_consts") as blk:
            @blk.sync
            def _(s):
                p.inc(s.dma_start(out=eye_sb[:, :], in_=eye_in[:, :]), "cst", 16)
                p.inc(s.dma_start(out=gates_sb[:, :], in_=gates_in[:, :]), "cst", 16)
                p.inc(s.dma_start(out=gatesb_sb[:, :], in_=gatesb_in[:, :]), "cst", 16)
                for i in range(4):
                    p.inc(s.dma_start(out=masks_sb[i][:, :], in_=masks_in[i, :, :]),
                          "cst", 16)
                s.wait_ge(p.sem("cst"), 112)

        sh = dict(wug_sh=wug_sh, wdn_sh=wdn_sh, wkv_sh=wkv_sh, wq_sh=wq_sh,
                  wo_sh=wo_sh)
        phase_wprep(p, sh, g16)

        bigA = ctx.enter_context(nc.sbuf_tensor("bigA", [128, NDC * TC], BF16))

        phase_ln(p, x_v, bigA, "ln", btag="l1")
        if upto == "ln1":
            _finish(p, nc, [("wd_dmy", 0)], xn_dbg, bigA)
            return nc

        hb = phase_ffn_a(p, bigA, h_v, wug_g, "fa", btag="fa1")
        if upto == "ffna":
            _finish(p, nc, [("fa_h0", p.cnt["fa_h0"]), ("fa_h1", p.cnt["fa_h1"])],
                    xn_dbg, bigA)
            return nc

        h_rdy1 = lambda fp: [("fa_h0", hb[0] + 16 * (fp + 1)),
                             ("fa_h1", hb[1] + 16 * (fp + 1))]
        dst1 = out_v if upto == "ffn1" else res1_v
        xb1 = phase_ffn_b(p, h_pairs, x_v, dst1, wdn_g, "fb", btag="fb1",
                          yscale=0.5, h_ready=h_rdy1)
        if upto == "ffn1":
            _finish(p, nc, [("fb_xo0", p.cnt["fb_xo0"]),
                            ("fb_xo1", p.cnt["fb_xo1"])])
            return nc

        res1_rdy = lambda dc: [(f"fb_xo{dc % 2}", xb1[dc % 2] + 16 * (dc // 2 + 1))]
        phase_ln(p, res1_v, bigA, "ln", src_ready=res1_rdy, btag="l2")

        phase_kvproj(p, bigA, wkv_g, k_local, vt_local, k_gath, vt_gath, eye_sb)

        bigB = ctx.enter_context(nc.sbuf_tensor("bigB", [128, H * TC], BF16))
        phase_qproj(p, bigA, wq_g, bigB)

        phase_flash(p, bigB, bigA, kg_of, k_local, vg_of, vl_of,
                    gates_sb, gatesb_sb, masks_sb)
        if upto == "attn_fm":
            _finish(p, nc, [("fl_fin", p.cnt["fl_fin"])], xn_dbg, bigA)
            return nc

        dst2 = out_v if upto == "attn" else res2_v
        xb2 = phase_wo(p, bigA, wo_g, res1_v, dst2, res_ready=res1_rdy)
        if upto == "attn":
            _finish(p, nc, [("wo_xo0", p.cnt["wo_xo0"]), ("wo_xo1", p.cnt["wo_xo1"])])
            return nc

        res2_rdy = lambda dc: [(f"wo_xo{dc % 2}", xb2[dc % 2] + 16 * (dc // 2 + 1))]
        bigC = bigB
        phase_ln(p, res2_v, bigC, "ln", src_ready=res2_rdy, btag="l3")

        h_war = [("fb_h0", p.cnt["fb_h0"]), ("fb_h1", p.cnt["fb_h1"])]
        hb2 = phase_ffn_a(p, bigC, h_v, wug_g, "fa", h_war=h_war, btag="fa2")
        h_rdy2 = lambda fp: [("fa_h0", hb2[0] + 16 * (fp + 1)),
                             ("fa_h1", hb2[1] + 16 * (fp + 1))]
        phase_ffn_b(p, h_pairs, res2_v, out_v, wdn_g, "fb", btag="fb2",
                    yscale=0.5, h_ready=h_rdy2)
        _finish(p, nc, [("fb_xo0", p.cnt["fb_xo0"]), ("fb_xo1", p.cnt["fb_xo1"])])
        return nc


def _finish(p, nc, waits, xn_dbg=None, bigA=None):
    with nc.Block("blk_fin") as blk:
        @blk.sync
        def _(s):
            for nm, val in waits:
                if val:
                    s.wait_ge(p.sem(nm), val)
            if xn_dbg is not None:
                p.inc(s.dma_start(out=xn_dbg[:, :], in_=bigA[:, :]), "fin", 16)
                s.wait_ge(p.sem("fin"), p.cnt["fin"])



def kernel(L0_attn_hidden_states, L0_attn_wq, L0_attn_wkv, L0_attn_wo,
           L0_ffn_W_gate, L0_ffn_W_up, L0_ffn_W_down):
    x = np.asarray(L0_attn_hidden_states)
    lay = host_layouts(L0_attn_wq, L0_attn_wkv, L0_attn_wo,
                       L0_ffn_W_gate, L0_ffn_W_up, L0_ffn_W_down)
    shards = host_shards(lay)
    nc = build_program("full")
    in_maps = []
    for core in range(8):
        sq, c = core // 4, core % 4
        xs = np.ascontiguousarray(x[sq, c * TC:(c + 1) * TC, :].T)
        m = dict(x_fm=xs)
        m.update(shards[core])
        m.update(host_consts(core))
        in_maps.append(m)
    res = run_bass_kernel_spmd(nc, in_maps, core_ids=list(range(8)))
    outp = np.empty((B, S, D), np.float32)
    for core in range(8):
        sq, c = core // 4, core % 4
        outp[sq, c * TC:(c + 1) * TC, :] = np.asarray(res.results[core]["out_fm"]).T
    return outp


# version 04 (final)
# speedup vs baseline: 1.2556x; 1.0349x over previous
import numpy as np
import ml_dtypes
from contextlib import ExitStack

import concourse.bass as bass
import concourse.mybir as mybir
from concourse.bass_utils import run_bass_kernel_spmd

F32 = mybir.dt.float32
BF16 = mybir.dt.bfloat16
AF = mybir.ActivationFunctionType
ALU = mybir.AluOpType
AX = mybir.AxisListType

B, S, D, H, DN, DV, DQ, DFF = 2, 4096, 4096, 32, 128, 128, 192, 11008
TC = 1024
NDC = D // 128
NFC = DFF // 128
NT = TC // 512
SCALE = float(DQ) ** -0.5
EPS = 1e-5
NREM = 24
NB_R = 6



def host_layouts(L0_attn_wq, L0_attn_wkv, L0_attn_wo, L0_ffn_W_gate,
                 L0_ffn_W_up, L0_ffn_W_down):
    wq = np.asarray(L0_attn_wq)
    wkv = np.asarray(L0_attn_wkv)
    wo = np.asarray(L0_attn_wo)
    W_gate = np.asarray(L0_ffn_W_gate)
    W_up = np.asarray(L0_ffn_W_up)
    W_down = np.asarray(L0_ffn_W_down)

    def blk_updown(Wfd):
        a = Wfd.reshape(NFC, 128, NDC, 128)
        return a.transpose(0, 3, 2, 1).reshape(NFC, 128, D)

    wupgate = np.concatenate([blk_updown(W_up), blk_updown(W_gate)], axis=2)

    a = W_down.reshape(8, 4, 128, 43, 2, 128)
    wdn = np.ascontiguousarray(a.transpose(0, 3, 5, 4, 1, 2)).reshape(344, 128, TC)

    q_cols = wq.reshape(D, H, DQ)[:, :, :DN]
    a = q_cols.reshape(NDC, 128, H, DN)
    wq_blk = np.ascontiguousarray(a.transpose(2, 1, 0, 3)).reshape(H, 128, D)

    k_cols = wkv[:, :H * DN].reshape(D, H, DN)
    v_cols = wkv[:, H * DN:H * (DN + DV)].reshape(D, H, DV)
    kv = np.stack([k_cols, v_cols], axis=2)
    a = kv.reshape(NDC, 128, H, 2, 128)
    wkv_blk = np.ascontiguousarray(a.transpose(2, 1, 3, 0, 4)).reshape(H, 128, 2 * D)

    a = wo.reshape(2, 16, 128, 8, 4, 128)
    wo_blk = np.ascontiguousarray(a.transpose(3, 0, 2, 1, 4, 5)).reshape(16, 128, 8192)

    return dict(wupgate=wupgate, wdn=wdn, wq_blk=wq_blk, wkv_blk=wkv_blk,
                wo_blk=wo_blk)


def host_shards(lay):
    wug_pad = np.zeros((88, 128, 2 * D), np.float32)
    wug_pad[:NFC] = lay["wupgate"]
    shards = []
    for r in range(8):
        shards.append(dict(
            wug_sh=np.ascontiguousarray(wug_pad[r::8]),
            wdn_sh=np.ascontiguousarray(lay["wdn"][43 * r:43 * (r + 1)]),
            wkv_sh=np.ascontiguousarray(lay["wkv_blk"][4 * r:4 * (r + 1)]),
            wq_sh=np.ascontiguousarray(lay["wq_blk"][4 * r:4 * (r + 1)]),
            wo_sh=np.ascontiguousarray(lay["wo_blk"][2 * r:2 * (r + 1)]),
        ))
    return shards


def host_consts(core):
    c = core % 4
    eye = np.eye(128, dtype=ml_dtypes.bfloat16)
    gates = np.zeros((128, NREM), np.float32)
    for kc in range(NREM):
        gates[:, kc] = 1.0 if (kc // 8) < c else 0.0
    gatesb = gates.astype(ml_dtypes.bfloat16)
    tri = np.triu(np.ones((128, 128), np.float32))
    ones = np.ones((128, 128), np.float32)
    zeros = np.zeros((128, 128), np.float32)
    masks = np.stack([
        np.concatenate([zeros] * i + [tri] + [ones] * (3 - i), axis=1)
        for i in range(4)]).astype(ml_dtypes.bfloat16)
    return dict(eye=eye, gates=gates, gatesb=gatesb, masks=masks)



class Prog:
    def __init__(self, nc, ctx):
        self.nc = nc
        self.ctx = ctx
        self._sems = {}
        self.cnt = {}

    def sem(self, name):
        if name not in self._sems:
            self._sems[name] = self.ctx.enter_context(self.nc.semaphore(name))
            self.cnt[name] = 0
        return self._sems[name]

    def inc(self, inst, name, n=1):
        inst.then_inc(self.sem(name), n)
        self.cnt[name] += n
        return self.cnt[name]

    def base(self, name):
        self.sem(name)
        return self.cnt[name]



from contextlib import contextmanager


@contextmanager
def _nullctx(x):
    yield x



def phase_wprep(p, sh, g16, tag="wp", blk=None):
    btag = tag
    """cc counts after this phase: wug pieces 1..11, wdn 12, wkv 13, wq 14, wo 15."""
    nc = p.nc
    cs0 = p.base("wp_cs")
    loc = dict(
        wug=nc.dram_tensor(f"{tag}_wug16", [11, 128, 2 * D], BF16),
        wdn=nc.dram_tensor(f"{tag}_wdn16", [43, 128, TC], BF16),
        wkv=nc.dram_tensor(f"{tag}_wkv16", [4, 128, 2 * D], BF16),
        wq=nc.dram_tensor(f"{tag}_wq16", [4, 128, D], BF16),
        wo=nc.dram_tensor(f"{tag}_wo16", [2, 128, 2 * D], BF16),
    )
    with (nc.Block(f"blk_{btag}") if blk is None else _nullctx(blk)) as blk:
        @blk.gpsimd
        def _(g):
            ncast = 0
            for i in range(11):
                p.inc(g.dma_start(out=loc["wug"][i, :, :], in_=sh["wug_sh"][i, :, :],
                                  max_dma_last_dim=2048), "wp_cs", 16)
                ncast += 1
                g.wait_ge(p.sem("wp_cs"), cs0 + 16 * ncast)
                p.inc(g.collective_compute(
                    "AllGather", ALU.bypass, replica_groups=[list(range(8))],
                    ins=[loc["wug"][i, :, :]],
                    outs=[g16["wug"].ap()[i * 1024:(i + 1) * 1024, :]],
                ), "cc", 1)
            for nm, npc in (("wdn", 43), ("wkv", 4), ("wq", 4), ("wo", 2)):
                for i in range(npc):
                    p.inc(g.dma_start(out=loc[nm][i, :, :], in_=sh[f"{nm}_sh"][i, :, :],
                                      max_dma_last_dim=2048), "wp_cs", 16)
                    ncast += 1
                g.wait_ge(p.sem("wp_cs"), cs0 + 16 * ncast)
                p.inc(g.collective_compute(
                    "AllGather", ALU.bypass, replica_groups=[list(range(8))],
                    ins=[loc[nm].ap().rearrange("c k m -> (c k) m")],
                    outs=[g16[nm].ap()],
                ), "cc", 1)



def phase_ln(p, src_dram, dst_sb, tag, src_ready=None, btag=None, blk=None,
             ctx=None, entry=None):
    btag = btag or tag
    """Stream src (32 dram APs [128, TC] f32) twice; write bf16 into
    dst_sb[:, dc*TC + t]. src_ready(dc) -> [(sem, val)] readiness waits."""
    nc = p.nc
    ld0 = [p.base(f"{tag}_ld{i}") for i in range(2)]
    ft0 = p.base(f"{tag}_ft")
    sq0 = p.base(f"{tag}_sq")
    rw0 = p.base(f"{tag}_row")
    bc0 = p.base(f"{tag}_bc")
    ap0 = p.base(f"{tag}_ap")
    ss0 = p.base(f"{tag}_ss")

    with ExitStack() as lctx:
        e = (ctx or lctx).enter_context
        el = lctx.enter_context
        xc = [e(nc.sbuf_tensor(f"{btag}_xc{i}", [128, TC], F32)) for i in range(2)]
        xsq = [e(nc.sbuf_tensor(f"{btag}_xsq0", [128, TC], F32))] * 2
        ones_col = e(nc.sbuf_tensor(f"{btag}_ones", [128, 1], F32))
        ones_row = e(nc.sbuf_tensor(f"{btag}_onesr", [1, 128], F32))
        srow = e(nc.sbuf_tensor(f"{btag}_srow", [1, TC], F32))
        qrow = e(nc.sbuf_tensor(f"{btag}_qrow", [1, TC], F32))
        mrow = e(nc.sbuf_tensor(f"{btag}_mrow", [1, TC], F32))
        vrow = e(nc.sbuf_tensor(f"{btag}_vrow", [1, TC], F32))
        arow = srow
        brow = qrow
        tmp = [e(nc.sbuf_tensor(f"{btag}_tmp0", [128, TC], F32))] * 2
        eps_sb = e(nc.sbuf_tensor(f"{btag}_eps", [1, 1], F32))
        st_ps = el(nc.psum_tensor(f"{btag}_stp", [1, TC], F32))
        sq_ps = el(nc.psum_tensor(f"{btag}_sqp", [1, TC], F32))
        abc_ps = el(nc.psum_tensor(f"{btag}_abc", [128, TC], F32))
        bbc_ps = el(nc.psum_tensor(f"{btag}_bbc", [128, TC], F32))
        blk = blk if blk is not None else el(nc.Block(f"blk_{btag}"))

        @blk.sync
        def _(s):
            for r in range(2):
                for dc in range(NDC):
                    sl = dc % 2
                    if r == 0:
                        if src_ready is not None:
                            for nm, val in src_ready(dc):
                                s.wait_ge(p.sem(nm), val)
                        if dc >= 2:
                            s.wait_ge(p.sem(f"{tag}_ft"), ft0 + dc - 1)
                            s.wait_ge(p.sem(f"{tag}_sq"), sq0 + dc - 1)
                    else:
                        if dc < 2:
                            s.wait_ge(p.sem(f"{tag}_ft"), ft0 + NDC)
                            s.wait_ge(p.sem(f"{tag}_sq"), sq0 + NDC)
                        else:
                            s.wait_ge(p.sem(f"{tag}_ap"), ap0 + dc - 1)
                    p.inc(s.dma_start(out=xc[sl][:, :], in_=src_dram[dc]),
                          f"{tag}_ld{sl}", 16)

        @blk.vector
        def _(v):
            p.inc(v.memset(ones_col[:, :], 1.0), f"{tag}_row")
            p.inc(v.memset(ones_row[:, :], 1.0), f"{tag}_row")
            p.inc(v.memset(eps_sb[:, :], EPS), f"{tag}_row")

        @blk.scalar
        def _(sc):
            for dc in range(NDC):
                sl = dc % 2
                sc.wait_ge(p.sem(f"{tag}_ld{sl}"), ld0[sl] + 16 * (dc // 2 + 1))
                if dc >= 1:
                    sc.wait_ge(p.sem(f"{tag}_ft"), ft0 + dc)
                p.inc(sc.activation(xsq[sl][:, :], xc[sl][:, :], AF.Square),
                      f"{tag}_sq")

        @blk.tensor
        def _(t):
            for nm, val in (entry or ()):
                t.wait_ge(p.sem(nm), val)
            t.wait_ge(p.sem(f"{tag}_row"), rw0 + 3)
            for dc in range(NDC):
                sl = dc % 2
                t.wait_ge(p.sem(f"{tag}_ld{sl}"), ld0[sl] + 16 * (dc // 2 + 1))
                t.wait_ge(p.sem(f"{tag}_sq"), sq0 + dc + 1)
                for th in range(NT):
                    tsl = slice(th * 512, (th + 1) * 512)
                    t.matmul(st_ps[:, tsl], ones_col[:, :], xc[sl][:, tsl],
                             start=(dc == 0), stop=(dc == NDC - 1))
                    inst = t.matmul(sq_ps[:, tsl], ones_col[:, :], xsq[sl][:, tsl],
                                    start=(dc == 0), stop=(dc == NDC - 1))
                p.inc(inst, f"{tag}_ft")

        @blk.vector
        def _(v):
            v.wait_ge(p.sem(f"{tag}_ft"), ft0 + NDC)
            p.inc(v.tensor_scalar(mrow[:, :], st_ps[:, :], 1.0 / D, None, ALU.mult),
                  f"{tag}_row")
            v.wait_ge(p.sem(f"{tag}_row"), rw0 + 4)
            p.inc(v.tensor_scalar(vrow[:, :], sq_ps[:, :], 1.0 / D, None, ALU.mult),
                  f"{tag}_row")
            v.wait_ge(p.sem(f"{tag}_row"), rw0 + 5)
            p.inc(v.tensor_tensor(qrow[:, :], mrow[:, :], mrow[:, :], ALU.mult),
                  f"{tag}_row")
            v.wait_ge(p.sem(f"{tag}_row"), rw0 + 6)
            p.inc(v.tensor_tensor(vrow[:, :], vrow[:, :], qrow[:, :], ALU.subtract),
                  f"{tag}_row")

        @blk.scalar
        def _(sc):
            sc.wait_ge(p.sem(f"{tag}_row"), rw0 + 7)
            p.inc(sc.activation(srow[:, :], vrow[:, :], AF.Ln, bias=eps_sb[:, :]),
                  f"{tag}_row")
            sc.wait_ge(p.sem(f"{tag}_row"), rw0 + 8)
            p.inc(sc.activation(arow[:, :], srow[:, :], AF.Exp, scale=-0.5),
                  f"{tag}_row")

        @blk.vector
        def _(v):
            v.wait_ge(p.sem(f"{tag}_row"), rw0 + 9)
            p.inc(v.tensor_tensor(brow[:, :], mrow[:, :], arow[:, :], ALU.mult),
                  f"{tag}_row")
            v.wait_ge(p.sem(f"{tag}_row"), rw0 + 10)
            p.inc(v.tensor_scalar(brow[:, :], brow[:, :], -1.0, None, ALU.mult),
                  f"{tag}_row")

        @blk.tensor
        def _(t):
            t.wait_ge(p.sem(f"{tag}_row"), rw0 + 11)
            for th in range(NT):
                tsl = slice(th * 512, (th + 1) * 512)
                t.matmul(abc_ps[:, tsl], ones_row[:, :], arow[:, tsl],
                         start=True, stop=True)
                inst = t.matmul(bbc_ps[:, tsl], ones_row[:, :], brow[:, tsl],
                                start=True, stop=True)
            p.inc(inst, f"{tag}_bc")

        @blk.vector
        def _(v):
            v.wait_ge(p.sem(f"{tag}_bc"), bc0 + 1)
            for dc in range(NDC):
                sl = dc % 2
                v.wait_ge(p.sem(f"{tag}_ld{sl}"), ld0[sl] + 16 * (NDC // 2 + dc // 2 + 1))
                if dc >= 1:
                    v.wait_ge(p.sem(f"{tag}_ap"), ap0 + dc)
                p.inc(v.tensor_tensor(tmp[sl][:, :], xc[sl][:, :], abc_ps[:, :],
                                      ALU.mult), f"{tag}_ss")
                v.wait_ge(p.sem(f"{tag}_ss"), ss0 + dc + 1)
                p.inc(v.tensor_tensor(dst_sb[:, dc * TC:(dc + 1) * TC],
                                      tmp[sl][:, :], bbc_ps[:, :], ALU.add),
                      f"{tag}_ap")



def phase_ffn_a(p, xn_sb, h_dram, wug_g, tag, h_war=None, btag=None, blk=None,
                ctx=None, entry=None):
    btag = btag or tag
    """wug_g: fc -> dram AP [128, 2D] bf16 (gathered). h_war: [(sem,val)] to
    wait once before the first h write (prior readers of h_scr)."""
    nc = p.nc
    w0 = [p.base(f"{tag}_w{i}") for i in range(2)]
    mm0 = p.base(f"{tag}_mm")
    si0 = p.base(f"{tag}_si")
    vt0 = p.base(f"{tag}_vt")
    hs0 = [p.base(f"{tag}_h{i}") for i in range(2)]

    with ExitStack() as lctx:
        e = (ctx or lctx).enter_context
        el = lctx.enter_context
        wsl = [e(nc.sbuf_tensor(f"{btag}_wb{i}", [128, 2 * D], BF16)) for i in range(2)]
        sg = [e(nc.sbuf_tensor(f"{btag}_sg{i}", [128, TC], F32)) for i in range(2)]
        hsb = [e(nc.sbuf_tensor(f"{btag}_hb{i}", [128, TC], BF16)) for i in range(2)]
        up_ps = [el(nc.psum_tensor(f"{btag}_up{i}", [128, TC], F32)) for i in range(2)]
        gt_ps = [el(nc.psum_tensor(f"{btag}_gt{i}", [128, TC], F32)) for i in range(2)]
        blk = blk if blk is not None else el(nc.Block(f"blk_{btag}"))

        @blk.gpsimd
        def _(g):
            for fc in range(NFC):
                sl = fc % 2
                g.wait_ge(p.sem("cc"), 1 + fc // 8)
                if fc >= 2:
                    g.wait_ge(p.sem(f"{tag}_mm"), mm0 + 4 * (fc - 1))
                p.inc(g.dma_start(out=wsl[sl][:, :], in_=wug_g(fc)), f"{tag}_w{sl}", 16)

        @blk.tensor
        def _(t):
            for nm, val in (entry or ()):
                t.wait_ge(p.sem(nm), val)
            for fc in range(NFC):
                sl = fc % 2
                t.wait_ge(p.sem(f"{tag}_w{sl}"), w0[sl] + 16 * (fc // 2 + 1))
                if fc >= 2:
                    t.wait_ge(p.sem(f"{tag}_vt"), vt0 + 2 * (fc - 1))
                    t.wait_ge(p.sem(f"{tag}_si"), si0 + 2 * (fc - 1))
                for th in range(NT):
                    tsl = slice(th * 512, (th + 1) * 512)
                    for dc in range(NDC):
                        inst = t.matmul(up_ps[sl][:, tsl],
                                        wsl[sl][:, dc * 128:(dc + 1) * 128],
                                        xn_sb[:, dc * TC + th * 512: dc * TC + (th + 1) * 512],
                                        start=(dc == 0), stop=(dc == NDC - 1))
                    p.inc(inst, f"{tag}_mm")
                    for dc in range(NDC):
                        inst = t.matmul(gt_ps[sl][:, tsl],
                                        wsl[sl][:, D + dc * 128: D + (dc + 1) * 128],
                                        xn_sb[:, dc * TC + th * 512: dc * TC + (th + 1) * 512],
                                        start=(dc == 0), stop=(dc == NDC - 1))
                    p.inc(inst, f"{tag}_mm")

        @blk.scalar
        def _(sc):
            for fc in range(NFC):
                sl = fc % 2
                for th in range(NT):
                    tsl = slice(th * 512, (th + 1) * 512)
                    sc.wait_ge(p.sem(f"{tag}_mm"), mm0 + 4 * fc + 2 * (th + 1))
                    if fc >= 2:
                        sc.wait_ge(p.sem(f"{tag}_vt"), vt0 + 2 * (fc - 2) + th + 1)
                    p.inc(sc.activation(sg[sl][:, tsl], gt_ps[sl][:, tsl], AF.Silu),
                          f"{tag}_si")

        @blk.vector
        def _(v):
            for fc in range(NFC):
                sl = fc % 2
                for th in range(NT):
                    tsl = slice(th * 512, (th + 1) * 512)
                    v.wait_ge(p.sem(f"{tag}_si"), si0 + 2 * fc + th + 1)
                    v.wait_ge(p.sem(f"{tag}_mm"), mm0 + 4 * fc + 2 * th + 1)
                    if fc >= 2 and th == 0:
                        v.wait_ge(p.sem(f"{tag}_h{sl}"), hs0[sl] + 16 * (fc // 2))
                    p.inc(v.tensor_tensor(hsb[sl][:, tsl], sg[sl][:, tsl],
                                          up_ps[sl][:, tsl], ALU.mult), f"{tag}_vt")

        @blk.sync
        def _(s):
            if h_war:
                for nm, val in h_war:
                    s.wait_ge(p.sem(nm), val)
            for fc in range(NFC):
                sl = fc % 2
                s.wait_ge(p.sem(f"{tag}_vt"), vt0 + 2 * (fc + 1))
                p.inc(s.dma_start(out=h_dram[fc], in_=hsb[sl][:, :]), f"{tag}_h{sl}", 16)
            s.wait_ge(p.sem(f"{tag}_h0"), p.cnt[f"{tag}_h0"])
            s.wait_ge(p.sem(f"{tag}_h1"), p.cnt[f"{tag}_h1"])

    return hs0



def phase_ffn_b(p, h_pairs, res_src, res_dst, wdn_g, tag, yscale=0.5,
                h_ready=None, res_ready=None, btag=None, blk=None, ctx=None,
                entry=None):
    btag = btag or tag
    """h_pairs: fp -> AP [128,2,TC] bf16; res_*: dc -> AP [128, TC] f32.
    wdn_g: (gi, fp) -> AP [128, TC] bf16. Out DMAs ride the scalar queue."""
    nc = p.nc
    w0 = [p.base(f"{tag}_w{i}") for i in range(2)]
    h0 = [p.base(f"{tag}_h{i}") for i in range(2)]
    mm0 = p.base(f"{tag}_mm")
    ev0 = p.base(f"{tag}_ev")
    rc0 = [p.base(f"{tag}_rc{i}") for i in range(2)]
    xo0 = [p.base(f"{tag}_xo{i}") for i in range(2)]
    ss0 = p.base(f"{tag}_ss")

    NP = 43
    with ExitStack() as lctx:
        e = (ctx or lctx).enter_context
        el = lctx.enter_context
        wd = [e(nc.sbuf_tensor(f"{btag}_wb{i}", [128, TC], BF16)) for i in range(2)]
        hh = [e(nc.sbuf_tensor(f"{btag}_hb{i}", [128, 2 * TC], BF16)) for i in range(2)]
        rc = [e(nc.sbuf_tensor(f"{btag}_rc{i}", [128, TC], F32)) for i in range(4)]
        xo = [e(nc.sbuf_tensor(f"{btag}_xo{i}", [128, TC], F32)) for i in range(2)]
        tmp = [e(nc.sbuf_tensor(f"{btag}_tm{i}", [128, TC], F32)) for i in range(2)]
        y_ps = [el(nc.psum_tensor(f"{btag}_y{j}", [128, TC], F32)) for j in range(4)]
        blk = blk if blk is not None else el(nc.Block(f"blk_{btag}"))

        @blk.gpsimd
        def _(g):
            g.wait_ge(p.sem("cc"), 12)
            for gi in range(8):
                for fp in range(NP):
                    i = gi * NP + fp
                    sl = i % 2
                    if i >= 2:
                        g.wait_ge(p.sem(f"{tag}_mm"), mm0 + i - 1)
                    p.inc(g.dma_start(out=wd[sl][:, :], in_=wdn_g(gi, fp)),
                          f"{tag}_w{sl}", 16)

        @blk.sync
        def _(s):
            for gi in range(8):
                for j in range(4):
                    if gi >= 1:
                        s.wait_ge(p.sem(f"{tag}_ev"), ev0 + 4 * gi)
                    if res_ready is not None:
                        for nm, val in res_ready(gi * 4 + j):
                            s.wait_ge(p.sem(nm), val)
                    p.inc(s.dma_start(out=rc[j][:, :],
                                      in_=res_src[gi * 4 + j]), f"{tag}_rc{gi % 2}", 16)
                for fp in range(NP):
                    i = gi * NP + fp
                    sl = i % 2
                    if i >= 2:
                        s.wait_ge(p.sem(f"{tag}_mm"), mm0 + i - 1)
                    if gi == 0 and h_ready is not None:
                        for nm, val in h_ready(fp):
                            s.wait_ge(p.sem(nm), val)
                    p.inc(s.dma_start(out=hh[sl][:, :], in_=h_pairs[fp]),
                          f"{tag}_h{sl}", 16)

        @blk.tensor
        def _(t):
            for nm, val in (entry or ()):
                t.wait_ge(p.sem(nm), val)
            for gi in range(8):
                if gi >= 1:
                    t.wait_ge(p.sem(f"{tag}_ev"), ev0 + gi * 4)
                for fp in range(NP):
                    i = gi * NP + fp
                    sl = i % 2
                    t.wait_ge(p.sem(f"{tag}_w{sl}"), w0[sl] + 16 * (i // 2 + 1))
                    t.wait_ge(p.sem(f"{tag}_h{sl}"), h0[sl] + 16 * (i // 2 + 1))
                    for fc2 in range(2):
                        for j in range(4):
                            for th in range(NT):
                                inst = t.matmul(
                                    y_ps[j][:, th * 512:(th + 1) * 512],
                                    wd[sl][:, (fc2 * 4 + j) * 128:(fc2 * 4 + j + 1) * 128],
                                    hh[sl][:, fc2 * TC + th * 512: fc2 * TC + (th + 1) * 512],
                                    start=(fp == 0 and fc2 == 0),
                                    stop=(fp == NP - 1 and fc2 == 1))
                    p.inc(inst, f"{tag}_mm")

        @blk.vector
        def _(v):
            nss = 0
            for gi in range(8):
                v.wait_ge(p.sem(f"{tag}_mm"), mm0 + (gi + 1) * NP)
                v.wait_ge(p.sem(f"{tag}_rc{gi % 2}"), rc0[gi % 2] + 16 * 4 * (gi // 2 + 1))
                for j in range(4):
                    idx = gi * 4 + j
                    if idx >= 2:
                        v.wait_ge(p.sem(f"{tag}_xo{idx % 2}"),
                                  xo0[idx % 2] + 16 * (idx // 2))
                        v.wait_ge(p.sem(f"{tag}_ev"), ev0 + idx - 1)
                    p.inc(v.tensor_scalar(tmp[idx % 2][:, :], y_ps[j][:, :], yscale,
                                          None, ALU.mult), f"{tag}_ss")
                    nss += 1
                    v.wait_ge(p.sem(f"{tag}_ss"), ss0 + nss)
                    p.inc(v.tensor_tensor(xo[idx % 2][:, :], tmp[idx % 2][:, :],
                                          rc[j][:, :], ALU.add),
                          f"{tag}_ev")

        @blk.scalar
        def _(sc):
            for gi in range(8):
                for j in range(4):
                    idx = gi * 4 + j
                    sc.wait_ge(p.sem(f"{tag}_ev"), ev0 + idx + 1)
                    p.inc(sc.dma_start(out=res_dst[idx], in_=xo[idx % 2][:, :]),
                          f"{tag}_xo{idx % 2}", 16)
            sc.wait_ge(p.sem(f"{tag}_xo0"), p.cnt[f"{tag}_xo0"])
            sc.wait_ge(p.sem(f"{tag}_xo1"), p.cnt[f"{tag}_xo1"])

    return xo0



def phase_kvproj(p, xn_sb, wkv_g, k_local, vt_local, k_gath, vt_gath,
                 eye_sb, tag="kp", blk=None, ctx=None, entry=None):
    btag = tag
    nc = p.nc
    w0 = [p.base(f"{tag}_w{i}") for i in range(2)]
    mm0 = p.base(f"{tag}_mm")
    ev0 = p.base(f"{tag}_ev")
    av0 = p.base(f"{tag}_av")
    tr0 = p.base(f"{tag}_tr")
    vt0 = p.base(f"{tag}_vt")
    kd0 = [p.base(f"{tag}_kd{i}") for i in range(2)]
    vd0 = [p.base(f"{tag}_vd{i}") for i in range(2)]

    with ExitStack() as lctx:
        e = (ctx or lctx).enter_context
        el = lctx.enter_context
        wsl = [e(nc.sbuf_tensor(f"{btag}_wb{i}", [128, 2 * D], BF16)) for i in range(2)]
        kst = [e(nc.sbuf_tensor(f"{btag}_k{i}", [128, TC], BF16)) for i in range(2)]
        vst = [e(nc.sbuf_tensor(f"{btag}_v{i}", [128, TC], BF16)) for i in range(2)]
        vtt = [e(nc.sbuf_tensor(f"{btag}_vt{i}", [128, TC], BF16)) for i in range(2)]
        k_ps = el(nc.psum_tensor(f"{btag}_kp", [128, TC], F32))
        v_ps = el(nc.psum_tensor(f"{btag}_vp", [128, TC], F32))
        t_ps = [el(nc.psum_tensor(f"{btag}_tp{i}", [128, 1024], BF16)) for i in range(2)]
        blk = blk if blk is not None else el(nc.Block(f"blk_{btag}"))

        @blk.gpsimd
        def _(g):
            g.wait_ge(p.sem("cc"), 13)
            for h in range(H):
                sl = h % 2
                if h >= 2:
                    g.wait_ge(p.sem(f"{tag}_mm"), mm0 + 2 * (h - 1))
                p.inc(g.dma_start(out=wsl[sl][:, :], in_=wkv_g(h)), f"{tag}_w{sl}", 16)
            g.wait_ge(p.sem(f"{tag}_kd0"), kd0[0] + 16 * (H // 2))
            g.wait_ge(p.sem(f"{tag}_kd1"), kd0[1] + 16 * (H // 2))
            g.wait_ge(p.sem(f"{tag}_vd0"), vd0[0] + 16 * (H // 2))
            g.wait_ge(p.sem(f"{tag}_vd1"), vd0[1] + 16 * (H // 2))
            p.inc(g.collective_compute(
                "AllGather", ALU.bypass,
                replica_groups=[[0, 1, 2, 3], [4, 5, 6, 7]],
                ins=[k_local.ap().rearrange("h k t -> (h k) t")],
                outs=[k_gath.ap()]), "cc", 1)
            p.inc(g.collective_compute(
                "AllGather", ALU.bypass,
                replica_groups=[[0, 1, 2, 3], [4, 5, 6, 7]],
                ins=[vt_local.ap().rearrange("h t d -> (h t) d")],
                outs=[vt_gath.ap()]), "cc", 1)

        @blk.tensor
        def _(t):
            for nm, val in (entry or ()):
                t.wait_ge(p.sem(nm), val)
            for h in range(H):
                sl = h % 2
                t.wait_ge(p.sem(f"{tag}_w{sl}"), w0[sl] + 16 * (h // 2 + 1))
                if h >= 1:
                    t.wait_ge(p.sem(f"{tag}_ev"), ev0 + h)
                    t.wait_ge(p.sem(f"{tag}_av"), av0 + h)
                for th in range(NT):
                    tsl = slice(th * 512, (th + 1) * 512)
                    for dc in range(NDC):
                        inst = t.matmul(k_ps[:, tsl],
                                        wsl[sl][:, dc * 128:(dc + 1) * 128],
                                        xn_sb[:, dc * TC + th * 512: dc * TC + (th + 1) * 512],
                                        start=(dc == 0), stop=(dc == NDC - 1))
                p.inc(inst, f"{tag}_mm")
                for th in range(NT):
                    tsl = slice(th * 512, (th + 1) * 512)
                    for dc in range(NDC):
                        inst = t.matmul(v_ps[:, tsl],
                                        wsl[sl][:, D + dc * 128: D + (dc + 1) * 128],
                                        xn_sb[:, dc * TC + th * 512: dc * TC + (th + 1) * 512],
                                        start=(dc == 0), stop=(dc == NDC - 1))
                p.inc(inst, f"{tag}_mm")
                t.wait_ge(p.sem(f"{tag}_av"), av0 + h + 1)
                for c in range(8):
                    gidx = h * 8 + c
                    if gidx >= 2:
                        t.wait_ge(p.sem(f"{tag}_vt"), vt0 + gidx - 1)
                    p.inc(t.transpose(t_ps[c % 2][:, 0:128],
                                      vst[sl][:, c * 128:(c + 1) * 128],
                                      eye_sb[:, :]), f"{tag}_tr")

        @blk.vector
        def _(v):
            for h in range(H):
                sl = h % 2
                v.wait_ge(p.sem(f"{tag}_mm"), mm0 + 2 * h + 1)
                if h >= 2:
                    v.wait_ge(p.sem(f"{tag}_kd{sl}"), kd0[sl] + 16 * (h // 2))
                p.inc(v.tensor_copy(kst[sl][:, :], k_ps[:, :]), f"{tag}_ev")
                for c in range(8):
                    gidx = h * 8 + c
                    v.wait_ge(p.sem(f"{tag}_tr"), tr0 + gidx + 1)
                    if h >= 2 and c == 0:
                        v.wait_ge(p.sem(f"{tag}_vd{sl}"), vd0[sl] + 16 * (h // 2))
                    p.inc(v.tensor_copy(vtt[sl][:, c * 128:(c + 1) * 128],
                                        t_ps[c % 2][:, 0:128]), f"{tag}_vt")

        @blk.scalar
        def _(sc):
            for h in range(H):
                sl = h % 2
                sc.wait_ge(p.sem(f"{tag}_mm"), mm0 + 2 * h + 2)
                if h >= 2:
                    sc.wait_ge(p.sem(f"{tag}_tr"), tr0 + 8 * (h - 1))
                p.inc(sc.activation(vst[sl][:, :], v_ps[:, :], AF.Copy), f"{tag}_av")

        @blk.sync
        def _(s):
            for h in range(H):
                sl = h % 2
                s.wait_ge(p.sem(f"{tag}_ev"), ev0 + h + 1)
                p.inc(s.dma_start(out=k_local[h, :, :], in_=kst[sl][:, :]),
                      f"{tag}_kd{sl}", 16)
                s.wait_ge(p.sem(f"{tag}_vt"), vt0 + 8 * (h + 1))
                vt_out = vt_local.ap().rearrange("h (c q) d -> h q c d", q=128)
                p.inc(s.dma_start(
                    out=vt_out[h, :, :, :],
                    in_=vtt[sl].ap().rearrange("q (c d) -> q c d", d=128)),
                    f"{tag}_vd{sl}", 16)



def phase_qproj(p, xn_sb, wq_g, q_sb, tag="qp", blk=None, ctx=None, entry=None):
    btag = tag
    nc = p.nc
    w0 = [p.base(f"{tag}_w{i}") for i in range(2)]
    mm0 = p.base(f"{tag}_mm")
    ev0 = p.base(f"{tag}_ev")

    with ExitStack() as lctx:
        e = (ctx or lctx).enter_context
        el = lctx.enter_context
        wsl = [e(nc.sbuf_tensor(f"{btag}_wb{i}", [128, D], BF16)) for i in range(2)]
        q_ps = [el(nc.psum_tensor(f"{btag}_q{i}", [128, TC], F32)) for i in range(2)]
        blk = blk if blk is not None else el(nc.Block(f"blk_{btag}"))

        @blk.gpsimd
        def _(g):
            g.wait_ge(p.sem("cc"), 14)
            for h in range(H):
                sl = h % 2
                if h >= 2:
                    g.wait_ge(p.sem(f"{tag}_mm"), mm0 + h - 1)
                p.inc(g.dma_start(out=wsl[sl][:, :], in_=wq_g(h)), f"{tag}_w{sl}", 16)

        @blk.tensor
        def _(t):
            for nm, val in (entry or ()):
                t.wait_ge(p.sem(nm), val)
            for h in range(H):
                sl = h % 2
                t.wait_ge(p.sem(f"{tag}_w{sl}"), w0[sl] + 16 * (h // 2 + 1))
                if h >= 2:
                    t.wait_ge(p.sem(f"{tag}_ev"), ev0 + h - 1)
                for th in range(NT):
                    tsl = slice(th * 512, (th + 1) * 512)
                    for dc in range(NDC):
                        inst = t.matmul(q_ps[sl][:, tsl],
                                        wsl[sl][:, dc * 128:(dc + 1) * 128],
                                        xn_sb[:, dc * TC + th * 512: dc * TC + (th + 1) * 512],
                                        start=(dc == 0), stop=(dc == NDC - 1))
                p.inc(inst, f"{tag}_mm")

        @blk.vector
        def _(v):
            for h in range(H):
                sl = h % 2
                v.wait_ge(p.sem(f"{tag}_mm"), mm0 + h + 1)
                p.inc(v.tensor_copy(q_sb[:, h * TC:(h + 1) * TC], q_ps[sl][:, :]),
                      f"{tag}_ev")



def phase_flash(p, q_sb, attn_sb, kg_of, kl_dram, vg_of, vl_of,
                gates_sb, gatesb_sb, masks_sb, tag="fl", blk=None, ctx=None,
                entry=None):
    nc = p.nc
    NP_PT = 8

    def chunk_quads(kc, kind):
        if kind == "r":
            return [0, 1]
        return [0, 1] if kc < 4 else [1]

    chunk_list = [("r", kc) for kc in range(NREM)] + [("l", kc) for kc in range(8)]

    sched = []
    g_slot = 0
    slot_seq = []
    cnt = dict(mm=0, cs=0, ex=0, msk=0, pv=0, fin=0, row=0, eb=0)
    for h in range(H):
        m = dict(norm=[], sc={}, fin=[])
        for j in range(10):
            m["norm"].append(dict(g=g_slot, mm=cnt["mm"] + 1, cs=cnt["cs"] + 1))
            slot_seq.append(("cs", cnt["cs"] + 1))
            cnt["mm"] += 1
            cnt["cs"] += 1
            g_slot += 1
        cnt["row"] += 5
        cnt["eb"] += 2
        m["row"] = cnt["row"]
        m["eb"] = cnt["eb"]
        for kind, kc in chunk_list:
            for qd in chunk_quads(kc, kind):
                key = (kind, kc, qd)
                cnt["mm"] += 1
                slot_seq.append(("ex", cnt["ex"] + 1))
                masked = (kind == "l" and 0 <= kc - 4 * qd <= 3)
                if masked:
                    cnt["msk"] += 1
                cnt["pv"] += 1
                m["sc"][key] = dict(g=g_slot, mm=cnt["mm"], ex=cnt["ex"] + 1,
                                    msk=(cnt["msk"] if masked else None),
                                    pt=cnt["pv"] - 1, pv=cnt["pv"])
                cnt["ex"] += 1
                g_slot += 1
        for qd in range(2):
            cnt["fin"] += 4
            m["fin"].append(cnt["fin"])
        sched.append(m)

    kh0 = [p.base(f"{tag}_kh{i}") for i in range(2)]
    kl0 = [p.base(f"{tag}_kl{i}") for i in range(2)]
    vr0 = [p.base(f"{tag}_vr{i}") for i in range(2)]
    vl0 = [p.base(f"{tag}_vl{i}") for i in range(2)]
    vg0 = p.base(f"{tag}_vg")
    mm0 = p.base(f"{tag}_mm")
    cs0 = p.base(f"{tag}_cs")
    ex0 = p.base(f"{tag}_ex")
    msk0 = p.base(f"{tag}_msk")
    pv0 = p.base(f"{tag}_pv")
    fin0 = p.base(f"{tag}_fin")
    row0 = p.base(f"{tag}_row")
    eb0 = p.base(f"{tag}_eb")
    sq0 = p.base(f"{tag}_sq")
    on0 = p.base(f"{tag}_on")

    def slot_wait(eng, g):
        if g >= 3:
            nm, val = slot_seq[g - 3]
            base = cs0 if nm == "cs" else ex0
            eng.wait_ge(p.sem(f"{tag}_{nm}"), base + val)

    with ExitStack() as lctx:
        e = (ctx or lctx).enter_context
        el = lctx.enter_context
        kh = [e(nc.sbuf_tensor(f"{tag}_kh{i}", [128, 3 * TC], BF16)) for i in range(2)]
        kl = [e(nc.sbuf_tensor(f"{tag}_kl{i}", [128, TC], BF16)) for i in range(2)]
        vr = [e(nc.sbuf_tensor(f"{tag}_vr{i}", [128, NREM * 128], BF16))
              for i in range(2)]
        vl = [e(nc.sbuf_tensor(f"{tag}_vl{i}", [128, 8 * 128], BF16))
              for i in range(2)]
        psb = [e(nc.sbuf_tensor(f"{tag}_p{i}", [128, 512], BF16))
               for i in range(NP_PT)]
        ksq = [e(nc.sbuf_tensor(f"{tag}_ksq{i}", [128, 512], BF16)) for i in range(2)]
        ones_col = e(nc.sbuf_tensor(f"{tag}_ones", [128, 1], BF16))
        ones_row = e(nc.sbuf_tensor(f"{tag}_onesr", [1, 128], F32))
        qmx = e(nc.sbuf_tensor(f"{tag}_qmx", [1, 2], F32))
        kmx = e(nc.sbuf_tensor(f"{tag}_kmx", [1, 8], F32))
        qq = e(nc.sbuf_tensor(f"{tag}_qq", [1, 1], F32))
        kk = e(nc.sbuf_tensor(f"{tag}_kk", [1, 1], F32))
        bb = e(nc.sbuf_tensor(f"{tag}_bb", [1, 1], F32))
        nb_sb = e(nc.sbuf_tensor(f"{tag}_nb", [1, 1], F32))
        ebs = [e(nc.sbuf_tensor(f"{tag}_eb{i}", [128, 1], F32)) for i in range(2)]
        liv_row = e(nc.sbuf_tensor(f"{tag}_lr", [1, 512], F32))
        liv_sb = e(nc.sbuf_tensor(f"{tag}_ls", [128, 512], F32))
        s_ps = [el(nc.psum_tensor(f"{tag}_s{i}", [128, 512], F32)) for i in range(3)]
        o_ps = [el(nc.psum_tensor(f"{tag}_o{i}", [128, 512], F32)) for i in range(2)]
        l_ps = [el(nc.psum_tensor(f"{tag}_l{i}", [1, 512], F32)) for i in range(2)]
        mi_ps = el(nc.psum_tensor(f"{tag}_mi", [128, 512], F32))
        blk = blk if blk is not None else el(nc.Block(f"blk_{tag}"))

        def vt_chunk(h, kind, kc):
            buf = vr[h % 2] if kind == "r" else vl[h % 2]
            return buf[:, kc * 128:(kc + 1) * 128]

        def k_chunk(h, kind, kc):
            buf = kh[h % 2] if kind == "r" else kl[h % 2]
            return buf[:, kc * 128:(kc + 1) * 128]

        def pv_done_mark(h):
            return max(v["pv"] for v in sched[h]["sc"].values())

        @blk.sync
        def _(s):
            for nm, val in (entry or ()):
                s.wait_ge(p.sem(nm), val)
            s.wait_ge(p.sem("cc"), 17)
            for h in range(H):
                sl = h % 2
                if h >= 2:
                    s.wait_ge(p.sem(f"{tag}_pv"), pv0 + pv_done_mark(h - 2))
                p.inc(s.dma_start(out=kh[sl].ap().rearrange("p (r t) -> p r t", r=3),
                                  in_=kg_of(h)), f"{tag}_kh{sl}", 16)
                p.inc(s.dma_start(out=kl[sl][:, :], in_=kl_dram[h, :, :]),
                      f"{tag}_kl{sl}", 16)
                vrv = vr[sl].ap().rearrange("p (c x) -> p c x", x=128)
                for r in range(3):
                    p.inc(s.dma_start(out=vrv[:, r * 8:(r + 1) * 8, :],
                                      in_=vg_of(h, r)), f"{tag}_vr{sl}", 16)
                vlv = vl[sl].ap().rearrange("p (c x) -> p c x", x=128)
                p.inc(s.dma_start(out=vlv[:, :, :], in_=vl_of(h)), f"{tag}_vl{sl}", 16)

        @blk.scalar
        def _(sc):
            for nm, val in (entry or ()):
                sc.wait_ge(p.sem(nm), val)
            for h in range(H):
                sl = h % 2
                m = sched[h]
                srcs = [q_sb[:, h * TC:h * TC + 512], q_sb[:, h * TC + 512:(h + 1) * TC]]
                srcs += [kh[sl][:, j * 512:(j + 1) * 512] for j in range(6)]
                srcs += [kl[sl][:, 0:512], kl[sl][:, 512:1024]]
                sc.wait_ge(p.sem(f"{tag}_kh{sl}"), kh0[sl] + 16 * (h // 2 + 1))
                sc.wait_ge(p.sem(f"{tag}_kl{sl}"), kl0[sl] + 16 * (h // 2 + 1))
                for j, srcap in enumerate(srcs):
                    gi = h * 10 + j
                    if gi >= 2:
                        sc.wait_ge(p.sem(f"{tag}_mm"),
                                   mm0 + _nmm_of_normslot(sched, gi - 2))
                    p.inc(sc.activation(ksq[gi % 2][:, :], srcap, AF.Square),
                          f"{tag}_sq")
                sc.wait_ge(p.sem(f"{tag}_row"), row0 + m["row"] - 2)
                p.inc(sc.activation(nb_sb[:, :], bb[:, :], AF.Sqrt), f"{tag}_row")
                for kind, kc in chunk_list:
                    for qd in chunk_quads(kc, kind):
                        d = m["sc"][(kind, kc, qd)]
                        sc.wait_ge(p.sem(f"{tag}_mm"), mm0 + d["mm"])
                        sc.wait_ge(p.sem(f"{tag}_eb"), eb0 + m["eb"])
                        pb = d["pv"] - NP_PT
                        if pb >= 1:
                            sc.wait_ge(p.sem(f"{tag}_pv"), pv0 + pb)
                        p.inc(sc.activation(psb[d["pt"] % NP_PT][:, :],
                                            s_ps[d["g"] % 3][:, :],
                                            AF.Exp, bias=ebs[sl][:, :], scale=SCALE),
                              f"{tag}_ex")

        @blk.vector
        def _(v):
            for nm, val in (entry or ()):
                v.wait_ge(p.sem(nm), val)
            p.inc(v.memset(ones_col[:, :], 1.0), f"{tag}_on")
            p.inc(v.memset(ones_row[:, :], 1.0), f"{tag}_on")
            for h in range(H):
                sl = h % 2
                m = sched[h]
                if h < 2:
                    v.wait_ge(p.sem(f"{tag}_on"), on0 + 2)
                v.wait_ge(p.sem(f"{tag}_vr{sl}"), vr0[sl] + 48 * (h // 2 + 1))
                for kc in range(NREM):
                    inst = v.tensor_scalar(vr[sl][:, kc * 128:(kc + 1) * 128],
                                           vr[sl][:, kc * 128:(kc + 1) * 128],
                                           gates_sb[:, kc:kc + 1], None, ALU.mult)
                p.inc(inst, f"{tag}_vg")
                for j in range(10):
                    d = m["norm"][j]
                    v.wait_ge(p.sem(f"{tag}_mm"), mm0 + d["mm"])
                    dst = qmx[:, j:j + 1] if j < 2 else kmx[:, (j - 2):(j - 1)]
                    p.inc(v.reduce_max(dst, s_ps[d["g"] % 3][0:1, 0:512], axis=AX.X),
                          f"{tag}_cs")
                v.wait_ge(p.sem(f"{tag}_cs"), cs0 + m["norm"][-1]["cs"])
                if h >= 1:
                    v.wait_ge(p.sem(f"{tag}_row"), row0 + sched[h - 1]["row"])
                p.inc(v.reduce_max(qq[:, :], qmx[:, :], axis=AX.X), f"{tag}_row")
                v.wait_ge(p.sem(f"{tag}_row"), row0 + m["row"] - 4)
                p.inc(v.reduce_max(kk[:, :], kmx[:, :], axis=AX.X), f"{tag}_row")
                v.wait_ge(p.sem(f"{tag}_row"), row0 + m["row"] - 3)
                p.inc(v.tensor_tensor(bb[:, :], qq[:, :], kk[:, :], ALU.mult),
                      f"{tag}_row")
                v.wait_ge(p.sem(f"{tag}_row"), row0 + m["row"] - 1)
                p.inc(v.tensor_scalar(nb_sb[:, :], nb_sb[:, :], -SCALE, None,
                                      ALU.mult), f"{tag}_row")
                v.wait_ge(p.sem(f"{tag}_eb"), eb0 + m["eb"] - 1)
                if h >= 2:
                    last_ex = max(d["ex"] for d in sched[h - 2]["sc"].values())
                    v.wait_ge(p.sem(f"{tag}_ex"), ex0 + last_ex)
                p.inc(v.tensor_copy(ebs[sl][:, :], mi_ps[:, 0:1]), f"{tag}_eb")
                for kind, kc in chunk_list:
                    for qd in chunk_quads(kc, kind):
                        d = m["sc"][(kind, kc, qd)]
                        if d["msk"] is None:
                            continue
                        v.wait_ge(p.sem(f"{tag}_ex"), ex0 + d["ex"])
                        p.inc(v.tensor_tensor(psb[d["pt"] % NP_PT][:, :],
                                              psb[d["pt"] % NP_PT][:, :],
                                              masks_sb[kc - 4 * qd][:, :], ALU.mult),
                              f"{tag}_msk")
                for qd in range(2):
                    fm = m["fin"][qd]
                    key = ("l", 4 * qd + 3, qd)
                    v.wait_ge(p.sem(f"{tag}_pv"), pv0 + m["sc"][key]["pv"])
                    if h >= 1 or qd >= 1:
                        v.wait_ge(p.sem(f"{tag}_fin"), fin0 + fm - 5)
                    p.inc(v.reciprocal(liv_row[:, :], l_ps[qd][0:1, :]), f"{tag}_fin")
                    v.wait_ge(p.sem(f"{tag}_fin"), fin0 + fm - 2)
                    p.inc(v.tensor_copy(liv_sb[:, :], mi_ps[:, :]), f"{tag}_fin")
                    v.wait_ge(p.sem(f"{tag}_fin"), fin0 + fm - 1)
                    p.inc(v.tensor_tensor(
                        attn_sb[:, h * TC + qd * 512:h * TC + (qd + 1) * 512],
                        o_ps[qd][:, :], liv_sb[:, :], ALU.mult), f"{tag}_fin")

        @blk.tensor
        def _(t):
            for nm, val in (entry or ()):
                t.wait_ge(p.sem(nm), val)
            for h in range(H):
                sl = h % 2
                m = sched[h]
                t.wait_ge(p.sem(f"{tag}_kh{sl}"), kh0[sl] + 16 * (h // 2 + 1))
                t.wait_ge(p.sem(f"{tag}_kl{sl}"), kl0[sl] + 16 * (h // 2 + 1))
                t.wait_ge(p.sem(f"{tag}_vl{sl}"), vl0[sl] + 16 * (h // 2 + 1))
                t.wait_ge(p.sem(f"{tag}_vg"), vg0 + h + 1)
                for j in range(10):
                    d = m["norm"][j]
                    t.wait_ge(p.sem(f"{tag}_sq"), sq0 + h * 10 + j + 1)
                    slot_wait(t, d["g"])
                    p.inc(t.matmul(s_ps[d["g"] % 3][0:1, 0:512], ones_col[:, :],
                                   ksq[(h * 10 + j) % 2][:, :],
                                   start=True, stop=True), f"{tag}_mm")
                t.wait_ge(p.sem(f"{tag}_row"), row0 + m["row"])
                if h >= 1:
                    t.wait_ge(p.sem(f"{tag}_fin"), fin0 + sched[h - 1]["fin"][1] - 1)
                p.inc(t.matmul(mi_ps[:, 0:1], ones_row[:, :], nb_sb[:, :],
                               start=True, stop=True), f"{tag}_eb")
                prev_chunk = None
                for kind, kc in chunk_list:
                    for qd in chunk_quads(kc, kind):
                        d = m["sc"][(kind, kc, qd)]
                        slot_wait(t, d["g"])
                        p.inc(t.matmul(s_ps[d["g"] % 3][:, :],
                                       k_chunk(h, kind, kc),
                                       q_sb[:, h * TC + qd * 512:
                                            h * TC + (qd + 1) * 512],
                                       start=True, stop=True), f"{tag}_mm")
                    if prev_chunk is not None:
                        _emit_pv(p, t, tag, sched, h, prev_chunk, chunk_quads,
                                 vt_chunk, gatesb_sb, ones_col, psb, o_ps, l_ps,
                                 ex0, msk0, fin0, eb0, NP_PT)
                    prev_chunk = (kind, kc)
                _emit_pv(p, t, tag, sched, h, prev_chunk, chunk_quads, vt_chunk,
                         gatesb_sb, ones_col, psb, o_ps, l_ps, ex0, msk0, fin0,
                         eb0, NP_PT)
                for qd in range(2):
                    fm = m["fin"][qd]
                    t.wait_ge(p.sem(f"{tag}_fin"), fin0 + fm - 3)
                    if qd == 0:
                        t.wait_ge(p.sem(f"{tag}_eb"), eb0 + m["eb"])
                    else:
                        t.wait_ge(p.sem(f"{tag}_fin"), fin0 + m["fin"][0] - 1)
                    p.inc(t.matmul(mi_ps[:, :], ones_row[:, :],
                                   liv_row[:, :], start=True, stop=True),
                          f"{tag}_fin")

    return


def _nmm_of_normslot(sched, gi):
    h, j = gi // 10, gi % 10
    return sched[h]["norm"][j]["mm"]


def _emit_pv(p, t, tag, sched, h, chunk, chunk_quads, vt_chunk, gatesb_sb,
             ones_col, psb, o_ps, l_ps, ex0, msk0, fin0, eb0, NP_PT):
    kind, kc = chunk
    m = sched[h]
    for qd in chunk_quads(kc, kind):
        d = m["sc"][(kind, kc, qd)]
        if d["msk"] is not None:
            t.wait_ge(p.sem(f"{tag}_msk"), msk0 + d["msk"])
        else:
            t.wait_ge(p.sem(f"{tag}_ex"), ex0 + d["ex"])
        first = (kind == "r" and kc == 0)
        last = (kind == "l" and kc == 4 * qd + 3)
        if first:
            if h >= 1:
                t.wait_ge(p.sem(f"{tag}_fin"), fin0 + sched[h - 1]["fin"][qd])
            else:
                pass
        t.matmul(o_ps[qd][:, :], vt_chunk(h, kind, kc),
                 psb[d["pt"] % NP_PT][:, :], start=first, stop=last)
        lhs_l = gatesb_sb[:, kc:kc + 1] if kind == "r" else ones_col[:, :]
        inst = t.matmul(l_ps[qd][0:1, :], lhs_l,
                        psb[d["pt"] % NP_PT][:, :], start=first, stop=last)
        p.inc(inst, f"{tag}_pv")



def phase_wo(p, attn_sb, wo_g, res_src, res_dst, tag="wo", res_ready=None,
             blk=None, ctx=None):
    btag = tag
    """attn @ wo + res_src -> res_dst. wo_g: (g, half) -> AP [128, 8192] bf16."""
    nc = p.nc
    w0 = [p.base(f"{tag}_w{i}") for i in range(2)]
    mm0 = p.base(f"{tag}_mm")
    ev0 = p.base(f"{tag}_ev")
    rc0 = [p.base(f"{tag}_rc{i}") for i in range(2)]
    xo0 = [p.base(f"{tag}_xo{i}") for i in range(2)]
    ss0 = p.base(f"{tag}_ss")

    with ExitStack() as lctx:
        e = (ctx or lctx).enter_context
        el = lctx.enter_context
        wsl = [e(nc.sbuf_tensor(f"{btag}_wb{i}", [128, 8192], BF16)) for i in range(2)]
        rc = [e(nc.sbuf_tensor(f"{btag}_rc{i}", [128, TC], F32)) for i in range(4)]
        xo = [e(nc.sbuf_tensor(f"{btag}_xo{i}", [128, TC], F32)) for i in range(2)]
        tmp = [e(nc.sbuf_tensor(f"{btag}_tm{i}", [128, TC], F32)) for i in range(2)]
        y_ps = [el(nc.psum_tensor(f"{btag}_y{j}", [128, TC], F32)) for j in range(4)]
        blk = blk if blk is not None else el(nc.Block(f"blk_{btag}"))

        @blk.gpsimd
        def _(g):
            g.wait_ge(p.sem("cc"), 15)
            for gi in range(8):
                for half in range(2):
                    i = gi * 2 + half
                    sl = i % 2
                    if i >= 2:
                        g.wait_ge(p.sem(f"{tag}_mm"), mm0 + i - 1)
                    p.inc(g.dma_start(out=wsl[sl][:, :], in_=wo_g(gi, half)),
                          f"{tag}_w{sl}", 16)

        @blk.sync
        def _(s):
            for gi in range(8):
                for j in range(4):
                    if gi >= 1:
                        s.wait_ge(p.sem(f"{tag}_ev"), ev0 + 4 * gi)
                    if res_ready is not None:
                        for nm, val in res_ready(gi * 4 + j):
                            s.wait_ge(p.sem(nm), val)
                    p.inc(s.dma_start(out=rc[j][:, :],
                                      in_=res_src[gi * 4 + j]), f"{tag}_rc{gi % 2}", 16)

        @blk.tensor
        def _(t):
            for gi in range(8):
                if gi >= 1:
                    t.wait_ge(p.sem(f"{tag}_ev"), ev0 + gi * 4)
                for half in range(2):
                    i = gi * 2 + half
                    sl = i % 2
                    t.wait_ge(p.sem(f"{tag}_w{sl}"), w0[sl] + 16 * (i // 2 + 1))
                    for kc16 in range(16):
                        kc = half * 16 + kc16
                        for j in range(4):
                            for th in range(NT):
                                inst = t.matmul(
                                    y_ps[j][:, th * 512:(th + 1) * 512],
                                    wsl[sl][:, kc16 * 512 + j * 128: kc16 * 512 + (j + 1) * 128],
                                    attn_sb[:, kc * TC + th * 512: kc * TC + (th + 1) * 512],
                                    start=(kc == 0), stop=(kc == 31))
                    p.inc(inst, f"{tag}_mm")

        @blk.vector
        def _(v):
            nss = 0
            for gi in range(8):
                v.wait_ge(p.sem(f"{tag}_mm"), mm0 + (gi + 1) * 2)
                v.wait_ge(p.sem(f"{tag}_rc{gi % 2}"), rc0[gi % 2] + 16 * 4 * (gi // 2 + 1))
                for j in range(4):
                    idx = gi * 4 + j
                    if idx >= 2:
                        v.wait_ge(p.sem(f"{tag}_xo{idx % 2}"),
                                  xo0[idx % 2] + 16 * (idx // 2))
                        v.wait_ge(p.sem(f"{tag}_ev"), ev0 + idx - 1)
                    p.inc(v.tensor_scalar(tmp[idx % 2][:, :], y_ps[j][:, :], 1.0,
                                          None, ALU.mult), f"{tag}_ss")
                    nss += 1
                    v.wait_ge(p.sem(f"{tag}_ss"), ss0 + nss)
                    p.inc(v.tensor_tensor(xo[idx % 2][:, :], tmp[idx % 2][:, :],
                                          rc[j][:, :], ALU.add),
                          f"{tag}_ev")

        @blk.scalar
        def _(sc):
            for gi in range(8):
                for j in range(4):
                    idx = gi * 4 + j
                    sc.wait_ge(p.sem(f"{tag}_ev"), ev0 + idx + 1)
                    p.inc(sc.dma_start(out=res_dst[idx], in_=xo[idx % 2][:, :]),
                          f"{tag}_xo{idx % 2}", 16)
            sc.wait_ge(p.sem(f"{tag}_xo0"), p.cnt[f"{tag}_xo0"])
            sc.wait_ge(p.sem(f"{tag}_xo1"), p.cnt[f"{tag}_xo1"])

    return xo0



def build_program(upto="full"):
    nc = bass.Bass("TRN2", target_bir_lowering=False)

    x_in = nc.dram_tensor("x_fm", [D, TC], F32, kind="ExternalInput")
    wug_sh = nc.dram_tensor("wug_sh", [11, 128, 2 * D], F32, kind="ExternalInput")
    wdn_sh = nc.dram_tensor("wdn_sh", [43, 128, TC], F32, kind="ExternalInput")
    wkv_sh = nc.dram_tensor("wkv_sh", [4, 128, 2 * D], F32, kind="ExternalInput")
    wq_sh = nc.dram_tensor("wq_sh", [4, 128, D], F32, kind="ExternalInput")
    wo_sh = nc.dram_tensor("wo_sh", [2, 128, 2 * D], F32, kind="ExternalInput")
    eye_in = nc.dram_tensor("eye", [128, 128], BF16, kind="ExternalInput")
    gates_in = nc.dram_tensor("gates", [128, NREM], F32, kind="ExternalInput")
    gatesb_in = nc.dram_tensor("gatesb", [128, NREM], BF16, kind="ExternalInput")
    masks_in = nc.dram_tensor("masks", [4, 128, 512], BF16, kind="ExternalInput")
    out = nc.dram_tensor("out_fm", [D, TC], F32, kind="ExternalOutput")

    g16 = dict(
        wug=nc.dram_tensor("g_wug", [88 * 128, 2 * D], BF16, addr_space="Shared"),
        wdn=nc.dram_tensor("g_wdn", [344 * 128, TC], BF16, addr_space="Shared"),
        wkv=nc.dram_tensor("g_wkv", [32 * 128, 2 * D], BF16, addr_space="Shared"),
        wq=nc.dram_tensor("g_wq", [32 * 128, D], BF16, addr_space="Shared"),
        wo=nc.dram_tensor("g_wo", [16 * 128, 2 * D], BF16, addr_space="Shared"),
    )
    h_kind = "ExternalOutput" if upto == "ffna" else "Internal"
    h_scr = nc.dram_tensor("h_scr", [NFC, 128, TC], BF16, kind=h_kind)
    res1 = nc.dram_tensor("res1", [NDC, 128, TC], F32)
    res2 = nc.dram_tensor("res2", [NDC, 128, TC], F32)
    k_local = nc.dram_tensor("k_local", [H, 128, TC], BF16)
    vt_local = nc.dram_tensor("vt_local", [H, TC, 128], BF16)
    k_gath = nc.dram_tensor("k_gath", [4 * H * 128, TC], BF16)
    vt_gath = nc.dram_tensor("vt_gath", [4 * H * TC, 128], BF16)

    xn_dbg = (nc.dram_tensor("xn_dbg", [128, NDC * TC], BF16, kind="ExternalOutput")
              if upto in ("ln1", "ffna", "attn_fm") else None)

    x_r = x_in.ap().rearrange("(dc k) t -> dc k t", k=128)
    out_r = out.ap().rearrange("(dc k) t -> dc k t", k=128)
    h_pr = h_scr.ap().rearrange("(fp two) k t -> fp k two t", two=2)
    x_v = [x_r[dc, :, :] for dc in range(NDC)]
    out_v = [out_r[dc, :, :] for dc in range(NDC)]
    res1_v = [res1[dc, :, :] for dc in range(NDC)]
    res2_v = [res2[dc, :, :] for dc in range(NDC)]
    h_v = [h_scr[fc, :, :] for fc in range(NFC)]
    h_pairs = [h_pr[fp, :, :, :] for fp in range(43)]

    wug_r = g16["wug"].ap().rearrange("(fc k) m -> fc k m", k=128)
    wdn_r = g16["wdn"].ap().rearrange("(i k) m -> i k m", k=128)
    wkv_r = g16["wkv"].ap().rearrange("(h k) m -> h k m", k=128)
    wq_r = g16["wq"].ap().rearrange("(h k) m -> h k m", k=128)
    wo_r = g16["wo"].ap().rearrange("(i k) m -> i k m", k=128)
    kg_r = k_gath.ap().rearrange("(r h k) t -> k r h t", h=H, k=128)
    vg_r = vt_gath.ap().rearrange("(r h c q) d -> q r h c d", h=H, c=8, q=128)
    vl_r = vt_local.ap().rearrange("h (c q) d -> q h c d", q=128)

    def wug_g(fc):
        return wug_r[fc, :, :]

    def wdn_g(gi, fp):
        return wdn_r[gi * 43 + fp, :, :]

    def wkv_g(h):
        return wkv_r[h, :, :]

    def wq_g(h):
        return wq_r[h, :, :]

    def wo_g(gi, half):
        return wo_r[gi * 2 + half, :, :]

    def kg_of(h):
        return kg_r[:, 0:3, h, :]

    def vg_of(h, r):
        return vg_r[:, r, h, :, :]

    def vl_of(h):
        return vl_r[:, h, :, :]

    with ExitStack() as ctx:
        p = Prog(nc, ctx)
        p.sem("cc")
        bigB = ctx.enter_context(nc.sbuf_tensor("bigB", [128, H * TC], BF16))
        sh = dict(wug_sh=wug_sh, wdn_sh=wdn_sh, wkv_sh=wkv_sh, wq_sh=wq_sh,
                  wo_sh=wo_sh)

        with ExitStack() as actx:
            bigA = actx.enter_context(nc.sbuf_tensor("bigA", [128, NDC * TC], BF16))

            with ExitStack() as g1:
                blk1 = g1.enter_context(nc.Block("blk_g1"))

                phase_wprep(p, sh, g16, blk=blk1)
                phase_ln(p, x_v, bigA, "ln", btag="l1", blk=blk1, ctx=g1)
                ap1 = p.cnt["ln_ap"]
                if upto == "ln1":
                    pass
                else:
                    hb = phase_ffn_a(p, bigA, h_v, wug_g, "fa", btag="fa1",
                                     blk=blk1, ctx=g1,
                                     entry=[("ln_ap", ap1)])
            if upto == "ln1":
                _finish(p, nc, [], xn_dbg, bigA)
                return nc
            if upto == "ffna":
                _finish(p, nc, [("fa_h0", p.cnt["fa_h0"]),
                                ("fa_h1", p.cnt["fa_h1"])], xn_dbg, bigA)
                return nc

            h_rdy1 = lambda fp: [("fa_h0", hb[0] + 16 * (fp + 1)),
                                 ("fa_h1", hb[1] + 16 * (fp + 1))]
            dst1 = out_v if upto == "ffn1" else res1_v
            with ExitStack() as g2:
                blk2 = g2.enter_context(nc.Block("blk_g2"))
                xb1 = phase_ffn_b(p, h_pairs, x_v, dst1, wdn_g, "fb", btag="fb1",
                                  yscale=0.5, h_ready=h_rdy1, blk=blk2, ctx=g2)
                ev1 = p.cnt["fb_ev"]
                res1_rdy = lambda dc: [(f"fb_xo{dc % 2}",
                                        xb1[dc % 2] + 16 * (dc // 2 + 1))]
                if upto != "ffn1":
                    phase_ln(p, res1_v, bigA, "ln", src_ready=res1_rdy, btag="l2",
                             blk=blk2, ctx=g2, entry=[("fb_ev", ev1)])
                    ap2 = p.cnt["ln_ap"]
            if upto == "ffn1":
                _finish(p, nc, [("fb_xo0", p.cnt["fb_xo0"]),
                                ("fb_xo1", p.cnt["fb_xo1"])])
                return nc

            with ExitStack() as g3:
                blk3 = g3.enter_context(nc.Block("blk_g3"))
                eye_sb = g3.enter_context(nc.sbuf_tensor("eye_sb", [128, 128], BF16))
                gates_sb = g3.enter_context(
                    nc.sbuf_tensor("gates_sb", [128, NREM], F32))
                gatesb_sb = g3.enter_context(
                    nc.sbuf_tensor("gatesb_sb", [128, NREM], BF16))
                masks_sb = [g3.enter_context(
                    nc.sbuf_tensor(f"mk{i}_sb", [128, 512], BF16))
                    for i in range(4)]

                @blk3.sync
                def _(s):
                    p.inc(s.dma_start(out=eye_sb[:, :], in_=eye_in[:, :]), "cst", 16)
                    p.inc(s.dma_start(out=gates_sb[:, :], in_=gates_in[:, :]),
                          "cst", 16)
                    p.inc(s.dma_start(out=gatesb_sb[:, :], in_=gatesb_in[:, :]),
                          "cst", 16)
                    for i in range(4):
                        p.inc(s.dma_start(out=masks_sb[i][:, :],
                                          in_=masks_in[i, :, :]), "cst", 16)

                cstw = [("cst", p.cnt["cst"])]
                phase_kvproj(p, bigA, wkv_g, k_local, vt_local, k_gath, vt_gath,
                             eye_sb, blk=blk3, entry=cstw)
                kp_done = [("kp_ev", p.cnt["kp_ev"]), ("kp_av", p.cnt["kp_av"]),
                           ("kp_vt", p.cnt["kp_vt"])]
                phase_qproj(p, bigA, wq_g, bigB, blk=blk3, entry=kp_done)
                qp_done = [("qp_ev", p.cnt["qp_ev"]),
                           ("qp_w0", p.cnt["qp_w0"]), ("qp_w1", p.cnt["qp_w1"])]
                phase_flash(p, bigB, bigA, kg_of, k_local, vg_of, vl_of,
                            gates_sb, gatesb_sb, masks_sb, blk=blk3, ctx=g3,
                            entry=qp_done + cstw + [("cc", 17)])
            if upto == "attn_fm":
                _finish(p, nc, [("fl_fin", p.cnt["fl_fin"])], xn_dbg, bigA)
                return nc

            dst2 = out_v if upto == "attn" else res2_v
            xb2 = phase_wo(p, bigA, wo_g, res1_v, dst2, res_ready=res1_rdy)
            if upto == "attn":
                _finish(p, nc, [("wo_xo0", p.cnt["wo_xo0"]),
                                ("wo_xo1", p.cnt["wo_xo1"])])
                return nc

        res2_rdy = lambda dc: [(f"wo_xo{dc % 2}", xb2[dc % 2] + 16 * (dc // 2 + 1))]
        h_war = [("fb_h0", p.cnt["fb_h0"]), ("fb_h1", p.cnt["fb_h1"])]
        with ExitStack() as g5:
            blk5 = g5.enter_context(nc.Block("blk_g5"))
            phase_ln(p, res2_v, bigB, "ln", src_ready=res2_rdy, btag="l3",
                     blk=blk5, ctx=g5)
            ap3 = p.cnt["ln_ap"]
            hb2 = phase_ffn_a(p, bigB, h_v, wug_g, "fa", h_war=h_war, btag="fa2",
                              blk=blk5, ctx=g5, entry=[("ln_ap", ap3)])
            h_rdy2 = lambda fp: [("fa_h0", hb2[0] + 16 * (fp + 1)),
                                 ("fa_h1", hb2[1] + 16 * (fp + 1))]
            fa2_done = [("fa_si", p.cnt["fa_si"]), ("fa_vt", p.cnt["fa_vt"])]
            phase_ffn_b(p, h_pairs, res2_v, out_v, wdn_g, "fb", btag="fb2",
                        yscale=0.5, h_ready=h_rdy2, blk=blk5, ctx=g5,
                        entry=fa2_done)
        _finish(p, nc, [("fb_xo0", p.cnt["fb_xo0"]), ("fb_xo1", p.cnt["fb_xo1"])])
        return nc


def _finish(p, nc, waits, xn_dbg=None, bigA=None):
    with nc.Block("blk_fin") as blk:
        @blk.sync
        def _(s):
            for nm, val in waits:
                if val:
                    s.wait_ge(p.sem(nm), val)
            if xn_dbg is not None:
                p.inc(s.dma_start(out=xn_dbg[:, :], in_=bigA[:, :]), "fin", 16)
                s.wait_ge(p.sem("fin"), p.cnt["fin"])



def kernel(L0_attn_hidden_states, L0_attn_wq, L0_attn_wkv, L0_attn_wo,
           L0_ffn_W_gate, L0_ffn_W_up, L0_ffn_W_down):
    x = np.asarray(L0_attn_hidden_states)
    lay = host_layouts(L0_attn_wq, L0_attn_wkv, L0_attn_wo,
                       L0_ffn_W_gate, L0_ffn_W_up, L0_ffn_W_down)
    shards = host_shards(lay)
    nc = build_program("full")
    in_maps = []
    for core in range(8):
        sq, c = core // 4, core % 4
        xs = np.ascontiguousarray(x[sq, c * TC:(c + 1) * TC, :].T)
        m = dict(x_fm=xs)
        m.update(shards[core])
        m.update(host_consts(core))
        in_maps.append(m)
    res = run_bass_kernel_spmd(nc, in_maps, core_ids=list(range(8)))
    outp = np.empty((B, S, D), np.float32)
    for core in range(8):
        sq, c = core // 4, core % 4
        outp[sq, c * TC:(c + 1) * TC, :] = np.asarray(res.results[core]["out_fm"]).T
    return outp
